# version 03
# baseline (speedup 1.0000x reference)
import os
import numpy as np
import ml_dtypes
import concourse.bass as bass
import concourse.mybir as mybir
from concourse.bass_utils import run_bass_kernel_spmd
from concourse.aot_env import allow_ici_all_to_all
from contextlib import ExitStack

F32 = mybir.dt.float32
BF16 = mybir.dt.bfloat16
U32 = mybir.dt.uint32
AF = mybir.ActivationFunctionType
ALU = mybir.AluOpType
AX = mybir.AxisListType

D = 2048
TTOK = 16384
TOWN = 2048
HFF = 2816
NCORE = 8
CAP = 2048
SCALE = 128.0 ** -0.5
LORA_S = 0.5
EPS = 1e-5
CSHIFT = 4.0
SENT = 1.0e9
NDC = 16
NHC = 22


class Sem:
    def __init__(self, h):
        self.h = h
        self.n = 0

    def inc(self, instr, k=1):
        instr.then_inc(self.h, k)
        self.n += k
        return self.n


def build_nc(DP=99):
    SUB = int(os.environ.get("KSUB", "9"))
    KCUT = set(x for x in os.environ.get("KCUT", "").split(",") if x)
    nc = bass.Bass("TRN2", target_bir_lowering=False, debug=False,
                   dynamic_dma_scratch_size=24576,
                   detect_race_conditions=False)

    def din(name, shape, dt=F32):
        return nc.dram_tensor(name, shape, dt, kind="ExternalInput").ap()

    def dout(name, shape, dt=F32):
        return nc.dram_tensor(name, shape, dt, kind="ExternalOutput").ap()

    def dtmp(name, shape, dt):
        return nc.dram_tensor(name, shape, dt).ap()

    x_my = din("x_my", [TOWN, D])
    WqT = din("WqT", [D, 256]); WkT = din("WkT", [D, 256]); WvT = din("WvT", [D, 256])
    WoT = din("WoT", [D, D])
    Bq = din("Bq", [16, D]); Bk = din("Bk", [16, D]); Bv = din("Bv", [16, D]); Bo = din("Bo", [16, D])
    AqT = din("AqT", [16, 256]); AkT = din("AkT", [16, 256]); AvT = din("AvT", [16, 256])
    AoT = din("AoT", [16, D])
    bq_c = din("bq_c", [128, 2]); bk_c = din("bk_c", [128, 2])
    bv_r = din("bv_r", [1, 256]); bo_r = din("bo_r", [1, D])
    Wg = din("Wg", [D, 8])
    W1t = din("W1t", [NHC, NDC, 128, 128])
    W3t = din("W3t", [NHC, NDC, 128, 128])
    W2d = din("W2d", [HFF, D])
    sw1t = din("sw1t", [NHC, NDC, 128, 128])
    sw3t = din("sw3t", [NHC, NDC, 128, 128])
    sw2d = din("sw2d", [HFF, D])
    ident = din("ident", [128, 128], BF16)
    ident32 = din("ident32", [128, 128], F32)
    TRI = din("TRI", [128, 128], BF16)
    iota8 = din("iota8", [128, 8])
    iota1k = din("iota1k", [128, 8])
    onesrow = din("onesrow", [1, 128], BF16)
    ones816 = din("ones816", [8, 16])
    maskp = din("maskp", [8, 8])
    tok1 = din("tok1", [128, 16])

    out_my = dout("out_my", [TOWN, D])

    xn_tm_c = dtmp("xn_tm_c", [TOWN, D], BF16)
    xn_fm_c = dtmp("xn_fm_c", [D, TOWN], BF16)
    xn_fm_full = dtmp("xn_fm_full", [NCORE * D, TOWN], BF16)
    hist_c = dtmp("hist_c", [8, 1], F32)
    hist_full = dtmp("hist_full", [64, 1], F32)
    xs_c = dtmp("xs_c", [TTOK, D], BF16)
    xs_all = dtmp("xs_all", [TTOK, D], BF16)
    Kfm = dtmp("Kfm", [256, TTOK], BF16)
    Qfm = dtmp("Qfm", [256, TTOK], BF16)
    att_c = dtmp("att_c", [TTOK, 256], BF16)
    att_all = dtmp("att_all", [TTOK, 256], BF16)
    WoTe = dtmp("WoTe", [D, D], BF16)
    W1b = dtmp("W1b", [NHC, NDC, 128, 128], BF16)
    W3b = dtmp("W3b", [NHC, NDC, 128, 128], BF16)
    W2b = dtmp("W2b", [HFF, D], BF16)
    sw1b = dtmp("sw1b", [NHC, NDC, 128, 128], BF16)
    sw3b = dtmp("sw3b", [NHC, NDC, 128, 128], BF16)
    sw2b = dtmp("sw2b", [HFF, D], BF16)
    oe_c = dtmp("oe_c", [CAP, D], BF16)
    oe_full = dtmp("oe_full", [NCORE * CAP, D], BF16)
    shared_my = dtmp("shared_my", [TOWN, D], BF16)

    dbg = {}
    if DP < 99:
        if DP >= 1:
            dbg["d_xn_tm"] = dout("d_xn_tm", [TOWN, D], BF16)
            dbg["d_xn_fm"] = dout("d_xn_fm", [NCORE * D, TOWN], BF16)
        if DP >= 2:
            dbg["d_route"] = dout("d_route", [128, 16 * 6])
            dbg["d_xs"] = dout("d_xs", [TTOK, D], BF16)
            if DP == 2:
                dbg["d_xsc"] = dout("d_xsc", [TTOK, D], BF16)
                dbg["d_slotu"] = dout("d_slotu", [128, 16], U32)
                dbg["d_eq"] = dout("d_eq", [128, 128])
                dbg["d_vl"] = dout("d_vl", [128, 128])
                dbg["d_nm"] = dout("d_nm", [128, 32])
        if DP >= 3:
            dbg["d_Kfm"] = dout("d_Kfm", [256, TTOK], BF16)
            dbg["d_Qfm"] = dout("d_Qfm", [256, TTOK], BF16)
        if DP >= 4:
            dbg["d_att"] = dout("d_att", [TTOK, 256], BF16)
        if DP >= 5:
            dbg["d_oe"] = dout("d_oe", [NCORE * CAP, D], BF16)
        if DP == 5:
            dbg["d_xd"] = dout("d_xd", [2048, 1024], BF16)
            dbg["d_hh"] = dout("d_hh", [NHC * 128, 1024], BF16)
        if DP >= 6:
            dbg["d_shared"] = dout("d_shared", [TOWN, D], BF16)

    with ExitStack() as stk:
        def sb(name, shape, dt):
            return stk.enter_context(nc.sbuf_tensor(name, shape, dt))

        def psumt(name, shape, dt):
            return stk.enter_context(nc.psum_tensor(name, shape, dt))

        def S(name):
            return Sem(stk.enter_context(nc.semaphore(name)))

        AR = 73728
        arena = sb("arena", [128, AR], BF16)
        f32ar = sb("f32ar", [128, 6144], F32)
        ident_sb = sb("ident_sb", [128, 128], BF16)
        id32_sb = sb("id32_sb", [128, 128], F32)
        tri_sb = sb("tri_sb", [128, 128], BF16)
        iota8_sb = sb("iota8_sb", [128, 8], F32)
        iota1k_sb = sb("iota1k_sb", [128, 8], F32)
        ones_sb = sb("ones_sb", [1, 128], BF16)
        ones816_sb = sb("ones816_sb", [8, 16], F32)
        wg_sb = sb("wg_sb", [128, NDC * 8], BF16)
        bqc_sb = sb("bqc_sb", [128, 2], F32)
        bkc_sb = sb("bkc_sb", [128, 2], F32)
        bvr_sb = sb("bvr_sb", [1, 256], BF16)
        bor_sb = sb("bor_sb", [1, D], BF16)
        maskp_sb = sb("maskp_sb", [8, 8], F32)
        tok1_sb = sb("tok1_sb", [128, 16], F32)
        mean_all = sb("mean_all", [128, 16], F32)
        var_all = sb("var_all", [128, 16], F32)
        sd_all = sb("sd_all", [128, 16], F32)
        rstd_all = sb("rstd_all", [128, 16], F32)
        logits_all = sb("logits_all", [128, 128], F32)
        nm_all = sb("nm_all", [128, 16], F32)
        pmax_all = sb("pmax_all", [128, 16], F32)
        se_all = sb("se_all", [128, 16], F32)
        gate_sb = sb("gate_sb", [128, 16], F32)
        gs_sb = sb("gs_sb", [128, 16], F32)
        slotu_sb = sb("slotu_sb", [128, 16], U32)
        slotf_sb = sb("slotf_sb", [128, 16], F32)
        eidx_sb = sb("eidx_sb", [128, 16], F32)
        keep_sb = sb("keep_sb", [128, 16], F32)
        posd_sb = sb("posd_sb", [128, 16], F32)
        oh_all = sb("oh_all", [128, 128], BF16)
        ohf_sb = sb("ohf_sb", [128, 8], F32)
        pos_all = sb("pos_all", [128, 128], F32)
        ohT_all = sb("ohT_all", [8, 2048], F32)
        histco = sb("histco", [8, 16], F32)
        pref = sb("pref", [8, 16], F32)
        offm1 = sb("offm1", [8, 16], F32)
        offmF = sb("offmF", [8, 16], F32)
        offc = sb("offc", [8, 1], F32)
        hf_sb = sb("hf_sb", [8, 8], F32)
        hm_sb = sb("hm_sb", [8, 8], F32)
        junk8 = sb("junk8", [128, 8], F32)
        junk8b = sb("junk8b", [128, 8], F32)
        logits_all2 = sb("logits_all2", [128, 128], F32)
        logits_all3 = sb("logits_all3", [128, 128], F32)
        r_c1 = sb("r_c1", [128, 1], F32)
        r_c2 = sb("r_c2", [128, 1], F32)
        r_c3 = sb("r_c3", [128, 1], F32)
        zero_sb = sb("zero_sb", [128, 128], F32)
        rec_sb = sb("rec_sb", [128, 1], F32)
        csh_sb = sb("csh_sb", [128, 1], F32)
        stat6 = sb("stat6", [128, 24], F32)
        stat2 = sb("stat2", [128, 2], F32)

        xt_f32 = [f32ar[:, 0:2048], f32ar[:, 2048:4096]]
        out_ev = [f32ar[:, 4096 + i * 512: 4096 + (i + 1) * 512] for i in range(2)]
        tmp1_f = f32ar[:, 5120:5632]
        tmp2_f = f32ar[:, 5632:6144]

        O_VAUG = 0
        VAUG_SZ = 128 * 258
        O_FM = VAUG_SZ
        O_ET = VAUG_SZ
        O_KH = O_ET + 16384
        O_QP = O_KH + 8192
        O_STG = O_QP + 1024
        O_HH = 0
        O_XD = 22528
        O_WB = O_XD + 16384
        O_W2 = 22528
        O_AFM = 0
        O_WOE = 32768

        def ar(o, n):
            return arena[:, o:o + n]

        xnb_ev = [ar(33024, 2048), ar(35072, 2048)]
        fmstage = ar(37120, 4096)
        lor_b = arena[0:16, 41216:43264]
        aot_b = arena[0:16, 43264:45312]
        at_b = arena[0:16, 45312:45568]
        raw_b = [ar(45568, 256), ar(45824, 256)]
        rawo_b = [ar(61696, 2048), ar(63744, 2048)]
        woe_b = [ar(65792, 2048), ar(67840, 2048)]
        O_WEFF = 49408

        def weff(w, dc):
            o = O_WEFF + ({"q": 0, "k": 1, "v": 2}[w] * NDC + dc) * 256
            return arena[:, o:o + 256]

        kq_ev = [ar(69888 + i * 512, 512) for i in range(4)]
        xdtm = [ar(47104, 2048), ar(49152, 2048)]
        xsh = [ar(51200, 2048), ar(53248, 2048)]
        accS = ar(57344, 2048)
        s1_ev = [ar(55296, 512), ar(55808, 512)]
        oe_ev = [ar(56320, 512), ar(56832, 512)]
        moebuf = [ar(65536, 2048), ar(67584, 2048)]
        shbuf = [ar(69632, 2048), ar(71680, 2048)]

        PA = [psumt(f"PA{i}", [128, 512], F32) for i in range(4)]
        PB = [psumt(f"PB{i}", [128, 512], F32) for i in range(2)]
        PT_full = [psumt(f"PT{i}", [128, 1024], BF16) for i in range(2)]
        PT = [t[:, 0:128] for t in PT_full]

        cc = S("cc")
        g0 = S("g0")
        f_ba = S("f_ba"); f_raw = [S("f_raw0"), S("f_raw1")]
        f_mm = S("f_mm"); f_cmb = S("f_cmb")
        f_wst = [S("f_wst0"), S("f_wst1")]
        a_x = [S("a_x0"), S("a_x1")]; a_fr = S("a_fr"); a_ms = S("a_ms")
        m_sq = S("m_sq"); m_rs = S("m_rs")
        b_x = [S("b_x0"), S("b_x1")]; b_fr = S("b_fr"); b_xn = S("b_xn")
        b_fmc = S("b_fmc"); b_tr = S("b_tr"); b_lg = S("b_lg"); b_le = S("b_le")
        b_st = [S("b_st0"), S("b_st1")]; b_sf = [S("b_sf0"), S("b_sf1")]
        r_nm = S("r_nm"); r_ex = S("r_ex"); r_vA = S("r_vA"); r_mmA = S("r_mmA")
        r_hs = S("r_hs"); r_hl = S("r_hl"); r_off = S("r_off")
        r_mmB = S("r_mmB"); r_vB = S("r_vB"); g_gs = S("g_gs")
        p3_fm = [S("p3_fm0"), S("p3_fm1")]
        p3_mmA = S("p3_mmA"); p3_mmV = S("p3_mmV")
        p3_ev = S("p3_ev"); p3_vv = S("p3_vv"); p3_vm = S("p3_vm")
        p3_ks = [S(f"p3_ks{i}") for i in range(4)]
        q_kh = [S("q_kh0"), S("q_kh1")]; q_qp = [S("q_qp0"), S("q_qp1")]
        q_mmS = S("q_mmS"); q_mmA = S("q_mmA"); q_ex = S("q_ex")
        q_rc = S("q_rc"); q_av = S("q_av"); q_st = S("q_st")
        z_gs = S("z_gs"); z_as = S("z_as"); z_ms = S("z_ms")
        x_sc = [S("x_sc0"), S("x_sc1")]
        m_sl = [S("m_sl0"), S("m_sl1")]; m_mg = S("m_mg")
        n_ga = [S("n_ga0"), S("n_ga1")]; n_gx = S("n_gx")
        n_tr = S("n_tr"); n_tc = S("n_tc")
        n_w1 = [S("n_w10"), S("n_w11")]; n_hm = S("n_hm"); n_si = S("n_si")
        n_ht = S("n_ht"); n_w2 = S("n_w2"); n_m2 = S("n_m2"); n_oe = S("n_oe")
        dx_s = S("dx_s"); wc_s = S("wc_s")
        n_os = [S("n_os0"), S("n_os1")]
        o_af = S("o_af"); o_wo = S("o_wo")
        o_x = [S("o_x0"), S("o_x1")]; o_sh = [S("o_sh0"), S("o_sh1")]
        o_mg = [S("o_mg0"), S("o_mg1")]
        o_mm = S("o_mm"); o_ev = S("o_ev"); o_st = [S("o_st0"), S("o_st1")]
        o_fr5 = S("o_fr5"); o_fr6 = S("o_fr6"); d_st = S("d_st")
        vch = S("vch")

        def VE(v, instr):
            vch.inc(instr)
            v.wait_ge(vch.h, vch.n)

        M = {}

        with nc.Block() as blk:
            @blk.gpsimd
            def _(g):
                for dst, src in [(ident_sb[:], ident), (id32_sb[:], ident32),
                                 (tri_sb[:], TRI), (iota8_sb[:], iota8),
                                 (iota1k_sb[:], iota1k), (ones_sb[:], onesrow),
                                 (ones816_sb[:], ones816),
                                 (bqc_sb[:], bq_c), (bkc_sb[:], bk_c),
                                 (bvr_sb[:], bv_r), (bor_sb[:], bo_r),
                                 (maskp_sb[:], maskp), (tok1_sb[:], tok1)]:
                    g0.inc(g.dma_start(out=dst, in_=src), 16)
                g0.inc(g.dma_start(
                    out=wg_sb[:].rearrange("p (dc e) -> p dc e", dc=NDC),
                    in_=Wg.rearrange("(dc p) e -> p dc e", p=128)), 16)
                g.memset(zero_sb[:], 0.0)
                g.memset(csh_sb[:], -CSHIFT)
                z_ms.inc(g.memset(ar(0, 2048), 0.0))
                g.wait_ge(z_ms.h, 1)
                for r in range(64):
                    g_gs.inc(g.dma_start(out=xs_c[r * 128:(r + 1) * 128, :],
                                         in_=ar(0, 2048)), 16)
        M["consts"] = g0.n
        M["xs_zero_g"] = g_gs.n

        with nc.Block() as blk:
            @blk.scalar
            def _(sc):
                sc.wait_ge(z_ms.h, 1)
                for r in range(64, 128):
                    z_as.inc(sc.dma_start(out=xs_c[r * 128:(r + 1) * 128, :],
                                          in_=ar(0, 2048)), 16)
        M["xs_zero_a"] = z_as.n

        with nc.Block() as blk:
            @blk.sync
            def _(s):
                for i in range(16):
                    if i >= 2:
                        s.wait_ge(a_fr.h, i - 1)
                    a_x[i % 2].inc(s.dma_start(out=xt_f32[i % 2],
                                               in_=x_my[i * 128:(i + 1) * 128, :]), 16)

            @blk.vector
            def _(v):
                for i in range(16):
                    v.wait_ge(a_x[i % 2].h, 16 * (i // 2 + 1))
                    for q4 in range(4):
                        ii = v.bn_stats(stat6[:, 6 * q4:6 * q4 + 6],
                                        xt_f32[i % 2][:, 512 * q4:512 * (q4 + 1)])
                    a_ms.inc(ii)
                    v.wait_ge(a_ms.h, a_ms.n)
                    a_ms.inc(v.bn_aggr(stat2[:], stat6[:]))
                    v.wait_ge(a_ms.h, a_ms.n)
                    v.tensor_copy(mean_all[:, i:i + 1], stat2[:, 0:1])
                    ts = v.tensor_scalar(var_all[:, i:i + 1], stat2[:, 1:2], EPS, None,
                                         op0=ALU.add)
                    if i == 15:
                        a_ms.inc(ts)
                    a_fr.inc(v.nop())
        M["stats"] = a_ms.n

        with nc.Block() as blk:
            @blk.scalar
            def _(sc):
                sc.wait_ge(a_ms.h, M["stats"])
                m_sq.inc(sc.activation(sd_all[:], var_all[:], AF.Sqrt))

            @blk.vector
            def _(v):
                v.wait_ge(m_sq.h, 1)
                m_rs.inc(v.reciprocal(rstd_all[:], sd_all[:]))
                v.wait_ge(m_rs.h, 1)
        M["rstd"] = m_rs.n

        for wi, (wname, WT_d, Bm, Am) in enumerate(
                [("q", WqT, Bq, AqT), ("k", WkT, Bk, AkT), ("v", WvT, Bv, AvT)]):
            bR = [f_raw[0].n, f_raw[1].n]
            bC = f_cmb.n
            bM = f_mm.n
            with nc.Block() as blk:
                @blk.gpsimd
                def _(g, Bm=Bm, Am=Am, WT_d=WT_d, bC=bC):
                    g.wait_ge(f_cmb.h, bC)
                    f_ba.inc(g.dma_start(out=lor_b[:, 0:D], in_=Bm), 16)
                    f_ba.inc(g.dma_start(out=at_b[:], in_=Am), 16)
                    for dc in range(NDC):
                        if dc >= 2:
                            g.wait_ge(f_cmb.h, bC + dc - 1)
                        f_raw[dc % 2].inc(g.dma_start(
                            out=raw_b[dc % 2][:], in_=WT_d[dc * 128:(dc + 1) * 128, :]), 16)
                M[f"ba_{wname}"] = f_ba.n

                @blk.tensor
                def _(t, wname=wname, bC=bC):
                    t.wait_ge(f_ba.h, M[f"ba_{wname}"])
                    for dc in range(NDC):
                        if dc >= 2:
                            t.wait_ge(f_cmb.h, bC + dc - 1)
                        f_mm.inc(t.matmul(PB[dc % 2][:, 0:256],
                                          lor_b[:, dc * 128:(dc + 1) * 128],
                                          at_b[:], start=True, stop=True))

                @blk.vector
                def _(v, wname=wname, bM=bM, bR=bR):
                    for dc in range(NDC):
                        v.wait_ge(f_mm.h, bM + dc + 1)
                        v.wait_ge(f_raw[dc % 2].h, bR[dc % 2] + 16 * (dc // 2 + 1))
                        f_cmb.inc(v.scalar_tensor_tensor(
                            weff(wname, dc), PB[dc % 2][:, 0:256], LORA_S,
                            raw_b[dc % 2][:], op0=ALU.mult, op1=ALU.add))
            M[f"fold_{wname}"] = f_cmb.n

        bR = [f_raw[0].n, f_raw[1].n]
        bC = f_cmb.n
        bM = f_mm.n
        bW = [f_wst[0].n, f_wst[1].n]
        with nc.Block() as blk:
            @blk.sync
            def _(s):
                for dc in range(NDC):
                    s.wait_ge(f_cmb.h, bC + 4 * (dc + 1))
                    f_wst[dc % 2].inc(s.dma_start(
                        out=WoTe[dc * 128:(dc + 1) * 128, :], in_=woe_b[dc % 2][:]), 16)

            @blk.gpsimd
            def _(g):
                g.wait_ge(f_cmb.h, bC)
                f_ba.inc(g.dma_start(out=lor_b[:, 0:D], in_=Bo), 16)
                f_ba.inc(g.dma_start(out=aot_b[:], in_=AoT), 16)
                M["ba_o"] = f_ba.n
                for dc in range(NDC):
                    if dc >= 2:
                        g.wait_ge(f_cmb.h, bC + 4 * (dc - 1))
                    f_raw[dc % 2].inc(g.dma_start(
                        out=rawo_b[dc % 2][:], in_=WoT[dc * 128:(dc + 1) * 128, :]), 16)

            @blk.tensor
            def _(t):
                t.wait_ge(f_ba.h, M["ba_o"])
                for dc in range(NDC):
                    for ot in range(4):
                        idx = dc * 4 + ot
                        if idx >= 2:
                            t.wait_ge(f_cmb.h, bC + idx - 1)
                        f_mm.inc(t.matmul(PB[idx % 2][:, 0:512],
                                          lor_b[:, dc * 128:(dc + 1) * 128],
                                          aot_b[:, ot * 512:(ot + 1) * 512],
                                          start=True, stop=True))

            @blk.vector
            def _(v):
                for dc in range(NDC):
                    if dc >= 2:
                        v.wait_ge(f_wst[dc % 2].h, bW[dc % 2] + 16 * (dc // 2 - 0))
                    for ot in range(4):
                        idx = dc * 4 + ot
                        v.wait_ge(f_mm.h, bM + idx + 1)
                        v.wait_ge(f_raw[dc % 2].h, bR[dc % 2] + 16 * (dc // 2 + 1))
                        f_cmb.inc(v.scalar_tensor_tensor(
                            woe_b[dc % 2][:, ot * 512:(ot + 1) * 512],
                            PB[idx % 2][:, 0:512], LORA_S,
                            rawo_b[dc % 2][:, ot * 512:(ot + 1) * 512],
                            op0=ALU.mult, op1=ALU.add))
        M["fold_o"] = f_cmb.n
        M["woTe_w0"] = f_wst[0].n
        M["woTe_w1"] = f_wst[1].n

        with nc.Block() as blk:
            @blk.sync
            def _(s):
                for i in range(16):
                    if i >= 2:
                        s.wait_ge(b_fr.h, i - 1)
                    b_x[i % 2].inc(s.dma_start(out=xt_f32[i % 2],
                                               in_=x_my[i * 128:(i + 1) * 128, :]), 16)
                    if "tm" not in KCUT:
                        s.wait_ge(b_xn.h, i + 1)
                        b_st[i % 2].inc(s.dma_start(out=xn_tm_c[i * 128:(i + 1) * 128, :],
                                                    in_=xnb_ev[i % 2][:]), 16)
                    if i % 2 == 1 and "fm" not in KCUT:
                        s.wait_ge(b_fmc.h, 16 * (i + 1))
                        b_sf[(i // 2) % 2].inc(s.dma_start(
                            out=xn_fm_c.rearrange("(dc p) t -> p dc t", p=128)
                                [:, :, (i - 1) * 128:(i + 1) * 128],
                            in_=fmstage[:].rearrange("p (dc t) -> p dc t", dc=NDC)), 16)

            @blk.vector
            def _(v):
                v.wait_ge(m_rs.h, M["rstd"])
                for i in range(16):
                    v.wait_ge(b_x[i % 2].h, 16 * (i // 2 + 1))
                    if i >= 2:
                        if "tm" not in KCUT:
                            v.wait_ge(b_st[i % 2].h, 16 * ((i - 2) // 2 + 1))
                        if "tr" not in KCUT:
                            v.wait_ge(b_tr.h, 16 * (i - 1))
                        P = (i - 2) // 2
                        if "fm" not in KCUT:
                            v.wait_ge(b_sf[0].h, 16 * (P // 2 + 1))
                            if P >= 1:
                                v.wait_ge(b_sf[1].h, 16 * ((P - 1) // 2 + 1))
                        if "lg" not in KCUT:
                            v.wait_ge(b_lg.h, i - 1)
                    b_xn.inc(v.tensor_scalar(xnb_ev[i % 2][:], xt_f32[i % 2],
                                             mean_all[:, i:i + 1], rstd_all[:, i:i + 1],
                                             op0=ALU.subtract, op1=ALU.mult))
                    b_fr.inc(v.nop())
                    if "tr" not in KCUT:
                        for dc in range(NDC):
                            g2 = i * 16 + dc
                            v.wait_ge(b_tr.h, g2 + 1)
                            b_fmc.inc(v.tensor_copy(
                                fmstage[:, dc * 256 + (i % 2) * 128:
                                        dc * 256 + (i % 2) * 128 + 128],
                                PT[dc % 2][:]))
                    if "lg" not in KCUT and "le" not in KCUT and "lev" not in KCUT:
                        v.wait_ge(b_lg.h, i + 1)
                        b_le.inc(v.tensor_copy(logits_all[:, 8 * i:8 * i + 8],
                                               PA[2 + i % 2][:, 0:8]))

            @blk.tensor
            def _(t):
                t.wait_ge(g0.h, M["consts"])
                for i in range(16):
                    t.wait_ge(b_xn.h, i + 1)
                    if "tr" not in KCUT:
                        for dc in range(NDC):
                            g2 = i * 16 + dc
                            if g2 >= 2:
                                t.wait_ge(b_fmc.h, g2 - 1)
                            b_tr.inc(t.transpose(PT[dc % 2][:],
                                                 xnb_ev[i % 2][:, dc * 128:(dc + 1) * 128],
                                                 ident_sb[:]))
                    if "lg" not in KCUT:
                        t.wait_ge(b_fmc.h, 16 * (i + 1))
                        if i >= 2 and "le" not in KCUT:
                            t.wait_ge(b_le.h, i - 1)
                        for dc in range(NDC):
                            mmi = t.matmul(PA[2 + i % 2][:, 0:8],
                                           fmstage[:, dc * 256 + (i % 2) * 128:
                                                   dc * 256 + (i % 2) * 128 + 128],
                                           wg_sb[:, dc * 8:(dc + 1) * 8],
                                           start=(dc == 0), stop=(dc == NDC - 1))
                        b_lg.inc(mmi)

            @blk.scalar
            def _(sc):
                if "lg" not in KCUT and "le" not in KCUT and "lev" in KCUT:
                    for i in range(16):
                        sc.wait_ge(b_lg.h, i + 1)
                        b_le.inc(sc.activation(logits_all[:, 8 * i:8 * i + 8],
                                               PA[2 + i % 2][:, 0:8], AF.Copy))

        with nc.Block() as blk:
            @blk.gpsimd
            def _(g):
                g.wait_ge(b_st[0].h, b_st[0].n)
                g.wait_ge(b_st[1].h, b_st[1].n)
                g.wait_ge(b_sf[0].h, b_sf[0].n)
                g.wait_ge(b_sf[1].h, b_sf[1].n)
                cc.inc(g.collective_compute(
                    "AllGather", ALU.bypass, replica_groups=[list(range(8))],
                    ins=[xn_fm_c], outs=[xn_fm_full]))
        M["cc_ag1"] = cc.n

        if DP >= 2:
            with nc.Block() as blk:
                @blk.vector
                def _(v):
                    v.wait_ge(b_le.h, 16)
                    for i in range(16):
                        last = v.reduce_max(nm_all[:, i:i + 1],
                                            logits_all[:, 8 * i:8 * i + 8],
                                            axis=AX.X, negate=True)
                    r_nm.inc(last)
                    v.wait_ge(r_nm.h, 1)
                    r_nm.inc(v.tensor_scalar(pmax_all[:], nm_all[:], -1.0, None,
                                             op0=ALU.mult))

                @blk.scalar
                def _(sc):
                    sc.wait_ge(r_nm.h, 2)
                    for i in range(16):
                        r_ex.inc(sc.activation(junk8[:], logits_all[:, 8 * i:8 * i + 8],
                                               AF.Exp, bias=nm_all[:, i:i + 1],
                                               accum_out=se_all[:, i:i + 1]))
            M["exp"] = r_ex.n

            with nc.Block() as blk:
                @blk.vector
                def _(v):
                    v.wait_ge(r_ex.h, M["exp"])
                    v.reciprocal(gate_sb[:], se_all[:])
                    for i in range(16):
                        if i >= 1:
                            v.wait_ge(r_mmA.h, 3 * (i - 1) + 3)
                        VE(v, v.tensor_scalar(junk8[:], logits_all[:, 8 * i:8 * i + 8],
                                              pmax_all[:, i:i + 1], None, op0=ALU.is_equal))
                        r_vA.inc(v.scalar_tensor_tensor(junk8b[:], junk8[:], -1000.0,
                                                        iota1k_sb[:],
                                                        op0=ALU.mult, op1=ALU.add))
                        v.wait_ge(r_vA.h, r_vA.n)
                        r_vA.inc(v.tensor_reduce(eidx_sb[:, i:i + 1], junk8b[:],
                                                 op=ALU.min, axis=AX.X))
                        v.wait_ge(r_vA.h, 5 * i + 2)
                        if DP == 2:
                            v.tensor_copy(logits_all2[:, 8 * i:8 * i + 8], junk8[:])
                            v.tensor_copy(logits_all3[:, 8 * i:8 * i + 8], junk8b[:])
                        VE(v, v.tensor_scalar(oh_all[:, 8 * i:8 * i + 8], iota8_sb[:],
                                              eidx_sb[:, i:i + 1], None, op0=ALU.is_equal))
                        r_vA.inc(v.tensor_copy(ohf_sb[:], oh_all[:, 8 * i:8 * i + 8]))
                        v.wait_ge(r_mmA.h, 3 * i + 1)
                        r_vA.inc(v.tensor_copy(pos_all[:, 8 * i:8 * i + 8],
                                               PA[i % 2][:, 0:8]))
                        v.wait_ge(r_mmA.h, 3 * i + 3)
                        v.tensor_copy(histco[:, i:i + 1], PA[2 + i % 2][0:8, 127:128])
                        r_vA.inc(v.tensor_copy(ohT_all[:, 128 * i:128 * (i + 1)],
                                               PB[i % 2][0:8, 0:128]))
                    r_vA.inc(v.nop())
                    v.wait_ge(r_vA.h, r_vA.n)
                    r_vA.inc(v.tensor_tensor_scan(pref[:], ones816_sb[:], histco[:],
                                                  0.0, op0=ALU.mult, op1=ALU.add))
                    v.wait_ge(r_vA.h, r_vA.n)

                @blk.tensor
                def _(t):
                    for i in range(16):
                        t.wait_ge(r_vA.h, 5 * i + 3)
                        if i >= 2:
                            t.wait_ge(r_vA.h, 5 * (i - 2) + 5)
                        r_mmA.inc(t.matmul(PA[i % 2][:, 0:8], tri_sb[:],
                                           oh_all[:, 8 * i:8 * i + 8],
                                           start=True, stop=True))
                        t.wait_ge(r_vA.h, 5 * i + 4)
                        r_mmA.inc(t.transpose(PA[2 + i % 2][0:8, 0:128],
                                              pos_all[:, 8 * i:8 * i + 8], id32_sb[:]))
                        r_mmA.inc(t.transpose(PB[i % 2][0:8, 0:128], ohf_sb[:],
                                              id32_sb[:]))
            M["p2a"] = r_vA.n

            with nc.Block() as blk:
                @blk.sync
                def _(s):
                    s.wait_ge(r_vA.h, M["p2a"])
                    r_hs.inc(s.dma_start(out=hist_c, in_=pref[:, 15:16]), 16)

                @blk.gpsimd
                def _(g):
                    g.wait_ge(r_hs.h, 16)
                    cc.inc(g.collective_compute(
                        "AllGather", ALU.bypass, replica_groups=[list(range(8))],
                        ins=[hist_c], outs=[hist_full]))
            M["cc_ag2"] = cc.n

            with nc.Block() as blk:
                @blk.sync
                def _(s):
                    s.wait_ge(cc.h, M["cc_ag2"])
                    with nc.allow_non_contiguous_dma("tiny 64-elem strided load"):
                        r_hl.inc(s.dma_start(
                            out=hf_sb[:].rearrange("e (j one) -> e j one", one=1),
                            in_=hist_full.rearrange("(j e) one -> e j one", e=8)), 16)

                @blk.vector
                def _(v):
                    v.wait_ge(r_hl.h, 16)
                    r_off.inc(v.tensor_tensor(hm_sb[:], hf_sb[:], maskp_sb[:], op=ALU.mult))
                    v.wait_ge(r_off.h, 1)
                    r_off.inc(v.tensor_reduce(offc[:], hm_sb[:], op=ALU.add, axis=AX.X))
                    v.wait_ge(r_off.h, 2)
                    VE(v, v.tensor_scalar(offm1[:], pref[:], offc[:], None, op0=ALU.add))
                    r_off.inc(v.tensor_tensor(offmF[:], offm1[:], histco[:],
                                              op=ALU.subtract))
            M["offm"] = r_off.n

            with nc.Block() as blk:
                @blk.tensor
                def _(t):
                    t.wait_ge(r_off.h, M["offm"])
                    for i in range(16):
                        if i >= 2:
                            t.wait_ge(r_vB.h, 3 * i - 3)
                        r_mmB.inc(t.matmul(PB[i % 2][:, 0:1],
                                           ohT_all[:, 128 * i:128 * (i + 1)],
                                           offmF[:, i:i + 1], start=True, stop=True))

                @blk.vector
                def _(v):
                    for i in range(16):
                        v.wait_ge(r_mmB.h, i + 1)
                        r_vB.inc(v.tensor_tensor(junk8[:], oh_all[:, 8 * i:8 * i + 8],
                                                 pos_all[:, 8 * i:8 * i + 8], op=ALU.mult))
                        v.wait_ge(r_vB.h, r_vB.n)
                        r_vB.inc(v.tensor_reduce(r_c1[:], junk8[:], op=ALU.add, axis=AX.X))
                        v.wait_ge(r_vB.h, r_vB.n)
                        VE(v, v.tensor_tensor(r_c2[:], r_c1[:], PB[i % 2][:, 0:1],
                                              op=ALU.add))
                        VE(v, v.tensor_scalar(posd_sb[:, i:i + 1], r_c2[:], -1.0, None,
                                              op0=ALU.add))
                        VE(v, v.tensor_scalar(keep_sb[:, i:i + 1], posd_sb[:, i:i + 1],
                                              float(CAP), None, op0=ALU.is_lt))
                        VE(v, v.tensor_scalar(r_c1[:], eidx_sb[:, i:i + 1], float(CAP),
                                              None, op0=ALU.mult))
                        VE(v, v.tensor_tensor(slotf_sb[:, i:i + 1], r_c1[:],
                                              posd_sb[:, i:i + 1], op=ALU.add))
                        VE(v, v.tensor_scalar(r_c2[:], keep_sb[:, i:i + 1], -1.0, 1.0,
                                              op0=ALU.mult, op1=ALU.add))
                        VE(v, v.tensor_tensor(r_c1[:], slotf_sb[:, i:i + 1],
                                              keep_sb[:, i:i + 1], op=ALU.mult))
                        VE(v, v.scalar_tensor_tensor(r_c3[:], r_c2[:], SENT,
                                                     r_c1[:], op0=ALU.mult, op1=ALU.add))
                        VE(v, v.tensor_copy(slotu_sb[:, i:i + 1], r_c3[:]))
                        r_vB.inc(v.tensor_tensor(gs_sb[:, i:i + 1], gate_sb[:, i:i + 1],
                                                 keep_sb[:, i:i + 1], op=ALU.mult))
            M["route"] = r_vB.n

            bBX = [b_x[0].n, b_x[1].n]
            bXS = [x_sc[0].n, x_sc[1].n]
            with nc.Block() as blk:
                @blk.sync
                def _(s):
                    s.wait_ge(b_st[0].h, b_st[0].n)
                    s.wait_ge(b_st[1].h, b_st[1].n)
                    for i in range(16):
                        if i >= 2:
                            s.wait_ge(x_sc[i % 2].h, bXS[i % 2] + 16 * ((i - 2) // 2 + 1))
                        b_x[i % 2].inc(s.dma_start(
                            out=xnb_ev[i % 2],
                            in_=xn_tm_c[i * 128:(i + 1) * 128, :]), 16)

                @blk.gpsimd
                def _(g):
                    g.wait_ge(r_vB.h, M["route"])
                    g.wait_ge(g_gs.h, M["xs_zero_g"])
                    g.wait_ge(z_as.h, M["xs_zero_a"])
                    for i in range(16):
                        g.wait_ge(b_x[i % 2].h, bBX[i % 2] + 16 * (i // 2 + 1))
                        x_sc[i % 2].inc(g.indirect_dma_start(
                            out=xs_c,
                            out_offset=bass.IndirectOffsetOnAxis(slotu_sb[:, i:i + 1], 0),
                            in_=xnb_ev[i % 2], in_offset=None,
                            bounds_check=TTOK - 1, oob_is_err=False), 16)

            def emit_a2a3():
                with nc.Block() as blk:
                    @blk.gpsimd
                    def _(g):
                        g.wait_ge(x_sc[0].h, x_sc[0].n)
                        g.wait_ge(x_sc[1].h, x_sc[1].n)
                        with allow_ici_all_to_all():
                            cc.inc(g.collective_compute(
                                "AllToAll", ALU.bypass, replica_groups=[list(range(8))],
                                ins=[xs_c], outs=[xs_all]))
                M["cc_a2a3"] = cc.n
            if DP == 2:
                emit_a2a3()

        if DP >= 3:
            def fmb(par):
                return O_FM + par * 8192

            with nc.Block() as blk:
                @blk.vector
                def _(v):
                    p3_vm.inc(v.memset(ar(O_VAUG, VAUG_SZ), 1.0))

                @blk.sync
                def _(s):
                    s.wait_ge(cc.h, M["cc_ag1"])
                    s.wait_ge(f_cmb.h, M["fold_o"])
                    for tt in range(32):
                        r, tl = tt // 4, tt % 4
                        if tt >= 2:
                            s.wait_ge(p3_mmA.h, 4 * (tt - 1))
                            s.wait_ge(p3_mmV.h, 4 * (tt - 1))
                        for dcg in range(4):
                            src = xn_fm_full[r * 2048 + dcg * 512: r * 2048 + (dcg + 1) * 512,
                                             tl * 512:(tl + 1) * 512]
                            p3_fm[tt % 2].inc(s.dma_start(
                                out=ar(fmb(tt % 2) + dcg * 2048, 2048).rearrange(
                                    "p (dc t) -> p dc t", dc=4),
                                in_=src.rearrange("(dc p) t -> p dc t", p=128)), 16)
                        for g in range(4):
                            pidx = (tt - 1) * 4 + g
                            if pidx >= 0:
                                s.wait_ge(p3_ev.h, pidx + 1)
                                pg = pidx % 4
                                dst = Kfm if pg < 2 else Qfm
                                hl = pg % 2
                                ptt = pidx // 4
                                p3_ks[pidx % 4].inc(s.dma_start(
                                    out=dst[hl * 128:(hl + 1) * 128,
                                            ptt * 512:(ptt + 1) * 512],
                                    in_=kq_ev[pidx % 4][:]), 16)
                    for g in range(4):
                        pidx = 31 * 4 + g
                        s.wait_ge(p3_ev.h, pidx + 1)
                        pg = pidx % 4
                        dst = Kfm if pg < 2 else Qfm
                        hl = pg % 2
                        p3_ks[pidx % 4].inc(s.dma_start(
                            out=dst[hl * 128:(hl + 1) * 128, 31 * 512:32 * 512],
                            in_=kq_ev[pidx % 4][:]), 16)

                @blk.tensor
                def _(t):
                    t.wait_ge(f_cmb.h, M["fold_o"])
                    t.wait_ge(p3_vm.h, 1)
                    for tt in range(32):
                        t.wait_ge(p3_fm[tt % 2].h, 64 * (tt // 2 + 1))
                        for g in range(4):
                            idx = tt * 4 + g
                            w = "k" if g < 2 else "q"
                            hl = g % 2
                            if idx >= 2:
                                t.wait_ge(p3_ev.h, idx - 1)
                            for dc in range(NDC):
                                mmi = t.matmul(PA[g % 2][:, 0:512],
                                               weff(w, dc)[:, hl * 128:(hl + 1) * 128],
                                               ar(fmb(tt % 2) + dc * 512, 512),
                                               start=(dc == 0), stop=(dc == NDC - 1))
                            p3_mmA.inc(mmi)
                        for tc in range(4):
                            idx = tt * 4 + tc
                            if idx >= 2:
                                t.wait_ge(p3_vv.h, idx - 1)
                            for dc in range(NDC):
                                t.matmul(PB[tc % 2][:, 0:256],
                                         ar(fmb(tt % 2) + dc * 512 + tc * 128, 128),
                                         weff("v", dc), start=(dc == 0), stop=False)
                            p3_mmV.inc(t.matmul(PB[tc % 2][:, 0:256], ones_sb[:],
                                                bvr_sb[:], start=False, stop=True))

                @blk.vector
                def _(v):
                    for tt in range(32):
                        for g in range(4):
                            idx = tt * 4 + g
                            bcol = (bkc_sb if g < 2 else bqc_sb)[:, g % 2:g % 2 + 1]
                            v.wait_ge(p3_mmA.h, idx + 1)
                            if idx >= 4:
                                v.wait_ge(p3_ks[idx % 4].h, 16 * (idx // 4))
                            p3_ev.inc(v.tensor_scalar(kq_ev[idx % 4][:], PA[g % 2][:],
                                                      bcol, None, op0=ALU.add))
                        for tc in range(4):
                            idx = tt * 4 + tc
                            v.wait_ge(p3_mmV.h, idx + 1)
                            off = O_VAUG + idx * 258
                            VE(v, v.tensor_copy(ar(off, 128), PB[tc % 2][:, 0:128]))
                            p3_vv.inc(v.tensor_copy(ar(off + 129, 128),
                                                    PB[tc % 2][:, 128:256]))
            M["p3_mmA"] = p3_mmA.n
            M["p3_ks"] = [p3_ks[i].n for i in range(4)]
            emit_a2a3()

            with nc.Block() as blk:
                @blk.gpsimd
                def _(g):
                    for dst, src in ((W1b, W1t), (W3b, W3t), (W2b, W2d),
                                     (sw1b, sw1t), (sw3b, sw3t), (sw2b, sw2d)):
                        wc_s.inc(g.dma_start(out=dst, in_=src), 16)

        if DP >= 4:
            def kh(par):
                return O_KH + par * 4096

            def qp(par):
                return O_QP + par * 512

            def et(par):
                return O_ET

            with nc.Block() as blk:
                @blk.sync
                def _(s):
                    for i4 in range(4):
                        s.wait_ge(p3_ks[i4].h, M["p3_ks"][i4])
                    for ph in range(8):
                        b, hl = ph // 2, ph % 2
                        if ph >= 2:
                            s.wait_ge(q_mmS.h, 256 * (ph - 1))
                        q_kh[ph % 2].inc(s.dma_start(
                            out=ar(kh(ph % 2), 4096),
                            in_=Kfm[hl * 128:(hl + 1) * 128, b * 4096:(b + 1) * 4096]), 16)
                        for qb in range(8):
                            gqb = ph * 8 + qb
                            if gqb >= 2:
                                s.wait_ge(q_mmS.h, 32 * (gqb - 1))
                            q_qp[gqb % 2].inc(s.dma_start(
                                out=ar(qp(gqb % 2), 512),
                                in_=Qfm[hl * 128:(hl + 1) * 128,
                                        b * 4096 + qb * 512: b * 4096 + (qb + 1) * 512]), 16)
                        if hl == 1:
                            s.wait_ge(q_av.h, 64 * (b + 1))
                            for half in range(2):
                                q_st.inc(s.dma_start(
                                    out=att_c[b * 4096 + half * 2048:
                                              b * 4096 + (half + 1) * 2048, :]
                                        .rearrange("(q p) f -> p q f", p=128),
                                    in_=ar(O_STG + half * 4096, 4096).rearrange(
                                        "p (q f) -> p q f", f=256)), 16)

                @blk.tensor
                def _(t):
                    for ph in range(8):
                        b, hl = ph // 2, ph % 2
                        t.wait_ge(q_kh[ph % 2].h, 16 * (ph // 2 + 1))
                        for qb in range(8):
                            gqb = ph * 8 + qb
                            t.wait_ge(q_qp[gqb % 2].h, 16 * (gqb // 2 + 1))
                            for kc in range(32):
                                gsc = gqb * 32 + kc
                                if gsc >= 2:
                                    t.wait_ge(q_ex.h, gsc - 1)
                                q_mmS.inc(t.matmul(PA[kc % 2][:],
                                                   ar(kh(ph % 2) + kc * 128, 128),
                                                   ar(qp(gqb % 2), 512),
                                                   start=True, stop=True))
                            t.wait_ge(q_ex.h, 32 * (gqb + 1))
                            for qs in range(4):
                                gqs = gqb * 4 + qs
                                if gqs >= 2:
                                    t.wait_ge(q_av.h, gqs - 1)
                                for kc in range(32):
                                    gtc = b * 32 + kc
                                    mmi = t.matmul(PA[2 + qs % 2][:, 0:129],
                                                   ar(et(gqb % 2) + kc * 512 + qs * 128, 128),
                                                   ar(O_VAUG + gtc * 258 + hl * 129, 129),
                                                   start=(kc == 0), stop=(kc == 31))
                                q_mmA.inc(mmi)

                @blk.scalar
                def _(sc):
                    for ph in range(8):
                        for qb in range(8):
                            gqb = ph * 8 + qb
                            for kc in range(32):
                                gsc = gqb * 32 + kc
                                sc.wait_ge(q_mmS.h, gsc + 1)
                                if kc == 0 and gqb >= 1:
                                    sc.wait_ge(q_mmA.h, 4 * gqb)
                                q_ex.inc(sc.activation(
                                    ar(et(gqb % 2) + kc * 512, 512), PA[kc % 2][:],
                                    AF.Exp, bias=csh_sb[:], scale=SCALE))

                @blk.vector
                def _(v):
                    for ph in range(8):
                        b, hl = ph // 2, ph % 2
                        for qb in range(8):
                            for qs in range(4):
                                gqs = (ph * 8 + qb) * 4 + qs
                                qspp = qb * 4 + qs
                                if hl == 0 and qspp == 0 and b >= 1:
                                    v.wait_ge(q_st.h, 32 * b)
                                v.wait_ge(q_mmA.h, gqs + 1)
                                q_rc.inc(v.reciprocal(rec_sb[:],
                                                      PA[2 + qs % 2][:, 128:129]))
                                v.wait_ge(q_rc.h, q_rc.n)
                                q_av.inc(v.tensor_scalar(
                                    ar(O_STG + qspp * 256 + hl * 128, 128),
                                    PA[2 + qs % 2][:, 0:128], rec_sb[:], None,
                                    op0=ALU.mult))
            M["p4_av"] = q_mmA.n
            M["p4_st"] = q_st.n

            with nc.Block() as blk:
                @blk.gpsimd
                def _(g):
                    g.wait_ge(q_st.h, M["p4_st"])
                    with allow_ici_all_to_all():
                        cc.inc(g.collective_compute(
                            "AllToAll", ALU.bypass, replica_groups=[list(range(8))],
                            ins=[att_c], outs=[att_all]))
            M["cc_a2a4"] = cc.n

        if DP >= 5:
            def emit_ffn(tag, w1src, w3src, w2src, out_dram, moe, prev_gate, wthr):
                for half in range(2):
                    gate = prev_gate if half == 0 else M[f"{tag}_w2g_{half - 1}"]
                    bTR = n_tr.n
                    bTC = n_tc.n
                    bGA = [n_ga[0].n, n_ga[1].n]
                    bGX = n_gx.n
                    if moe:
                        bSL = [m_sl[0].n, m_sl[1].n]
                        bMG = m_mg.n
                        with nc.Block() as blk:
                            @blk.sync
                            def _(s, half=half, gate=gate, bSL=bSL, bMG=bMG):
                                s.wait_ge(cc.h, M["cc_a2a3"])
                                s.wait_ge(gate[0].h, gate[1])
                                for st_ in range(8):
                                    for j in range(8):
                                        gidx = st_ * 8 + j
                                        if gidx >= 2:
                                            s.wait_ge(m_mg.h, bMG + gidx - 1)
                                        m_sl[j % 2].inc(s.dma_start(
                                            out=xsh[j % 2],
                                            in_=xs_all[j * 2048 + half * 1024 + st_ * 128:
                                                       j * 2048 + half * 1024 + (st_ + 1) * 128,
                                                       :]), 16)

                            @blk.vector
                            def _(v, half=half, gate=gate, bSL=bSL, bTR=bTR):
                                v.wait_ge(gate[0].h, gate[1])

                                def copies(st2):
                                    for dc in range(NDC):
                                        g2 = st2 * 16 + dc
                                        v.wait_ge(n_tr.h, bTR + g2 + 1)
                                        n_tc.inc(v.tensor_copy(
                                            ar(O_XD + dc * 1024 + st2 * 128, 128),
                                            PT[dc % 2][:]))

                                for st_ in range(8):
                                    for j in range(8):
                                        v.wait_ge(m_sl[j % 2].h,
                                                  bSL[j % 2] + 16 * (st_ * 4 + j // 2 + 1))
                                        if j == 0:
                                            if st_ >= 2:
                                                v.wait_ge(n_tr.h, bTR + 16 * (st_ - 1))
                                            ii5 = v.tensor_copy(accS, xsh[0])
                                        else:
                                            dst = xdtm[st_ % 2] if j % 2 == 1 else accS
                                            srcacc = accS if j % 2 == 1 else xdtm[st_ % 2]
                                            ii5 = v.tensor_tensor(dst, srcacc,
                                                                  xsh[j % 2], op=ALU.add)
                                        m_mg.inc(ii5)
                                        v.wait_ge(m_mg.h, m_mg.n)
                                    if st_ >= 1:
                                        copies(st_ - 1)
                                copies(7)

                            @blk.tensor
                            def _(t, gate=gate, bMG=bMG, bTC=bTC):
                                t.wait_ge(gate[0].h, gate[1])
                                for st_ in range(8):
                                    t.wait_ge(m_mg.h, bMG + 8 * (st_ + 1))
                                    for dc in range(NDC):
                                        g2 = st_ * 16 + dc
                                        if g2 >= 2:
                                            t.wait_ge(n_tc.h, bTC + g2 - 1)
                                        n_tr.inc(t.transpose(
                                            PT[dc % 2][:],
                                            xdtm[st_ % 2][:, dc * 128:(dc + 1) * 128],
                                            ident_sb[:]))
                        M[f"{tag}_xd_{half}"] = ("v", n_tc.n)
                    else:
                        with nc.Block() as blk:
                            @blk.sync
                            def _(s, half=half, gate=gate):
                                s.wait_ge(gate[0].h, gate[1])
                                s.wait_ge(b_sf[0].h, b_sf[0].n)
                                s.wait_ge(b_sf[1].h, b_sf[1].n)
                                for dcg in range(4):
                                    n_gx.inc(s.dma_start(
                                        out=ar(O_XD + dcg * 4096, 4096).rearrange(
                                            "p (dc t) -> p dc t", dc=4),
                                        in_=xn_fm_c.rearrange("(dc p) t -> p dc t", p=128)
                                            [:, dcg * 4:(dcg + 1) * 4,
                                             half * 1024:(half + 1) * 1024]), 16)
                        M[f"{tag}_xd_{half}"] = ("s", n_gx.n)

                    if tag == "sh" and "shh" in KCUT:
                        M[f"{tag}_h_{half}"] = n_ht.n
                        M[f"{tag}_w2g_{half}"] = (n_gx, n_gx.n)
                        continue
                    bW1 = [n_w1[0].n, n_w1[1].n]
                    bHM = n_hm.n
                    bSI = n_si.n
                    bHT = n_ht.n
                    with nc.Block() as blk:
                        @blk.sync
                        def _(s, w1src=w1src, w3src=w3src, gate=gate, bHT=bHT,
                              half=half, wthr=wthr):
                            s.wait_ge(gate[0].h, gate[1])
                            if half == 0:
                                s.wait_ge(wc_s.h, wthr[0])
                            for hc in range(NHC):
                                if hc >= 2:
                                    s.wait_ge(n_ht.h, bHT + 2 * (hc - 1))
                                for wj, wsrc in ((0, w1src), (1, w3src)):
                                    for dcg in range(4):
                                        n_w1[hc % 2].inc(s.dma_start(
                                            out=ar(O_WB + (hc % 2) * 4096 + wj * 2048
                                                   + dcg * 512, 512).rearrange(
                                                "p (dc h) -> p dc h", dc=4),
                                            in_=wsrc[hc, dcg * 4:(dcg + 1) * 4].rearrange(
                                                "dc p h -> p dc h")), 16)

                        @blk.tensor
                        def _(t, half=half, tag=tag, bW1=bW1, bHT=bHT):
                            kind, val = M[f"{tag}_xd_{half}"]
                            t.wait_ge((n_tc if kind == "v" else n_gx).h, val)
                            for hc in range(NHC):
                                t.wait_ge(n_w1[hc % 2].h, bW1[hc % 2] + 128 * (hc // 2 + 1))
                                for t2 in range(2):
                                    hidx = hc * 2 + t2
                                    if hidx >= 2:
                                        t.wait_ge(n_ht.h, bHT + hidx - 1)
                                    for wj in range(2):
                                        dst = PA[t2 % 2] if wj == 0 else PA[2 + t2 % 2]
                                        for dc in range(NDC):
                                            mmi = t.matmul(dst[:],
                                                           ar(O_WB + (hc % 2) * 4096
                                                              + wj * 2048 + dc * 128, 128),
                                                           ar(O_XD + dc * 1024 + t2 * 512, 512),
                                                           start=(dc == 0),
                                                           stop=(dc == NDC - 1))
                                    n_hm.inc(mmi)

                        @blk.scalar
                        def _(sc, bHM=bHM, bHT=bHT):
                            for hc in range(NHC):
                                for t2 in range(2):
                                    hidx = hc * 2 + t2
                                    sc.wait_ge(n_hm.h, bHM + hidx + 1)
                                    if hidx >= 2:
                                        sc.wait_ge(n_ht.h, bHT + hidx - 1)
                                    n_si.inc(sc.activation(s1_ev[t2 % 2][:],
                                                           PA[t2 % 2][:], AF.Silu))

                        @blk.vector
                        def _(v, bSI=bSI):
                            for hc in range(NHC):
                                for t2 in range(2):
                                    hidx = hc * 2 + t2
                                    v.wait_ge(n_si.h, bSI + hidx + 1)
                                    n_ht.inc(v.tensor_tensor(
                                        ar(O_HH + hc * 1024 + t2 * 512, 512),
                                        s1_ev[t2 % 2][:], PA[2 + t2 % 2][:],
                                        op=ALU.mult))

                        if DP == 5 and moe and half == 0:
                            @blk.sync
                            def _(s, bHT=bHT, half=half, tag=tag):
                                kind, val = M[f"{tag}_xd_{half}"]
                                s.wait_ge((n_tc if kind == "v" else n_gx).h, val)
                                for dc in range(NDC):
                                    dx_s.inc(s.dma_start(
                                        out=dbg["d_xd"][dc * 128:(dc + 1) * 128, :],
                                        in_=ar(O_XD + dc * 1024, 1024)), 16)
                                s.wait_ge(n_ht.h, bHT + 2 * NHC)
                                for hc in range(NHC):
                                    dx_s.inc(s.dma_start(
                                        out=dbg["d_hh"][hc * 128:(hc + 1) * 128, :],
                                        in_=ar(O_HH + hc * 1024, 1024)), 16)
                    M[f"{tag}_h_{half}"] = n_ht.n

                    if tag == "sh" and "shw2" in KCUT:
                        M[f"{tag}_w2g_{half}"] = (n_ht, n_ht.n)
                        continue
                    for oh in range(2):
                        bW2 = n_w2.n
                        bM2 = n_m2.n
                        bOE = n_oe.n
                        bOS = [n_os[0].n, n_os[1].n]
                        with nc.Block() as blk:
                            @blk.tensor
                            def _(t, bW2=bW2, bOE=bOE):
                                t.wait_ge(n_w2.h, bW2 + 16 * NHC)
                                for stl in range(8):
                                    for otl in range(2):
                                        idx = stl * 2 + otl
                                        if idx >= 2:
                                            t.wait_ge(n_oe.h, bOE + idx - 1)
                                        for hc in range(NHC):
                                            mmi = t.matmul(PA[idx % 2][:],
                                                           ar(O_HH + hc * 1024 + stl * 128, 128),
                                                           ar(O_W2 + hc * 1024 + otl * 512, 512),
                                                           start=(hc == 0),
                                                           stop=(hc == NHC - 1))
                                        n_m2.inc(mmi)

                            @blk.vector
                            def _(v, bM2=bM2, bOS=bOS):
                                for stl in range(8):
                                    for otl in range(2):
                                        idx = stl * 2 + otl
                                        v.wait_ge(n_m2.h, bM2 + idx + 1)
                                        if idx >= 2:
                                            v.wait_ge(n_os[idx % 2].h,
                                                      bOS[idx % 2] + 16 * (idx // 2))
                                        n_oe.inc(v.tensor_copy(oe_ev[idx % 2][:],
                                                               PA[idx % 2][:]))

                            @blk.sync
                            def _(s, out_dram=out_dram, half=half, oh=oh, bOE=bOE,
                                  w2src=w2src, tag=tag, bM2=bM2, wthr=wthr):
                                s.wait_ge(n_ht.h, M[f"{tag}_h_{half}"])
                                if DP == 5 and moe and half == 0:
                                    s.wait_ge(dx_s.h, 16 * (NDC + NHC))
                                if half == 0 and oh == 0:
                                    s.wait_ge(wc_s.h, wthr[1])
                                if oh == 1:
                                    s.wait_ge(n_m2.h, bM2)
                                for hc in range(NHC):
                                    n_w2.inc(s.dma_start(
                                        out=ar(O_W2 + hc * 1024, 1024),
                                        in_=w2src[hc * 128:(hc + 1) * 128,
                                                  oh * 1024:(oh + 1) * 1024]), 16)
                                for stl in range(8):
                                    for otl in range(2):
                                        idx = stl * 2 + otl
                                        s.wait_ge(n_oe.h, bOE + idx + 1)
                                        n_os[idx % 2].inc(s.dma_start(
                                            out=out_dram[half * 1024 + stl * 128:
                                                         half * 1024 + (stl + 1) * 128,
                                                         oh * 1024 + otl * 512:
                                                         oh * 1024 + (otl + 1) * 512],
                                            in_=oe_ev[idx % 2][:]), 16)
                    M[f"{tag}_w2g_{half}"] = (n_m2, n_m2.n)
                M[f"{tag}_st"] = (n_os[0].n, n_os[1].n)

            if DP >= 4:
                first_gate = (q_mmA, M["p4_av"])
            else:
                first_gate = (p3_mmA, M.get("p3_mmA", 0)) if DP >= 3 else (cc, 0)
            emit_ffn("moe", W1b, W3b, W2b, oe_c, True, first_gate, (32, 48))

            with nc.Block() as blk:
                @blk.gpsimd
                def _(g):
                    g.wait_ge(n_os[0].h, M["moe_st"][0])
                    g.wait_ge(n_os[1].h, M["moe_st"][1])
                    cc.inc(g.collective_compute(
                        "AllGather", ALU.bypass, replica_groups=[list(range(8))],
                        ins=[oe_c], outs=[oe_full]))
            M["cc_ag5"] = cc.n

        if DP >= 6:
            emit_ffn("sh", sw1b, sw3b, sw2b, shared_my, False, M["moe_w2g_1"], (80, 96))

        if DP >= 7:
            with nc.Block() as blk:
                @blk.sync
                def _(s):
                    s.wait_ge(cc.h, M["cc_a2a4"])
                    g = M["sh_w2g_1"]
                    s.wait_ge(g[0].h, g[1])
                    for fc in range(16):
                        j, hl = fc // 2, fc % 2
                        o_af.inc(s.dma_start(
                            out=ar(O_AFM + fc * 2048, 2048),
                            in_=att_all[j * 2048:(j + 1) * 2048,
                                        hl * 128:(hl + 1) * 128],
                            transpose=True), 16)
                    s.wait_ge(f_wst[0].h, M["woTe_w0"])
                    s.wait_ge(f_wst[1].h, M["woTe_w1"])
                    for fc in range(16):
                        o_wo.inc(s.dma_start(
                            out=ar(O_WOE + fc * 2048, 2048),
                            in_=WoTe[fc * 128:(fc + 1) * 128, :]), 16)
                    for tc in range(16):
                        if tc >= 2:
                            s.wait_ge(o_fr6.h, tc - 1)
                        o_x[tc % 2].inc(s.dma_start(
                            out=xt_f32[tc % 2], in_=x_my[tc * 128:(tc + 1) * 128, :]), 16)
                        o_sh[tc % 2].inc(s.dma_start(
                            out=shbuf[tc % 2][:],
                            in_=shared_my[tc * 128:(tc + 1) * 128, :]), 16)
                        for ot in range(4):
                            idx = tc * 4 + ot
                            s.wait_ge(o_ev.h, idx + 1)
                            o_st[idx % 2].inc(s.dma_start(
                                out=out_my[tc * 128:(tc + 1) * 128,
                                           ot * 512:(ot + 1) * 512],
                                in_=out_ev[idx % 2]), 16)

                @blk.gpsimd
                def _(g):
                    g.wait_ge(cc.h, M["cc_ag5"])
                    for tc in range(16):
                        if tc >= 2:
                            g.wait_ge(o_fr5.h, tc - 1)
                        g.memset(moebuf[tc % 2][:], 0.0)
                        o_mg[tc % 2].inc(g.indirect_dma_start(
                            out=moebuf[tc % 2][:], out_offset=None,
                            in_=oe_full,
                            in_offset=bass.IndirectOffsetOnAxis(slotu_sb[:, tc:tc + 1], 0),
                            bounds_check=NCORE * CAP - 1, oob_is_err=False), 16)

                @blk.tensor
                def _(t):
                    t.wait_ge(o_af.h, 16 * 16)
                    t.wait_ge(o_wo.h, 16 * 16)
                    for tc in range(16):
                        for ot in range(4):
                            idx = tc * 4 + ot
                            if idx >= 2:
                                t.wait_ge(o_ev.h, idx - 1)
                            for fc in range(16):
                                t.matmul(PA[idx % 2][:],
                                         ar(O_AFM + fc * 2048 + tc * 128, 128),
                                         ar(O_WOE + fc * 2048 + ot * 512, 512),
                                         start=(fc == 0), stop=False)
                            o_mm.inc(t.matmul(PA[idx % 2][:], ones_sb[:],
                                              bor_sb[:, ot * 512:(ot + 1) * 512],
                                              start=False, stop=True))

                @blk.vector
                def _(v):
                    for tc in range(16):
                        v.wait_ge(o_mg[tc % 2].h, 16 * (tc // 2 + 1))
                        v.wait_ge(o_x[tc % 2].h, 16 * (tc // 2 + 1))
                        v.wait_ge(o_sh[tc % 2].h, 16 * (tc // 2 + 1))
                        for ot in range(4):
                            idx = tc * 4 + ot
                            v.wait_ge(o_mm.h, idx + 1)
                            if idx >= 2:
                                v.wait_ge(o_st[idx % 2].h, 16 * (idx // 2))
                            VE(v, v.scalar_tensor_tensor(
                                tmp1_f, moebuf[tc % 2][:, ot * 512:(ot + 1) * 512],
                                gs_sb[:, tc:tc + 1], PA[idx % 2][:],
                                op0=ALU.mult, op1=ALU.add))
                            VE(v, v.tensor_tensor(
                                tmp2_f, tmp1_f,
                                xt_f32[tc % 2][:, ot * 512:(ot + 1) * 512], op=ALU.add))
                            o_ev.inc(v.tensor_tensor(
                                out_ev[idx % 2], tmp2_f,
                                shbuf[tc % 2][:, ot * 512:(ot + 1) * 512], op=ALU.add))
                        o_fr5.inc(v.nop())
                        o_fr6.inc(v.nop())

        with nc.Block() as blk:
            @blk.sync
            def _(s):
                if DP < 99:
                    if DP >= 1:
                        s.wait_ge(cc.h, M["cc_ag1"])
                        s.wait_ge(b_st[0].h, b_st[0].n)
                        s.wait_ge(b_st[1].h, b_st[1].n)
                        d_st.inc(s.dma_start(out=dbg["d_xn_tm"], in_=xn_tm_c), 16)
                        d_st.inc(s.dma_start(out=dbg["d_xn_fm"], in_=xn_fm_full), 16)
                    if DP >= 2:
                        s.wait_ge(cc.h, M["cc_a2a3"])
                        d_st.inc(s.dma_start(out=dbg["d_xs"], in_=xs_all), 16)
                        if DP == 2:
                            d_st.inc(s.dma_start(out=dbg["d_xsc"], in_=xs_c), 16)
                            d_st.inc(s.dma_start(out=dbg["d_slotu"], in_=slotu_sb[:]), 16)
                            d_st.inc(s.dma_start(out=dbg["d_eq"], in_=logits_all2[:]), 16)
                            d_st.inc(s.dma_start(out=dbg["d_vl"], in_=logits_all3[:]), 16)
                            d_st.inc(s.dma_start(out=dbg["d_nm"][:, 0:16], in_=nm_all[:]), 16)
                            d_st.inc(s.dma_start(out=dbg["d_nm"][:, 16:32], in_=pmax_all[:]), 16)
                        for k, srcb in enumerate([eidx_sb, gate_sb, posd_sb, keep_sb,
                                                  slotf_sb, gs_sb]):
                            d_st.inc(s.dma_start(out=dbg["d_route"][:, k * 16:(k + 1) * 16],
                                                 in_=srcb[:]), 16)
                    if DP >= 3:
                        for i4 in range(4):
                            s.wait_ge(p3_ks[i4].h, M["p3_ks"][i4])
                        d_st.inc(s.dma_start(out=dbg["d_Kfm"], in_=Kfm), 16)
                        d_st.inc(s.dma_start(out=dbg["d_Qfm"], in_=Qfm), 16)
                    if DP >= 4:
                        s.wait_ge(cc.h, M["cc_a2a4"])
                        d_st.inc(s.dma_start(out=dbg["d_att"], in_=att_all), 16)
                    if DP >= 5:
                        s.wait_ge(cc.h, M["cc_ag5"])
                        d_st.inc(s.dma_start(out=dbg["d_oe"], in_=oe_full), 16)
                    if DP >= 6:
                        s.wait_ge(n_os[0].h, M["sh_st"][0])
                        s.wait_ge(n_os[1].h, M["sh_st"][1])
                        if "shdump" not in KCUT:
                            d_st.inc(s.dma_start(out=dbg["d_shared"], in_=shared_my), 16)
                    s.wait_ge(d_st.h, d_st.n)
                if DP >= 7:
                    s.wait_ge(o_st[0].h, o_st[0].n)
                    s.wait_ge(o_st[1].h, o_st[1].n)

            @blk.gpsimd
            def _(g):
                g.wait_ge(g_gs.h, g_gs.n)

    return nc, dbg


def _tile_w(W):
    return np.ascontiguousarray(
        np.asarray(W).reshape(NDC, 128, NHC, 128).transpose(2, 0, 1, 3))


def _prep_inputs(inputs):
    f32 = np.float32
    x = np.ascontiguousarray(np.asarray(inputs["L0_attn_x"], f32).reshape(TTOK, D))
    WT = {w: np.ascontiguousarray(np.asarray(inputs[f"L0_attn_W{w}"], f32).T)
          for w in "qkvo"}
    A_T = {w: np.ascontiguousarray(np.asarray(inputs[f"L0_attn_A{w}"], f32).T)
           for w in "qkvo"}
    Bm = {w: np.ascontiguousarray(np.asarray(inputs[f"L0_attn_B{w}"], f32))
          for w in "qkvo"}
    bias = {w: np.asarray(inputs[f"L0_attn_b{w}"], f32) for w in "qkvo"}
    Wg = np.ascontiguousarray(np.asarray(inputs["L0_ffn_Wg"], f32))
    W1 = np.asarray(inputs["L0_ffn_W1"], f32)
    W3 = np.asarray(inputs["L0_ffn_W3"], f32)
    W2 = np.asarray(inputs["L0_ffn_W2"], f32)
    sw1t = _tile_w(inputs["L0_ffn_sw1"])
    sw3t = _tile_w(inputs["L0_ffn_sw3"])
    sw2 = np.ascontiguousarray(np.asarray(inputs["L0_ffn_sw2"], f32))

    bf = ml_dtypes.bfloat16
    consts = {
        "ident": np.eye(128, dtype=f32).astype(bf),
        "ident32": np.eye(128, dtype=f32),
        "TRI": np.triu(np.ones((128, 128), f32)).astype(bf),
        "iota8": np.tile(np.arange(8, dtype=f32), (128, 1)),
        "iota1k": np.tile(np.arange(8, dtype=f32) + 1000.0, (128, 1)),
        "onesrow": np.ones((1, 128), f32).astype(bf),
        "ones816": np.ones((8, 16), f32),
    }

    in_maps = []
    for c in range(NCORE):
        fsl = slice(256 * c, 256 * (c + 1))
        m = {
            "x_my": np.ascontiguousarray(x[c * TOWN:(c + 1) * TOWN]),
            "WqT": np.ascontiguousarray(WT["q"][:, fsl]),
            "WkT": np.ascontiguousarray(WT["k"][:, fsl]),
            "WvT": np.ascontiguousarray(WT["v"][:, fsl]),
            "WoT": WT["o"],
            "Bq": Bm["q"], "Bk": Bm["k"], "Bv": Bm["v"], "Bo": Bm["o"],
            "AqT": np.ascontiguousarray(A_T["q"][:, fsl]),
            "AkT": np.ascontiguousarray(A_T["k"][:, fsl]),
            "AvT": np.ascontiguousarray(A_T["v"][:, fsl]),
            "AoT": A_T["o"],
            "bq_c": np.ascontiguousarray(bias["q"][fsl].reshape(2, 128).T),
            "bk_c": np.ascontiguousarray(bias["k"][fsl].reshape(2, 128).T),
            "bv_r": np.ascontiguousarray(bias["v"][fsl].reshape(1, 256)),
            "bo_r": np.ascontiguousarray(bias["o"].reshape(1, D)),
            "Wg": Wg,
            "W1t": _tile_w(W1[c]),
            "W3t": _tile_w(W3[c]),
            "W2d": np.ascontiguousarray(W2[c]),
            "sw1t": sw1t, "sw3t": sw3t, "sw2d": sw2,
            "maskp": np.tile((np.arange(8, dtype=f32) < c).astype(f32), (8, 1)),
            "tok1": (c * TOWN + 1 + np.arange(128, dtype=f32)[:, None]
                     + 128.0 * np.arange(16, dtype=f32)[None, :]),
        }
        m.update(consts)
        in_maps.append(m)
    return in_maps


def kernel(**inputs):
    in_maps = _prep_inputs(inputs)
    nc, _ = build_nc(DP=99)
    res = run_bass_kernel_spmd(nc, in_maps, core_ids=list(range(NCORE)))
    out = np.concatenate([np.asarray(r["out_my"], np.float32) for r in res.results],
                         axis=0)
    return out.reshape(4, 4096, 2048)


# version 04
# speedup vs baseline: 1.0918x; 1.0918x over previous
import os
import numpy as np
import ml_dtypes
import concourse.bass as bass
import concourse.mybir as mybir
from concourse.bass_utils import run_bass_kernel_spmd
from concourse.aot_env import allow_ici_all_to_all
from contextlib import ExitStack

F32 = mybir.dt.float32
BF16 = mybir.dt.bfloat16
U32 = mybir.dt.uint32
AF = mybir.ActivationFunctionType
ALU = mybir.AluOpType
AX = mybir.AxisListType

D = 2048
TTOK = 16384
TOWN = 2048
HFF = 2816
NCORE = 8
CAP = 2048
SCALE = 128.0 ** -0.5
LORA_S = 0.5
EPS = 1e-5
CSHIFT = 4.0
SENT = 1.0e9
NDC = 16
NHC = 22


class Sem:
    def __init__(self, h):
        self.h = h
        self.n = 0

    def inc(self, instr, k=1):
        instr.then_inc(self.h, k)
        self.n += k
        return self.n


def build_nc(DP=99):
    SUB = int(os.environ.get("KSUB", "9"))
    KCUT = set(x for x in os.environ.get("KCUT", "").split(",") if x)
    nc = bass.Bass("TRN2", target_bir_lowering=False, debug=False,
                   dynamic_dma_scratch_size=24576,
                   detect_race_conditions=False)

    def din(name, shape, dt=F32):
        return nc.dram_tensor(name, shape, dt, kind="ExternalInput").ap()

    def dout(name, shape, dt=F32):
        return nc.dram_tensor(name, shape, dt, kind="ExternalOutput").ap()

    def dtmp(name, shape, dt):
        return nc.dram_tensor(name, shape, dt).ap()

    x_my = din("x_my", [TOWN, D])
    WqT = din("WqT", [D, 256]); WkT = din("WkT", [D, 256]); WvT = din("WvT", [D, 256])
    WoT = din("WoT", [D, D])
    Bq = din("Bq", [16, D]); Bk = din("Bk", [16, D]); Bv = din("Bv", [16, D]); Bo = din("Bo", [16, D])
    AqT = din("AqT", [16, 256]); AkT = din("AkT", [16, 256]); AvT = din("AvT", [16, 256])
    AoT = din("AoT", [16, D])
    bq_c = din("bq_c", [128, 2]); bk_c = din("bk_c", [128, 2])
    bv_r = din("bv_r", [1, 256]); bo_r = din("bo_r", [1, D])
    Wg = din("Wg", [D, 8])
    W1t = din("W1t", [NHC, NDC, 128, 128])
    W3t = din("W3t", [NHC, NDC, 128, 128])
    W2d = din("W2d", [HFF, D])
    sw1t = din("sw1t", [NHC, NDC, 128, 128])
    sw3t = din("sw3t", [NHC, NDC, 128, 128])
    sw2d = din("sw2d", [HFF, D])
    ident = din("ident", [128, 128], BF16)
    ident32 = din("ident32", [128, 128], F32)
    TRI = din("TRI", [128, 128], BF16)
    iota8 = din("iota8", [128, 8])
    iota1k = din("iota1k", [128, 8])
    onesrow = din("onesrow", [1, 128], BF16)
    ones816 = din("ones816", [8, 16])
    maskp = din("maskp", [8, 8])
    tok1 = din("tok1", [128, 16])

    out_my = dout("out_my", [TOWN, D])

    xn_tm_c = dtmp("xn_tm_c", [TOWN, D], BF16)
    xn_fm_c = dtmp("xn_fm_c", [D, TOWN], BF16)
    xn_fm_full = dtmp("xn_fm_full", [NCORE * D, TOWN], BF16)
    hist_c = dtmp("hist_c", [8, 1], F32)
    hist_full = dtmp("hist_full", [64, 1], F32)
    xs_c = dtmp("xs_c", [TTOK, D], BF16)
    xs_all = dtmp("xs_all", [TTOK, D], BF16)
    Kfm = dtmp("Kfm", [256, TTOK], BF16)
    Qfm = dtmp("Qfm", [256, TTOK], BF16)
    att_c = dtmp("att_c", [TTOK, 256], BF16)
    att_all = dtmp("att_all", [TTOK, 256], BF16)
    WoTe = dtmp("WoTe", [D, D], BF16)
    oe_c = dtmp("oe_c", [CAP, D], BF16)
    oe_full = dtmp("oe_full", [NCORE * CAP, D], BF16)
    shared_my = dtmp("shared_my", [TOWN, D], BF16)

    dbg = {}
    if DP < 99:
        if DP >= 1:
            dbg["d_xn_tm"] = dout("d_xn_tm", [TOWN, D], BF16)
            dbg["d_xn_fm"] = dout("d_xn_fm", [NCORE * D, TOWN], BF16)
        if DP >= 2:
            dbg["d_route"] = dout("d_route", [128, 16 * 6])
            dbg["d_xs"] = dout("d_xs", [TTOK, D], BF16)
            if DP == 2:
                dbg["d_xsc"] = dout("d_xsc", [TTOK, D], BF16)
                dbg["d_slotu"] = dout("d_slotu", [128, 16], U32)
                dbg["d_eq"] = dout("d_eq", [128, 128])
                dbg["d_vl"] = dout("d_vl", [128, 128])
                dbg["d_nm"] = dout("d_nm", [128, 32])
        if DP >= 3:
            dbg["d_Kfm"] = dout("d_Kfm", [256, TTOK], BF16)
            dbg["d_Qfm"] = dout("d_Qfm", [256, TTOK], BF16)
        if DP >= 4:
            dbg["d_att"] = dout("d_att", [TTOK, 256], BF16)
        if DP >= 5:
            dbg["d_oe"] = dout("d_oe", [NCORE * CAP, D], BF16)
        if DP == 5:
            dbg["d_xd"] = dout("d_xd", [2048, 1024], BF16)
            dbg["d_hh"] = dout("d_hh", [NHC * 128, 1024], BF16)
        if DP >= 6:
            dbg["d_shared"] = dout("d_shared", [TOWN, D], BF16)

    with ExitStack() as stk:
        def sb(name, shape, dt):
            return stk.enter_context(nc.sbuf_tensor(name, shape, dt))

        def psumt(name, shape, dt):
            return stk.enter_context(nc.psum_tensor(name, shape, dt))

        def S(name):
            return Sem(stk.enter_context(nc.semaphore(name)))

        AR = 73728
        arena = sb("arena", [128, AR], BF16)
        f32ar = sb("f32ar", [128, 6144], F32)
        ident_sb = sb("ident_sb", [128, 128], BF16)
        id32_sb = sb("id32_sb", [128, 128], F32)
        tri_sb = sb("tri_sb", [128, 128], BF16)
        iota8_sb = sb("iota8_sb", [128, 8], F32)
        iota1k_sb = sb("iota1k_sb", [128, 8], F32)
        ones_sb = sb("ones_sb", [1, 128], BF16)
        ones816_sb = sb("ones816_sb", [8, 16], F32)
        wg_sb = sb("wg_sb", [128, NDC * 8], BF16)
        bqc_sb = sb("bqc_sb", [128, 2], F32)
        bkc_sb = sb("bkc_sb", [128, 2], F32)
        bvr_sb = sb("bvr_sb", [1, 256], BF16)
        bor_sb = sb("bor_sb", [1, D], BF16)
        maskp_sb = sb("maskp_sb", [8, 8], F32)
        tok1_sb = sb("tok1_sb", [128, 16], F32)
        mean_all = sb("mean_all", [128, 16], F32)
        var_all = sb("var_all", [128, 16], F32)
        sd_all = sb("sd_all", [128, 16], F32)
        rstd_all = sb("rstd_all", [128, 16], F32)
        logits_all = sb("logits_all", [128, 128], F32)
        nm_all = sb("nm_all", [128, 16], F32)
        pmax_all = sb("pmax_all", [128, 16], F32)
        se_all = sb("se_all", [128, 16], F32)
        gate_sb = sb("gate_sb", [128, 16], F32)
        gs_sb = sb("gs_sb", [128, 16], F32)
        slotu_sb = sb("slotu_sb", [128, 16], U32)
        slotf_sb = sb("slotf_sb", [128, 16], F32)
        eidx_sb = sb("eidx_sb", [128, 16], F32)
        keep_sb = sb("keep_sb", [128, 16], F32)
        posd_sb = sb("posd_sb", [128, 16], F32)
        oh_all = sb("oh_all", [128, 128], BF16)
        ohf_sb = sb("ohf_sb", [128, 8], F32)
        pos_all = sb("pos_all", [128, 128], F32)
        ohT_all = sb("ohT_all", [8, 2048], F32)
        histco = sb("histco", [8, 16], F32)
        pref = sb("pref", [8, 16], F32)
        offm1 = sb("offm1", [8, 16], F32)
        offmF = sb("offmF", [8, 16], F32)
        offc = sb("offc", [8, 1], F32)
        hf_sb = sb("hf_sb", [8, 8], F32)
        hm_sb = sb("hm_sb", [8, 8], F32)
        junk8 = sb("junk8", [128, 8], F32)
        junk8b = sb("junk8b", [128, 8], F32)
        logits_all2 = sb("logits_all2", [128, 128], F32)
        logits_all3 = sb("logits_all3", [128, 128], F32)
        r_c1 = sb("r_c1", [128, 1], F32)
        r_c2 = sb("r_c2", [128, 1], F32)
        r_c3 = sb("r_c3", [128, 1], F32)
        zero_sb = sb("zero_sb", [128, 128], F32)
        rec_sb = sb("rec_sb", [128, 1], F32)
        csh_sb = sb("csh_sb", [128, 1], F32)
        stat6 = sb("stat6", [128, 24], F32)
        stat2 = sb("stat2", [128, 2], F32)

        xt_f32 = [f32ar[:, 0:2048], f32ar[:, 2048:4096]]
        out_ev = [f32ar[:, 4096 + i * 512: 4096 + (i + 1) * 512] for i in range(2)]
        tmp1_f = f32ar[:, 5120:5632]
        tmp2_f = f32ar[:, 5632:6144]

        O_VAUG = 0
        VAUG_SZ = 128 * 258
        O_FM = VAUG_SZ
        O_ET = VAUG_SZ
        O_KH = O_ET + 16384
        O_QP = O_KH + 8192
        O_STG = O_QP + 1024
        O_HH = 0
        O_XD = 22528
        O_WB = O_XD + 16384
        O_W2 = 22528
        O_AFM = 0
        O_WOE = 32768

        def ar(o, n):
            return arena[:, o:o + n]

        xnb_ev = [ar(33024, 2048), ar(35072, 2048)]
        fmstage = ar(37120, 4096)
        lor_b = arena[0:16, 41216:43264]
        aot_b = arena[0:16, 43264:45312]
        at_b = arena[0:16, 45312:45568]
        raw_b = [ar(45568, 256), ar(45824, 256)]
        rawo_b = [ar(61696, 2048), ar(63744, 2048)]
        woe_b = [ar(65792, 2048), ar(67840, 2048)]
        O_WEFF = 49408

        def weff(w, dc):
            o = O_WEFF + ({"q": 0, "k": 1, "v": 2}[w] * NDC + dc) * 256
            return arena[:, o:o + 256]

        kq_ev = [ar(69888 + i * 512, 512) for i in range(4)]
        xdtm = [ar(47104, 2048), ar(49152, 2048)]
        xsh = [ar(51200, 2048), ar(53248, 2048)]
        accS = ar(57344, 2048)
        s1_ev = [ar(55296, 512), ar(55808, 512)]
        oe_ev = [ar(56320, 512), ar(56832, 512)]
        moebuf = [ar(65536, 2048), ar(67584, 2048)]
        shbuf = [ar(69632, 2048), ar(71680, 2048)]

        PA = [psumt(f"PA{i}", [128, 512], F32) for i in range(4)]
        PB = [psumt(f"PB{i}", [128, 512], F32) for i in range(2)]
        PT_full = [psumt(f"PT{i}", [128, 1024], BF16) for i in range(2)]
        PT = [t[:, 0:128] for t in PT_full]

        cc = S("cc")
        g0 = S("g0")
        f_ba = S("f_ba"); f_raw = [S("f_raw0"), S("f_raw1")]
        f_mm = S("f_mm"); f_cmb = S("f_cmb")
        f_wst = [S("f_wst0"), S("f_wst1")]
        a_x = [S("a_x0"), S("a_x1")]; a_fr = S("a_fr"); a_ms = S("a_ms")
        m_sq = S("m_sq"); m_rs = S("m_rs")
        b_x = [S("b_x0"), S("b_x1")]; b_fr = S("b_fr"); b_xn = S("b_xn")
        b_fmc = S("b_fmc"); b_tr = S("b_tr"); b_lg = S("b_lg"); b_le = S("b_le")
        b_st = [S("b_st0"), S("b_st1")]; b_sf = [S("b_sf0"), S("b_sf1")]
        r_nm = S("r_nm"); r_ex = S("r_ex"); r_vA = S("r_vA"); r_mmA = S("r_mmA")
        r_hs = S("r_hs"); r_hl = S("r_hl"); r_off = S("r_off")
        r_mmB = S("r_mmB"); r_vB = S("r_vB"); g_gs = S("g_gs")
        p3_fm = [S("p3_fm0"), S("p3_fm1")]
        p3_mmA = S("p3_mmA"); p3_mmV = S("p3_mmV")
        p3_ev = S("p3_ev"); p3_vv = S("p3_vv"); p3_vm = S("p3_vm")
        p3_ks = [S(f"p3_ks{i}") for i in range(4)]
        q_kh = [S("q_kh0"), S("q_kh1")]; q_qp = [S("q_qp0"), S("q_qp1")]
        q_mmS = S("q_mmS"); q_mmA = S("q_mmA"); q_ex = S("q_ex")
        q_rc = S("q_rc"); q_av = S("q_av"); q_st = S("q_st")
        z_gs = S("z_gs"); z_as = S("z_as"); z_ms = S("z_ms")
        x_sc = [S("x_sc0"), S("x_sc1")]
        m_sl = [S("m_sl0"), S("m_sl1")]; m_mg = S("m_mg")
        n_ga = [S("n_ga0"), S("n_ga1")]; n_gx = S("n_gx")
        n_tr = S("n_tr"); n_tc = S("n_tc")
        n_w1 = [S("n_w10"), S("n_w11")]; n_hm = S("n_hm"); n_si = S("n_si")
        n_ht = S("n_ht"); n_w2 = S("n_w2"); n_m2 = S("n_m2"); n_oe = S("n_oe")
        dx_s = S("dx_s")
        n_os = [S("n_os0"), S("n_os1")]
        o_af = S("o_af"); o_wo = S("o_wo")
        o_x = [S("o_x0"), S("o_x1")]; o_sh = [S("o_sh0"), S("o_sh1")]
        o_mg = [S("o_mg0"), S("o_mg1")]
        o_mm = S("o_mm"); o_ev = S("o_ev"); o_st = [S("o_st0"), S("o_st1")]
        o_fr5 = S("o_fr5"); o_fr6 = S("o_fr6"); d_st = S("d_st")
        vch = S("vch")

        def VE(v, instr):
            vch.inc(instr)
            v.wait_ge(vch.h, vch.n)

        M = {}

        with nc.Block() as blk:
            @blk.gpsimd
            def _(g):
                for dst, src in [(ident_sb[:], ident), (id32_sb[:], ident32),
                                 (tri_sb[:], TRI), (iota8_sb[:], iota8),
                                 (iota1k_sb[:], iota1k), (ones_sb[:], onesrow),
                                 (ones816_sb[:], ones816),
                                 (bqc_sb[:], bq_c), (bkc_sb[:], bk_c),
                                 (bvr_sb[:], bv_r), (bor_sb[:], bo_r),
                                 (maskp_sb[:], maskp), (tok1_sb[:], tok1)]:
                    g0.inc(g.dma_start(out=dst, in_=src), 16)
                g0.inc(g.dma_start(
                    out=wg_sb[:].rearrange("p (dc e) -> p dc e", dc=NDC),
                    in_=Wg.rearrange("(dc p) e -> p dc e", p=128)), 16)
                g.memset(zero_sb[:], 0.0)
                g.memset(csh_sb[:], -CSHIFT)
                z_ms.inc(g.memset(ar(0, 2048), 0.0))
                g.wait_ge(z_ms.h, 1)
                for r in range(64):
                    g_gs.inc(g.dma_start(out=xs_c[r * 128:(r + 1) * 128, :],
                                         in_=ar(0, 2048)), 16)
        M["consts"] = g0.n
        M["xs_zero_g"] = g_gs.n

        with nc.Block() as blk:
            @blk.scalar
            def _(sc):
                sc.wait_ge(z_ms.h, 1)
                for r in range(64, 128):
                    z_as.inc(sc.dma_start(out=xs_c[r * 128:(r + 1) * 128, :],
                                          in_=ar(0, 2048)), 16)
        M["xs_zero_a"] = z_as.n

        with nc.Block() as blk:
            @blk.sync
            def _(s):
                for i in range(16):
                    if i >= 2:
                        s.wait_ge(a_fr.h, i - 1)
                    a_x[i % 2].inc(s.dma_start(out=xt_f32[i % 2],
                                               in_=x_my[i * 128:(i + 1) * 128, :]), 16)

            @blk.vector
            def _(v):
                for i in range(16):
                    v.wait_ge(a_x[i % 2].h, 16 * (i // 2 + 1))
                    for q4 in range(4):
                        ii = v.bn_stats(stat6[:, 6 * q4:6 * q4 + 6],
                                        xt_f32[i % 2][:, 512 * q4:512 * (q4 + 1)])
                    a_ms.inc(ii)
                    v.wait_ge(a_ms.h, a_ms.n)
                    a_ms.inc(v.bn_aggr(stat2[:], stat6[:]))
                    v.wait_ge(a_ms.h, a_ms.n)
                    v.tensor_copy(mean_all[:, i:i + 1], stat2[:, 0:1])
                    ts = v.tensor_scalar(var_all[:, i:i + 1], stat2[:, 1:2], EPS, None,
                                         op0=ALU.add)
                    if i == 15:
                        a_ms.inc(ts)
                    a_fr.inc(v.nop())
        M["stats"] = a_ms.n

        with nc.Block() as blk:
            @blk.scalar
            def _(sc):
                sc.wait_ge(a_ms.h, M["stats"])
                m_sq.inc(sc.activation(sd_all[:], var_all[:], AF.Sqrt))

            @blk.vector
            def _(v):
                v.wait_ge(m_sq.h, 1)
                m_rs.inc(v.reciprocal(rstd_all[:], sd_all[:]))
                v.wait_ge(m_rs.h, 1)
        M["rstd"] = m_rs.n

        for wi, (wname, WT_d, Bm, Am) in enumerate(
                [("q", WqT, Bq, AqT), ("k", WkT, Bk, AkT), ("v", WvT, Bv, AvT)]):
            bR = [f_raw[0].n, f_raw[1].n]
            bC = f_cmb.n
            bM = f_mm.n
            with nc.Block() as blk:
                @blk.gpsimd
                def _(g, Bm=Bm, Am=Am, WT_d=WT_d, bC=bC):
                    g.wait_ge(f_cmb.h, bC)
                    f_ba.inc(g.dma_start(out=lor_b[:, 0:D], in_=Bm), 16)
                    f_ba.inc(g.dma_start(out=at_b[:], in_=Am), 16)
                    for dc in range(NDC):
                        if dc >= 2:
                            g.wait_ge(f_cmb.h, bC + dc - 1)
                        f_raw[dc % 2].inc(g.dma_start(
                            out=raw_b[dc % 2][:], in_=WT_d[dc * 128:(dc + 1) * 128, :]), 16)
                M[f"ba_{wname}"] = f_ba.n

                @blk.tensor
                def _(t, wname=wname, bC=bC):
                    t.wait_ge(f_ba.h, M[f"ba_{wname}"])
                    for dc in range(NDC):
                        if dc >= 2:
                            t.wait_ge(f_cmb.h, bC + dc - 1)
                        f_mm.inc(t.matmul(PB[dc % 2][:, 0:256],
                                          lor_b[:, dc * 128:(dc + 1) * 128],
                                          at_b[:], start=True, stop=True))

                @blk.vector
                def _(v, wname=wname, bM=bM, bR=bR):
                    for dc in range(NDC):
                        v.wait_ge(f_mm.h, bM + dc + 1)
                        v.wait_ge(f_raw[dc % 2].h, bR[dc % 2] + 16 * (dc // 2 + 1))
                        f_cmb.inc(v.scalar_tensor_tensor(
                            weff(wname, dc), PB[dc % 2][:, 0:256], LORA_S,
                            raw_b[dc % 2][:], op0=ALU.mult, op1=ALU.add))
            M[f"fold_{wname}"] = f_cmb.n

        bR = [f_raw[0].n, f_raw[1].n]
        bC = f_cmb.n
        bM = f_mm.n
        bW = [f_wst[0].n, f_wst[1].n]
        with nc.Block() as blk:
            @blk.sync
            def _(s):
                for dc in range(NDC):
                    s.wait_ge(f_cmb.h, bC + 4 * (dc + 1))
                    f_wst[dc % 2].inc(s.dma_start(
                        out=WoTe[dc * 128:(dc + 1) * 128, :], in_=woe_b[dc % 2][:]), 16)

            @blk.gpsimd
            def _(g):
                g.wait_ge(f_cmb.h, bC)
                f_ba.inc(g.dma_start(out=lor_b[:, 0:D], in_=Bo), 16)
                f_ba.inc(g.dma_start(out=aot_b[:], in_=AoT), 16)
                M["ba_o"] = f_ba.n
                for dc in range(NDC):
                    if dc >= 2:
                        g.wait_ge(f_cmb.h, bC + 4 * (dc - 1))
                    f_raw[dc % 2].inc(g.dma_start(
                        out=rawo_b[dc % 2][:], in_=WoT[dc * 128:(dc + 1) * 128, :]), 16)

            @blk.tensor
            def _(t):
                t.wait_ge(f_ba.h, M["ba_o"])
                for dc in range(NDC):
                    for ot in range(4):
                        idx = dc * 4 + ot
                        if idx >= 2:
                            t.wait_ge(f_cmb.h, bC + idx - 1)
                        f_mm.inc(t.matmul(PB[idx % 2][:, 0:512],
                                          lor_b[:, dc * 128:(dc + 1) * 128],
                                          aot_b[:, ot * 512:(ot + 1) * 512],
                                          start=True, stop=True))

            @blk.vector
            def _(v):
                for dc in range(NDC):
                    if dc >= 2:
                        v.wait_ge(f_wst[dc % 2].h, bW[dc % 2] + 16 * (dc // 2 - 0))
                    for ot in range(4):
                        idx = dc * 4 + ot
                        v.wait_ge(f_mm.h, bM + idx + 1)
                        v.wait_ge(f_raw[dc % 2].h, bR[dc % 2] + 16 * (dc // 2 + 1))
                        f_cmb.inc(v.scalar_tensor_tensor(
                            woe_b[dc % 2][:, ot * 512:(ot + 1) * 512],
                            PB[idx % 2][:, 0:512], LORA_S,
                            rawo_b[dc % 2][:, ot * 512:(ot + 1) * 512],
                            op0=ALU.mult, op1=ALU.add))
        M["fold_o"] = f_cmb.n
        M["woTe_w0"] = f_wst[0].n
        M["woTe_w1"] = f_wst[1].n

        with nc.Block() as blk:
            @blk.sync
            def _(s):
                for i in range(16):
                    if i >= 2:
                        s.wait_ge(b_fr.h, i - 1)
                    b_x[i % 2].inc(s.dma_start(out=xt_f32[i % 2],
                                               in_=x_my[i * 128:(i + 1) * 128, :]), 16)
                    if "tm" not in KCUT:
                        s.wait_ge(b_xn.h, i + 1)
                        b_st[i % 2].inc(s.dma_start(out=xn_tm_c[i * 128:(i + 1) * 128, :],
                                                    in_=xnb_ev[i % 2][:]), 16)
                    if i % 2 == 1 and "fm" not in KCUT:
                        s.wait_ge(b_fmc.h, 16 * (i + 1))
                        b_sf[(i // 2) % 2].inc(s.dma_start(
                            out=xn_fm_c.rearrange("(dc p) t -> p dc t", p=128)
                                [:, :, (i - 1) * 128:(i + 1) * 128],
                            in_=fmstage[:].rearrange("p (dc t) -> p dc t", dc=NDC)), 16)

            @blk.vector
            def _(v):
                v.wait_ge(m_rs.h, M["rstd"])
                for i in range(16):
                    v.wait_ge(b_x[i % 2].h, 16 * (i // 2 + 1))
                    if i >= 2:
                        if "tm" not in KCUT:
                            v.wait_ge(b_st[i % 2].h, 16 * ((i - 2) // 2 + 1))
                        if "tr" not in KCUT:
                            v.wait_ge(b_tr.h, 16 * (i - 1))
                        P = (i - 2) // 2
                        if "fm" not in KCUT:
                            v.wait_ge(b_sf[0].h, 16 * (P // 2 + 1))
                            if P >= 1:
                                v.wait_ge(b_sf[1].h, 16 * ((P - 1) // 2 + 1))
                        if "lg" not in KCUT:
                            v.wait_ge(b_lg.h, i - 1)
                    b_xn.inc(v.tensor_scalar(xnb_ev[i % 2][:], xt_f32[i % 2],
                                             mean_all[:, i:i + 1], rstd_all[:, i:i + 1],
                                             op0=ALU.subtract, op1=ALU.mult))
                    b_fr.inc(v.nop())
                    if "tr" not in KCUT:
                        for dc in range(NDC):
                            g2 = i * 16 + dc
                            v.wait_ge(b_tr.h, g2 + 1)
                            b_fmc.inc(v.tensor_copy(
                                fmstage[:, dc * 256 + (i % 2) * 128:
                                        dc * 256 + (i % 2) * 128 + 128],
                                PT[dc % 2][:]))
                    if "lg" not in KCUT and "le" not in KCUT and "lev" not in KCUT:
                        v.wait_ge(b_lg.h, i + 1)
                        b_le.inc(v.tensor_copy(logits_all[:, 8 * i:8 * i + 8],
                                               PA[2 + i % 2][:, 0:8]))

            @blk.tensor
            def _(t):
                t.wait_ge(g0.h, M["consts"])
                for i in range(16):
                    t.wait_ge(b_xn.h, i + 1)
                    if "tr" not in KCUT:
                        for dc in range(NDC):
                            g2 = i * 16 + dc
                            if g2 >= 2:
                                t.wait_ge(b_fmc.h, g2 - 1)
                            b_tr.inc(t.transpose(PT[dc % 2][:],
                                                 xnb_ev[i % 2][:, dc * 128:(dc + 1) * 128],
                                                 ident_sb[:]))
                    if "lg" not in KCUT:
                        t.wait_ge(b_fmc.h, 16 * (i + 1))
                        if i >= 2 and "le" not in KCUT:
                            t.wait_ge(b_le.h, i - 1)
                        for dc in range(NDC):
                            mmi = t.matmul(PA[2 + i % 2][:, 0:8],
                                           fmstage[:, dc * 256 + (i % 2) * 128:
                                                   dc * 256 + (i % 2) * 128 + 128],
                                           wg_sb[:, dc * 8:(dc + 1) * 8],
                                           start=(dc == 0), stop=(dc == NDC - 1))
                        b_lg.inc(mmi)

            @blk.scalar
            def _(sc):
                if "lg" not in KCUT and "le" not in KCUT and "lev" in KCUT:
                    for i in range(16):
                        sc.wait_ge(b_lg.h, i + 1)
                        b_le.inc(sc.activation(logits_all[:, 8 * i:8 * i + 8],
                                               PA[2 + i % 2][:, 0:8], AF.Copy))

        with nc.Block() as blk:
            @blk.gpsimd
            def _(g):
                g.wait_ge(b_st[0].h, b_st[0].n)
                g.wait_ge(b_st[1].h, b_st[1].n)
                g.wait_ge(b_sf[0].h, b_sf[0].n)
                g.wait_ge(b_sf[1].h, b_sf[1].n)
                cc.inc(g.collective_compute(
                    "AllGather", ALU.bypass, replica_groups=[list(range(8))],
                    ins=[xn_fm_c], outs=[xn_fm_full]))
        M["cc_ag1"] = cc.n

        if DP >= 2:
            with nc.Block() as blk:
                @blk.vector
                def _(v):
                    v.wait_ge(b_le.h, 16)
                    for i in range(16):
                        last = v.reduce_max(nm_all[:, i:i + 1],
                                            logits_all[:, 8 * i:8 * i + 8],
                                            axis=AX.X, negate=True)
                    r_nm.inc(last)
                    v.wait_ge(r_nm.h, 1)
                    r_nm.inc(v.tensor_scalar(pmax_all[:], nm_all[:], -1.0, None,
                                             op0=ALU.mult))

                @blk.scalar
                def _(sc):
                    sc.wait_ge(r_nm.h, 2)
                    for i in range(16):
                        r_ex.inc(sc.activation(junk8[:], logits_all[:, 8 * i:8 * i + 8],
                                               AF.Exp, bias=nm_all[:, i:i + 1],
                                               accum_out=se_all[:, i:i + 1]))
            M["exp"] = r_ex.n

            with nc.Block() as blk:
                @blk.vector
                def _(v):
                    v.wait_ge(r_ex.h, M["exp"])
                    v.reciprocal(gate_sb[:], se_all[:])
                    for i in range(16):
                        if i >= 1:
                            v.wait_ge(r_mmA.h, 3 * (i - 1) + 3)
                        VE(v, v.tensor_scalar(junk8[:], logits_all[:, 8 * i:8 * i + 8],
                                              pmax_all[:, i:i + 1], None, op0=ALU.is_equal))
                        r_vA.inc(v.scalar_tensor_tensor(junk8b[:], junk8[:], -1000.0,
                                                        iota1k_sb[:],
                                                        op0=ALU.mult, op1=ALU.add))
                        v.wait_ge(r_vA.h, r_vA.n)
                        r_vA.inc(v.tensor_reduce(eidx_sb[:, i:i + 1], junk8b[:],
                                                 op=ALU.min, axis=AX.X))
                        v.wait_ge(r_vA.h, 5 * i + 2)
                        if DP == 2:
                            v.tensor_copy(logits_all2[:, 8 * i:8 * i + 8], junk8[:])
                            v.tensor_copy(logits_all3[:, 8 * i:8 * i + 8], junk8b[:])
                        VE(v, v.tensor_scalar(oh_all[:, 8 * i:8 * i + 8], iota8_sb[:],
                                              eidx_sb[:, i:i + 1], None, op0=ALU.is_equal))
                        r_vA.inc(v.tensor_copy(ohf_sb[:], oh_all[:, 8 * i:8 * i + 8]))
                        v.wait_ge(r_mmA.h, 3 * i + 1)
                        r_vA.inc(v.tensor_copy(pos_all[:, 8 * i:8 * i + 8],
                                               PA[i % 2][:, 0:8]))
                        v.wait_ge(r_mmA.h, 3 * i + 3)
                        v.tensor_copy(histco[:, i:i + 1], PA[2 + i % 2][0:8, 127:128])
                        r_vA.inc(v.tensor_copy(ohT_all[:, 128 * i:128 * (i + 1)],
                                               PB[i % 2][0:8, 0:128]))
                    r_vA.inc(v.nop())
                    v.wait_ge(r_vA.h, r_vA.n)
                    r_vA.inc(v.tensor_tensor_scan(pref[:], ones816_sb[:], histco[:],
                                                  0.0, op0=ALU.mult, op1=ALU.add))
                    v.wait_ge(r_vA.h, r_vA.n)

                @blk.tensor
                def _(t):
                    for i in range(16):
                        t.wait_ge(r_vA.h, 5 * i + 3)
                        if i >= 2:
                            t.wait_ge(r_vA.h, 5 * (i - 2) + 5)
                        r_mmA.inc(t.matmul(PA[i % 2][:, 0:8], tri_sb[:],
                                           oh_all[:, 8 * i:8 * i + 8],
                                           start=True, stop=True))
                        t.wait_ge(r_vA.h, 5 * i + 4)
                        r_mmA.inc(t.transpose(PA[2 + i % 2][0:8, 0:128],
                                              pos_all[:, 8 * i:8 * i + 8], id32_sb[:]))
                        r_mmA.inc(t.transpose(PB[i % 2][0:8, 0:128], ohf_sb[:],
                                              id32_sb[:]))
            M["p2a"] = r_vA.n

            with nc.Block() as blk:
                @blk.sync
                def _(s):
                    s.wait_ge(r_vA.h, M["p2a"])
                    r_hs.inc(s.dma_start(out=hist_c, in_=pref[:, 15:16]), 16)

                @blk.gpsimd
                def _(g):
                    g.wait_ge(r_hs.h, 16)
                    cc.inc(g.collective_compute(
                        "AllGather", ALU.bypass, replica_groups=[list(range(8))],
                        ins=[hist_c], outs=[hist_full]))
            M["cc_ag2"] = cc.n

            with nc.Block() as blk:
                @blk.sync
                def _(s):
                    s.wait_ge(cc.h, M["cc_ag2"])
                    with nc.allow_non_contiguous_dma("tiny 64-elem strided load"):
                        r_hl.inc(s.dma_start(
                            out=hf_sb[:].rearrange("e (j one) -> e j one", one=1),
                            in_=hist_full.rearrange("(j e) one -> e j one", e=8)), 16)

                @blk.vector
                def _(v):
                    v.wait_ge(r_hl.h, 16)
                    r_off.inc(v.tensor_tensor(hm_sb[:], hf_sb[:], maskp_sb[:], op=ALU.mult))
                    v.wait_ge(r_off.h, 1)
                    r_off.inc(v.tensor_reduce(offc[:], hm_sb[:], op=ALU.add, axis=AX.X))
                    v.wait_ge(r_off.h, 2)
                    VE(v, v.tensor_scalar(offm1[:], pref[:], offc[:], None, op0=ALU.add))
                    r_off.inc(v.tensor_tensor(offmF[:], offm1[:], histco[:],
                                              op=ALU.subtract))
            M["offm"] = r_off.n

            with nc.Block() as blk:
                @blk.tensor
                def _(t):
                    t.wait_ge(r_off.h, M["offm"])
                    for i in range(16):
                        if i >= 2:
                            t.wait_ge(r_vB.h, 3 * i - 3)
                        r_mmB.inc(t.matmul(PB[i % 2][:, 0:1],
                                           ohT_all[:, 128 * i:128 * (i + 1)],
                                           offmF[:, i:i + 1], start=True, stop=True))

                @blk.vector
                def _(v):
                    for i in range(16):
                        v.wait_ge(r_mmB.h, i + 1)
                        r_vB.inc(v.tensor_tensor(junk8[:], oh_all[:, 8 * i:8 * i + 8],
                                                 pos_all[:, 8 * i:8 * i + 8], op=ALU.mult))
                        v.wait_ge(r_vB.h, r_vB.n)
                        r_vB.inc(v.tensor_reduce(r_c1[:], junk8[:], op=ALU.add, axis=AX.X))
                        v.wait_ge(r_vB.h, r_vB.n)
                        VE(v, v.tensor_tensor(r_c2[:], r_c1[:], PB[i % 2][:, 0:1],
                                              op=ALU.add))
                        VE(v, v.tensor_scalar(posd_sb[:, i:i + 1], r_c2[:], -1.0, None,
                                              op0=ALU.add))
                        VE(v, v.tensor_scalar(keep_sb[:, i:i + 1], posd_sb[:, i:i + 1],
                                              float(CAP), None, op0=ALU.is_lt))
                        VE(v, v.tensor_scalar(r_c1[:], eidx_sb[:, i:i + 1], float(CAP),
                                              None, op0=ALU.mult))
                        VE(v, v.tensor_tensor(slotf_sb[:, i:i + 1], r_c1[:],
                                              posd_sb[:, i:i + 1], op=ALU.add))
                        VE(v, v.tensor_scalar(r_c2[:], keep_sb[:, i:i + 1], -1.0, 1.0,
                                              op0=ALU.mult, op1=ALU.add))
                        VE(v, v.tensor_tensor(r_c1[:], slotf_sb[:, i:i + 1],
                                              keep_sb[:, i:i + 1], op=ALU.mult))
                        VE(v, v.scalar_tensor_tensor(r_c3[:], r_c2[:], SENT,
                                                     r_c1[:], op0=ALU.mult, op1=ALU.add))
                        VE(v, v.tensor_copy(slotu_sb[:, i:i + 1], r_c3[:]))
                        r_vB.inc(v.tensor_tensor(gs_sb[:, i:i + 1], gate_sb[:, i:i + 1],
                                                 keep_sb[:, i:i + 1], op=ALU.mult))
            M["route"] = r_vB.n

            bBX = [b_x[0].n, b_x[1].n]
            bXS = [x_sc[0].n, x_sc[1].n]
            with nc.Block() as blk:
                @blk.sync
                def _(s):
                    s.wait_ge(b_st[0].h, b_st[0].n)
                    s.wait_ge(b_st[1].h, b_st[1].n)
                    for i in range(16):
                        if i >= 2:
                            s.wait_ge(x_sc[i % 2].h, bXS[i % 2] + 16 * ((i - 2) // 2 + 1))
                        b_x[i % 2].inc(s.dma_start(
                            out=xnb_ev[i % 2],
                            in_=xn_tm_c[i * 128:(i + 1) * 128, :]), 16)

                @blk.gpsimd
                def _(g):
                    g.wait_ge(r_vB.h, M["route"])
                    g.wait_ge(g_gs.h, M["xs_zero_g"])
                    g.wait_ge(z_as.h, M["xs_zero_a"])
                    for i in range(16):
                        g.wait_ge(b_x[i % 2].h, bBX[i % 2] + 16 * (i // 2 + 1))
                        x_sc[i % 2].inc(g.indirect_dma_start(
                            out=xs_c,
                            out_offset=bass.IndirectOffsetOnAxis(slotu_sb[:, i:i + 1], 0),
                            in_=xnb_ev[i % 2], in_offset=None,
                            bounds_check=TTOK - 1, oob_is_err=False), 16)

            def emit_a2a3():
                with nc.Block() as blk:
                    @blk.gpsimd
                    def _(g):
                        g.wait_ge(x_sc[0].h, x_sc[0].n)
                        g.wait_ge(x_sc[1].h, x_sc[1].n)
                        with allow_ici_all_to_all():
                            cc.inc(g.collective_compute(
                                "AllToAll", ALU.bypass, replica_groups=[list(range(8))],
                                ins=[xs_c], outs=[xs_all]))
                M["cc_a2a3"] = cc.n
            if DP == 2:
                emit_a2a3()

        if DP >= 3:
            def fmb(par):
                return O_FM + par * 8192

            with nc.Block() as blk:
                @blk.vector
                def _(v):
                    p3_vm.inc(v.memset(ar(O_VAUG, VAUG_SZ), 1.0))

                @blk.sync
                def _(s):
                    s.wait_ge(cc.h, M["cc_ag1"])
                    s.wait_ge(f_cmb.h, M["fold_o"])
                    for tt in range(32):
                        r, tl = tt // 4, tt % 4
                        if tt >= 2:
                            s.wait_ge(p3_mmA.h, 4 * (tt - 1))
                            s.wait_ge(p3_mmV.h, 4 * (tt - 1))
                        for dcg in range(4):
                            src = xn_fm_full[r * 2048 + dcg * 512: r * 2048 + (dcg + 1) * 512,
                                             tl * 512:(tl + 1) * 512]
                            p3_fm[tt % 2].inc(s.dma_start(
                                out=ar(fmb(tt % 2) + dcg * 2048, 2048).rearrange(
                                    "p (dc t) -> p dc t", dc=4),
                                in_=src.rearrange("(dc p) t -> p dc t", p=128)), 16)
                        for g in range(4):
                            pidx = (tt - 1) * 4 + g
                            if pidx >= 0:
                                s.wait_ge(p3_ev.h, pidx + 1)
                                pg = pidx % 4
                                dst = Kfm if pg < 2 else Qfm
                                hl = pg % 2
                                ptt = pidx // 4
                                p3_ks[pidx % 4].inc(s.dma_start(
                                    out=dst[hl * 128:(hl + 1) * 128,
                                            ptt * 512:(ptt + 1) * 512],
                                    in_=kq_ev[pidx % 4][:]), 16)
                    for g in range(4):
                        pidx = 31 * 4 + g
                        s.wait_ge(p3_ev.h, pidx + 1)
                        pg = pidx % 4
                        dst = Kfm if pg < 2 else Qfm
                        hl = pg % 2
                        p3_ks[pidx % 4].inc(s.dma_start(
                            out=dst[hl * 128:(hl + 1) * 128, 31 * 512:32 * 512],
                            in_=kq_ev[pidx % 4][:]), 16)

                @blk.tensor
                def _(t):
                    t.wait_ge(f_cmb.h, M["fold_o"])
                    t.wait_ge(p3_vm.h, 1)
                    for tt in range(32):
                        t.wait_ge(p3_fm[tt % 2].h, 64 * (tt // 2 + 1))
                        for g in range(4):
                            idx = tt * 4 + g
                            w = "k" if g < 2 else "q"
                            hl = g % 2
                            if idx >= 2:
                                t.wait_ge(p3_ev.h, idx - 1)
                            for dc in range(NDC):
                                mmi = t.matmul(PA[g % 2][:, 0:512],
                                               weff(w, dc)[:, hl * 128:(hl + 1) * 128],
                                               ar(fmb(tt % 2) + dc * 512, 512),
                                               start=(dc == 0), stop=(dc == NDC - 1))
                            p3_mmA.inc(mmi)
                        for tc in range(4):
                            idx = tt * 4 + tc
                            if idx >= 2:
                                t.wait_ge(p3_vv.h, idx - 1)
                            for dc in range(NDC):
                                t.matmul(PB[tc % 2][:, 0:256],
                                         ar(fmb(tt % 2) + dc * 512 + tc * 128, 128),
                                         weff("v", dc), start=(dc == 0), stop=False)
                            p3_mmV.inc(t.matmul(PB[tc % 2][:, 0:256], ones_sb[:],
                                                bvr_sb[:], start=False, stop=True))

                @blk.vector
                def _(v):
                    for tt in range(32):
                        for g in range(4):
                            idx = tt * 4 + g
                            bcol = (bkc_sb if g < 2 else bqc_sb)[:, g % 2:g % 2 + 1]
                            v.wait_ge(p3_mmA.h, idx + 1)
                            if idx >= 4:
                                v.wait_ge(p3_ks[idx % 4].h, 16 * (idx // 4))
                            p3_ev.inc(v.tensor_scalar(kq_ev[idx % 4][:], PA[g % 2][:],
                                                      bcol, None, op0=ALU.add))
                        for tc in range(4):
                            idx = tt * 4 + tc
                            v.wait_ge(p3_mmV.h, idx + 1)
                            off = O_VAUG + idx * 258
                            VE(v, v.tensor_copy(ar(off, 128), PB[tc % 2][:, 0:128]))
                            p3_vv.inc(v.tensor_copy(ar(off + 129, 128),
                                                    PB[tc % 2][:, 128:256]))
            M["p3_mmA"] = p3_mmA.n
            M["p3_ks"] = [p3_ks[i].n for i in range(4)]
            if DP == 3:
                emit_a2a3()

        if DP >= 4:
            def kh(par):
                return O_KH + par * 4096

            def qp(par):
                return O_QP + par * 512

            def et(par):
                return O_ET

            kh1_thr = q_kh[1].n + 16
            with nc.Block() as blk:
                @blk.gpsimd
                def _(g, kh1_thr=kh1_thr):
                    g.wait_ge(q_kh[1].h, kh1_thr)
                    g.wait_ge(x_sc[0].h, x_sc[0].n)
                    g.wait_ge(x_sc[1].h, x_sc[1].n)
                    with allow_ici_all_to_all():
                        cc.inc(g.collective_compute(
                            "AllToAll", ALU.bypass, replica_groups=[list(range(8))],
                            ins=[xs_c], outs=[xs_all]))

                @blk.sync
                def _(s):
                    for i4 in range(4):
                        s.wait_ge(p3_ks[i4].h, M["p3_ks"][i4])
                    for ph in range(8):
                        b, hl = ph // 2, ph % 2
                        if ph >= 2:
                            s.wait_ge(q_mmS.h, 256 * (ph - 1))
                        q_kh[ph % 2].inc(s.dma_start(
                            out=ar(kh(ph % 2), 4096),
                            in_=Kfm[hl * 128:(hl + 1) * 128, b * 4096:(b + 1) * 4096]), 16)
                        for qb in range(8):
                            gqb = ph * 8 + qb
                            if gqb >= 2:
                                s.wait_ge(q_mmS.h, 32 * (gqb - 1))
                            q_qp[gqb % 2].inc(s.dma_start(
                                out=ar(qp(gqb % 2), 512),
                                in_=Qfm[hl * 128:(hl + 1) * 128,
                                        b * 4096 + qb * 512: b * 4096 + (qb + 1) * 512]), 16)
                        if hl == 1:
                            s.wait_ge(q_av.h, 64 * (b + 1))
                            for half in range(2):
                                q_st.inc(s.dma_start(
                                    out=att_c[b * 4096 + half * 2048:
                                              b * 4096 + (half + 1) * 2048, :]
                                        .rearrange("(q p) f -> p q f", p=128),
                                    in_=ar(O_STG + half * 4096, 4096).rearrange(
                                        "p (q f) -> p q f", f=256)), 16)

                @blk.tensor
                def _(t):
                    for ph in range(8):
                        b, hl = ph // 2, ph % 2
                        t.wait_ge(q_kh[ph % 2].h, 16 * (ph // 2 + 1))
                        for qb in range(8):
                            gqb = ph * 8 + qb
                            t.wait_ge(q_qp[gqb % 2].h, 16 * (gqb // 2 + 1))
                            for kc in range(32):
                                gsc = gqb * 32 + kc
                                if gsc >= 2:
                                    t.wait_ge(q_ex.h, gsc - 1)
                                q_mmS.inc(t.matmul(PA[kc % 2][:],
                                                   ar(kh(ph % 2) + kc * 128, 128),
                                                   ar(qp(gqb % 2), 512),
                                                   start=True, stop=True))
                            t.wait_ge(q_ex.h, 32 * (gqb + 1))
                            for qs in range(4):
                                gqs = gqb * 4 + qs
                                if gqs >= 2:
                                    t.wait_ge(q_av.h, gqs - 1)
                                for kc in range(32):
                                    gtc = b * 32 + kc
                                    mmi = t.matmul(PA[2 + qs % 2][:, 0:129],
                                                   ar(et(gqb % 2) + kc * 512 + qs * 128, 128),
                                                   ar(O_VAUG + gtc * 258 + hl * 129, 129),
                                                   start=(kc == 0), stop=(kc == 31))
                                q_mmA.inc(mmi)

                @blk.scalar
                def _(sc):
                    for ph in range(8):
                        for qb in range(8):
                            gqb = ph * 8 + qb
                            for kc in range(32):
                                gsc = gqb * 32 + kc
                                sc.wait_ge(q_mmS.h, gsc + 1)
                                if kc == 0 and gqb >= 1:
                                    sc.wait_ge(q_mmA.h, 4 * gqb)
                                q_ex.inc(sc.activation(
                                    ar(et(gqb % 2) + kc * 512, 512), PA[kc % 2][:],
                                    AF.Exp, bias=csh_sb[:], scale=SCALE))

                @blk.vector
                def _(v):
                    for ph in range(8):
                        b, hl = ph // 2, ph % 2
                        for qb in range(8):
                            for qs in range(4):
                                gqs = (ph * 8 + qb) * 4 + qs
                                qspp = qb * 4 + qs
                                if hl == 0 and qspp == 0 and b >= 1:
                                    v.wait_ge(q_st.h, 32 * b)
                                v.wait_ge(q_mmA.h, gqs + 1)
                                q_rc.inc(v.reciprocal(rec_sb[:],
                                                      PA[2 + qs % 2][:, 128:129]))
                                v.wait_ge(q_rc.h, q_rc.n)
                                q_av.inc(v.tensor_scalar(
                                    ar(O_STG + qspp * 256 + hl * 128, 128),
                                    PA[2 + qs % 2][:, 0:128], rec_sb[:], None,
                                    op0=ALU.mult))
            M["p4_av"] = q_mmA.n
            M["p4_st"] = q_st.n
            M["cc_a2a3"] = cc.n

            with nc.Block() as blk:
                @blk.gpsimd
                def _(g):
                    g.wait_ge(q_st.h, M["p4_st"])
                    with allow_ici_all_to_all():
                        cc.inc(g.collective_compute(
                            "AllToAll", ALU.bypass, replica_groups=[list(range(8))],
                            ins=[att_c], outs=[att_all]))
            M["cc_a2a4"] = cc.n

        if DP >= 5:
            def emit_ffn(tag, w1src, w3src, w2src, out_dram, moe, prev_gate):
                for half in range(2):
                    gate = prev_gate if half == 0 else M[f"{tag}_w2g_{half - 1}"]
                    bTR = n_tr.n
                    bTC = n_tc.n
                    bGA = [n_ga[0].n, n_ga[1].n]
                    bGX = n_gx.n
                    if moe:
                        bSL = [m_sl[0].n, m_sl[1].n]
                        bMG = m_mg.n
                        with nc.Block() as blk:
                            @blk.sync
                            def _(s, half=half, gate=gate, bSL=bSL, bMG=bMG):
                                s.wait_ge(cc.h, M["cc_a2a3"])
                                s.wait_ge(gate[0].h, gate[1])
                                for st_ in range(8):
                                    for j in range(8):
                                        gidx = st_ * 8 + j
                                        if gidx >= 2:
                                            s.wait_ge(m_mg.h, bMG + gidx - 1)
                                        m_sl[j % 2].inc(s.dma_start(
                                            out=xsh[j % 2],
                                            in_=xs_all[j * 2048 + half * 1024 + st_ * 128:
                                                       j * 2048 + half * 1024 + (st_ + 1) * 128,
                                                       :]), 16)

                            @blk.vector
                            def _(v, half=half, gate=gate, bSL=bSL, bTR=bTR):
                                v.wait_ge(gate[0].h, gate[1])

                                def copies(st2):
                                    for dc in range(NDC):
                                        g2 = st2 * 16 + dc
                                        v.wait_ge(n_tr.h, bTR + g2 + 1)
                                        n_tc.inc(v.tensor_copy(
                                            ar(O_XD + dc * 1024 + st2 * 128, 128),
                                            PT[dc % 2][:]))

                                for st_ in range(8):
                                    for j in range(8):
                                        v.wait_ge(m_sl[j % 2].h,
                                                  bSL[j % 2] + 16 * (st_ * 4 + j // 2 + 1))
                                        if j == 0:
                                            if st_ >= 2:
                                                v.wait_ge(n_tr.h, bTR + 16 * (st_ - 1))
                                            ii5 = v.tensor_copy(accS, xsh[0])
                                        else:
                                            dst = xdtm[st_ % 2] if j % 2 == 1 else accS
                                            srcacc = accS if j % 2 == 1 else xdtm[st_ % 2]
                                            ii5 = v.tensor_tensor(dst, srcacc,
                                                                  xsh[j % 2], op=ALU.add)
                                        m_mg.inc(ii5)
                                        v.wait_ge(m_mg.h, m_mg.n)
                                    if st_ >= 1:
                                        copies(st_ - 1)
                                copies(7)

                            @blk.tensor
                            def _(t, gate=gate, bMG=bMG, bTC=bTC):
                                t.wait_ge(gate[0].h, gate[1])
                                for st_ in range(8):
                                    t.wait_ge(m_mg.h, bMG + 8 * (st_ + 1))
                                    for dc in range(NDC):
                                        g2 = st_ * 16 + dc
                                        if g2 >= 2:
                                            t.wait_ge(n_tc.h, bTC + g2 - 1)
                                        n_tr.inc(t.transpose(
                                            PT[dc % 2][:],
                                            xdtm[st_ % 2][:, dc * 128:(dc + 1) * 128],
                                            ident_sb[:]))
                        M[f"{tag}_xd_{half}"] = ("v", n_tc.n)
                    else:
                        with nc.Block() as blk:
                            @blk.sync
                            def _(s, half=half, gate=gate):
                                s.wait_ge(gate[0].h, gate[1])
                                s.wait_ge(b_sf[0].h, b_sf[0].n)
                                s.wait_ge(b_sf[1].h, b_sf[1].n)
                                for dcg in range(4):
                                    n_gx.inc(s.dma_start(
                                        out=ar(O_XD + dcg * 4096, 4096).rearrange(
                                            "p (dc t) -> p dc t", dc=4),
                                        in_=xn_fm_c.rearrange("(dc p) t -> p dc t", p=128)
                                            [:, dcg * 4:(dcg + 1) * 4,
                                             half * 1024:(half + 1) * 1024]), 16)
                        M[f"{tag}_xd_{half}"] = ("s", n_gx.n)

                    if tag == "sh" and "shh" in KCUT:
                        M[f"{tag}_h_{half}"] = n_ht.n
                        M[f"{tag}_w2g_{half}"] = (n_gx, n_gx.n)
                        continue
                    bW1 = [n_w1[0].n, n_w1[1].n]
                    bHM = n_hm.n
                    bSI = n_si.n
                    bHT = n_ht.n
                    with nc.Block() as blk:
                        @blk.gpsimd
                        def _(g, w1src=w1src, w3src=w3src, gate=gate, bHT=bHT):
                            g.wait_ge(gate[0].h, gate[1])
                            for hc in range(NHC):
                                if hc >= 2:
                                    g.wait_ge(n_ht.h, bHT + 2 * (hc - 1))
                                for wj, wsrc in ((0, w1src), (1, w3src)):
                                    for dcg in range(4):
                                        n_w1[hc % 2].inc(g.dma_start(
                                            out=ar(O_WB + (hc % 2) * 4096 + wj * 2048
                                                   + dcg * 512, 512).rearrange(
                                                "p (dc h) -> p dc h", dc=4),
                                            in_=wsrc[hc, dcg * 4:(dcg + 1) * 4].rearrange(
                                                "dc p h -> p dc h")), 16)

                        @blk.tensor
                        def _(t, half=half, tag=tag, bW1=bW1, bHT=bHT):
                            kind, val = M[f"{tag}_xd_{half}"]
                            t.wait_ge((n_tc if kind == "v" else n_gx).h, val)
                            for hc in range(NHC):
                                t.wait_ge(n_w1[hc % 2].h, bW1[hc % 2] + 128 * (hc // 2 + 1))
                                for t2 in range(2):
                                    hidx = hc * 2 + t2
                                    if hidx >= 2:
                                        t.wait_ge(n_ht.h, bHT + hidx - 1)
                                    for wj in range(2):
                                        dst = PA[t2 % 2] if wj == 0 else PA[2 + t2 % 2]
                                        for dc in range(NDC):
                                            mmi = t.matmul(dst[:],
                                                           ar(O_WB + (hc % 2) * 4096
                                                              + wj * 2048 + dc * 128, 128),
                                                           ar(O_XD + dc * 1024 + t2 * 512, 512),
                                                           start=(dc == 0),
                                                           stop=(dc == NDC - 1))
                                    n_hm.inc(mmi)

                        @blk.scalar
                        def _(sc, bHM=bHM, bHT=bHT):
                            for hc in range(NHC):
                                for t2 in range(2):
                                    hidx = hc * 2 + t2
                                    sc.wait_ge(n_hm.h, bHM + hidx + 1)
                                    if hidx >= 2:
                                        sc.wait_ge(n_ht.h, bHT + hidx - 1)
                                    n_si.inc(sc.activation(s1_ev[t2 % 2][:],
                                                           PA[t2 % 2][:], AF.Silu))

                        @blk.vector
                        def _(v, bSI=bSI):
                            for hc in range(NHC):
                                for t2 in range(2):
                                    hidx = hc * 2 + t2
                                    v.wait_ge(n_si.h, bSI + hidx + 1)
                                    n_ht.inc(v.tensor_tensor(
                                        ar(O_HH + hc * 1024 + t2 * 512, 512),
                                        s1_ev[t2 % 2][:], PA[2 + t2 % 2][:],
                                        op=ALU.mult))

                        if DP == 5 and moe and half == 0:
                            @blk.sync
                            def _(s, bHT=bHT, half=half, tag=tag):
                                kind, val = M[f"{tag}_xd_{half}"]
                                s.wait_ge((n_tc if kind == "v" else n_gx).h, val)
                                for dc in range(NDC):
                                    dx_s.inc(s.dma_start(
                                        out=dbg["d_xd"][dc * 128:(dc + 1) * 128, :],
                                        in_=ar(O_XD + dc * 1024, 1024)), 16)
                                s.wait_ge(n_ht.h, bHT + 2 * NHC)
                                for hc in range(NHC):
                                    dx_s.inc(s.dma_start(
                                        out=dbg["d_hh"][hc * 128:(hc + 1) * 128, :],
                                        in_=ar(O_HH + hc * 1024, 1024)), 16)
                    M[f"{tag}_h_{half}"] = n_ht.n

                    if tag == "sh" and "shw2" in KCUT:
                        M[f"{tag}_w2g_{half}"] = (n_ht, n_ht.n)
                        continue
                    for oh in range(2):
                        bW2 = n_w2.n
                        bM2 = n_m2.n
                        bOE = n_oe.n
                        bOS = [n_os[0].n, n_os[1].n]
                        with nc.Block() as blk:
                            @blk.tensor
                            def _(t, bW2=bW2, bOE=bOE):
                                t.wait_ge(n_w2.h, bW2 + 16 * NHC)
                                for stl in range(8):
                                    for otl in range(2):
                                        idx = stl * 2 + otl
                                        if idx >= 2:
                                            t.wait_ge(n_oe.h, bOE + idx - 1)
                                        for hc in range(NHC):
                                            mmi = t.matmul(PA[idx % 2][:],
                                                           ar(O_HH + hc * 1024 + stl * 128, 128),
                                                           ar(O_W2 + hc * 1024 + otl * 512, 512),
                                                           start=(hc == 0),
                                                           stop=(hc == NHC - 1))
                                        n_m2.inc(mmi)

                            @blk.vector
                            def _(v, bM2=bM2, bOS=bOS):
                                for stl in range(8):
                                    for otl in range(2):
                                        idx = stl * 2 + otl
                                        v.wait_ge(n_m2.h, bM2 + idx + 1)
                                        if idx >= 2:
                                            v.wait_ge(n_os[idx % 2].h,
                                                      bOS[idx % 2] + 16 * (idx // 2))
                                        n_oe.inc(v.tensor_copy(oe_ev[idx % 2][:],
                                                               PA[idx % 2][:]))

                            @blk.gpsimd
                            def _(g, w2src=w2src, tag=tag, half=half, oh=oh, bM2=bM2):
                                g.wait_ge(n_ht.h, M[f"{tag}_h_{half}"])
                                if DP == 5 and moe and half == 0:
                                    g.wait_ge(dx_s.h, 16 * (NDC + NHC))
                                if oh == 1:
                                    g.wait_ge(n_m2.h, bM2)
                                for hc in range(NHC):
                                    n_w2.inc(g.dma_start(
                                        out=ar(O_W2 + hc * 1024, 1024),
                                        in_=w2src[hc * 128:(hc + 1) * 128,
                                                  oh * 1024:(oh + 1) * 1024]), 16)

                            @blk.sync
                            def _(s, out_dram=out_dram, half=half, oh=oh, bOE=bOE):
                                for stl in range(8):
                                    for otl in range(2):
                                        idx = stl * 2 + otl
                                        s.wait_ge(n_oe.h, bOE + idx + 1)
                                        n_os[idx % 2].inc(s.dma_start(
                                            out=out_dram[half * 1024 + stl * 128:
                                                         half * 1024 + (stl + 1) * 128,
                                                         oh * 1024 + otl * 512:
                                                         oh * 1024 + (otl + 1) * 512],
                                            in_=oe_ev[idx % 2][:]), 16)
                    M[f"{tag}_w2g_{half}"] = (n_m2, n_m2.n)
                M[f"{tag}_st"] = (n_os[0].n, n_os[1].n)

            if DP >= 4:
                first_gate = (q_mmA, M["p4_av"])
            else:
                first_gate = (p3_mmA, M.get("p3_mmA", 0)) if DP >= 3 else (cc, 0)
            emit_ffn("moe", W1t, W3t, W2d, oe_c, True, first_gate)

            with nc.Block() as blk:
                @blk.gpsimd
                def _(g):
                    g.wait_ge(n_os[0].h, M["moe_st"][0])
                    g.wait_ge(n_os[1].h, M["moe_st"][1])
                    cc.inc(g.collective_compute(
                        "AllGather", ALU.bypass, replica_groups=[list(range(8))],
                        ins=[oe_c], outs=[oe_full]))
            M["cc_ag5"] = cc.n

        if DP >= 6:
            emit_ffn("sh", sw1t, sw3t, sw2d, shared_my, False, M["moe_w2g_1"])

        if DP >= 7:
            with nc.Block() as blk:
                @blk.sync
                def _(s):
                    s.wait_ge(cc.h, M["cc_a2a4"])
                    g = M["sh_w2g_1"]
                    s.wait_ge(g[0].h, g[1])
                    for fc in range(16):
                        j, hl = fc // 2, fc % 2
                        o_af.inc(s.dma_start(
                            out=ar(O_AFM + fc * 2048, 2048),
                            in_=att_all[j * 2048:(j + 1) * 2048,
                                        hl * 128:(hl + 1) * 128],
                            transpose=True), 16)
                    s.wait_ge(f_wst[0].h, M["woTe_w0"])
                    s.wait_ge(f_wst[1].h, M["woTe_w1"])
                    for fc in range(16):
                        o_wo.inc(s.dma_start(
                            out=ar(O_WOE + fc * 2048, 2048),
                            in_=WoTe[fc * 128:(fc + 1) * 128, :]), 16)
                    for tc in range(16):
                        if tc >= 2:
                            s.wait_ge(o_fr6.h, tc - 1)
                        o_x[tc % 2].inc(s.dma_start(
                            out=xt_f32[tc % 2], in_=x_my[tc * 128:(tc + 1) * 128, :]), 16)
                        o_sh[tc % 2].inc(s.dma_start(
                            out=shbuf[tc % 2][:],
                            in_=shared_my[tc * 128:(tc + 1) * 128, :]), 16)
                        for ot in range(4):
                            idx = tc * 4 + ot
                            s.wait_ge(o_ev.h, idx + 1)
                            o_st[idx % 2].inc(s.dma_start(
                                out=out_my[tc * 128:(tc + 1) * 128,
                                           ot * 512:(ot + 1) * 512],
                                in_=out_ev[idx % 2]), 16)

                @blk.gpsimd
                def _(g):
                    g.wait_ge(cc.h, M["cc_ag5"])
                    for tc in range(16):
                        if tc >= 2:
                            g.wait_ge(o_fr5.h, tc - 1)
                        g.memset(moebuf[tc % 2][:], 0.0)
                        o_mg[tc % 2].inc(g.indirect_dma_start(
                            out=moebuf[tc % 2][:], out_offset=None,
                            in_=oe_full,
                            in_offset=bass.IndirectOffsetOnAxis(slotu_sb[:, tc:tc + 1], 0),
                            bounds_check=NCORE * CAP - 1, oob_is_err=False), 16)

                @blk.tensor
                def _(t):
                    t.wait_ge(o_af.h, 16 * 16)
                    t.wait_ge(o_wo.h, 16 * 16)
                    for tc in range(16):
                        for ot in range(4):
                            idx = tc * 4 + ot
                            if idx >= 2:
                                t.wait_ge(o_ev.h, idx - 1)
                            for fc in range(16):
                                t.matmul(PA[idx % 2][:],
                                         ar(O_AFM + fc * 2048 + tc * 128, 128),
                                         ar(O_WOE + fc * 2048 + ot * 512, 512),
                                         start=(fc == 0), stop=False)
                            o_mm.inc(t.matmul(PA[idx % 2][:], ones_sb[:],
                                              bor_sb[:, ot * 512:(ot + 1) * 512],
                                              start=False, stop=True))

                @blk.vector
                def _(v):
                    for tc in range(16):
                        v.wait_ge(o_mg[tc % 2].h, 16 * (tc // 2 + 1))
                        v.wait_ge(o_x[tc % 2].h, 16 * (tc // 2 + 1))
                        v.wait_ge(o_sh[tc % 2].h, 16 * (tc // 2 + 1))
                        for ot in range(4):
                            idx = tc * 4 + ot
                            v.wait_ge(o_mm.h, idx + 1)
                            if idx >= 2:
                                v.wait_ge(o_st[idx % 2].h, 16 * (idx // 2))
                            VE(v, v.scalar_tensor_tensor(
                                tmp1_f, moebuf[tc % 2][:, ot * 512:(ot + 1) * 512],
                                gs_sb[:, tc:tc + 1], PA[idx % 2][:],
                                op0=ALU.mult, op1=ALU.add))
                            VE(v, v.tensor_tensor(
                                tmp2_f, tmp1_f,
                                xt_f32[tc % 2][:, ot * 512:(ot + 1) * 512], op=ALU.add))
                            o_ev.inc(v.tensor_tensor(
                                out_ev[idx % 2], tmp2_f,
                                shbuf[tc % 2][:, ot * 512:(ot + 1) * 512], op=ALU.add))
                        o_fr5.inc(v.nop())
                        o_fr6.inc(v.nop())

        with nc.Block() as blk:
            @blk.sync
            def _(s):
                if DP < 99:
                    if DP >= 1:
                        s.wait_ge(cc.h, M["cc_ag1"])
                        s.wait_ge(b_st[0].h, b_st[0].n)
                        s.wait_ge(b_st[1].h, b_st[1].n)
                        d_st.inc(s.dma_start(out=dbg["d_xn_tm"], in_=xn_tm_c), 16)
                        d_st.inc(s.dma_start(out=dbg["d_xn_fm"], in_=xn_fm_full), 16)
                    if DP >= 2:
                        s.wait_ge(cc.h, M["cc_a2a3"])
                        d_st.inc(s.dma_start(out=dbg["d_xs"], in_=xs_all), 16)
                        if DP == 2:
                            d_st.inc(s.dma_start(out=dbg["d_xsc"], in_=xs_c), 16)
                            d_st.inc(s.dma_start(out=dbg["d_slotu"], in_=slotu_sb[:]), 16)
                            d_st.inc(s.dma_start(out=dbg["d_eq"], in_=logits_all2[:]), 16)
                            d_st.inc(s.dma_start(out=dbg["d_vl"], in_=logits_all3[:]), 16)
                            d_st.inc(s.dma_start(out=dbg["d_nm"][:, 0:16], in_=nm_all[:]), 16)
                            d_st.inc(s.dma_start(out=dbg["d_nm"][:, 16:32], in_=pmax_all[:]), 16)
                        for k, srcb in enumerate([eidx_sb, gate_sb, posd_sb, keep_sb,
                                                  slotf_sb, gs_sb]):
                            d_st.inc(s.dma_start(out=dbg["d_route"][:, k * 16:(k + 1) * 16],
                                                 in_=srcb[:]), 16)
                    if DP >= 3:
                        for i4 in range(4):
                            s.wait_ge(p3_ks[i4].h, M["p3_ks"][i4])
                        d_st.inc(s.dma_start(out=dbg["d_Kfm"], in_=Kfm), 16)
                        d_st.inc(s.dma_start(out=dbg["d_Qfm"], in_=Qfm), 16)
                    if DP >= 4:
                        s.wait_ge(cc.h, M["cc_a2a4"])
                        d_st.inc(s.dma_start(out=dbg["d_att"], in_=att_all), 16)
                    if DP >= 5:
                        s.wait_ge(cc.h, M["cc_ag5"])
                        d_st.inc(s.dma_start(out=dbg["d_oe"], in_=oe_full), 16)
                    if DP >= 6:
                        s.wait_ge(n_os[0].h, M["sh_st"][0])
                        s.wait_ge(n_os[1].h, M["sh_st"][1])
                        if "shdump" not in KCUT:
                            d_st.inc(s.dma_start(out=dbg["d_shared"], in_=shared_my), 16)
                    s.wait_ge(d_st.h, d_st.n)
                if DP >= 7:
                    s.wait_ge(o_st[0].h, o_st[0].n)
                    s.wait_ge(o_st[1].h, o_st[1].n)

            @blk.gpsimd
            def _(g):
                g.wait_ge(g_gs.h, g_gs.n)

    return nc, dbg


def _tile_w(W):
    return np.ascontiguousarray(
        np.asarray(W).reshape(NDC, 128, NHC, 128).transpose(2, 0, 1, 3))


def _prep_inputs(inputs):
    f32 = np.float32
    x = np.ascontiguousarray(np.asarray(inputs["L0_attn_x"], f32).reshape(TTOK, D))
    WT = {w: np.ascontiguousarray(np.asarray(inputs[f"L0_attn_W{w}"], f32).T)
          for w in "qkvo"}
    A_T = {w: np.ascontiguousarray(np.asarray(inputs[f"L0_attn_A{w}"], f32).T)
           for w in "qkvo"}
    Bm = {w: np.ascontiguousarray(np.asarray(inputs[f"L0_attn_B{w}"], f32))
          for w in "qkvo"}
    bias = {w: np.asarray(inputs[f"L0_attn_b{w}"], f32) for w in "qkvo"}
    Wg = np.ascontiguousarray(np.asarray(inputs["L0_ffn_Wg"], f32))
    W1 = np.asarray(inputs["L0_ffn_W1"], f32)
    W3 = np.asarray(inputs["L0_ffn_W3"], f32)
    W2 = np.asarray(inputs["L0_ffn_W2"], f32)
    sw1t = _tile_w(inputs["L0_ffn_sw1"])
    sw3t = _tile_w(inputs["L0_ffn_sw3"])
    sw2 = np.ascontiguousarray(np.asarray(inputs["L0_ffn_sw2"], f32))

    bf = ml_dtypes.bfloat16
    consts = {
        "ident": np.eye(128, dtype=f32).astype(bf),
        "ident32": np.eye(128, dtype=f32),
        "TRI": np.triu(np.ones((128, 128), f32)).astype(bf),
        "iota8": np.tile(np.arange(8, dtype=f32), (128, 1)),
        "iota1k": np.tile(np.arange(8, dtype=f32) + 1000.0, (128, 1)),
        "onesrow": np.ones((1, 128), f32).astype(bf),
        "ones816": np.ones((8, 16), f32),
    }

    in_maps = []
    for c in range(NCORE):
        fsl = slice(256 * c, 256 * (c + 1))
        m = {
            "x_my": np.ascontiguousarray(x[c * TOWN:(c + 1) * TOWN]),
            "WqT": np.ascontiguousarray(WT["q"][:, fsl]),
            "WkT": np.ascontiguousarray(WT["k"][:, fsl]),
            "WvT": np.ascontiguousarray(WT["v"][:, fsl]),
            "WoT": WT["o"],
            "Bq": Bm["q"], "Bk": Bm["k"], "Bv": Bm["v"], "Bo": Bm["o"],
            "AqT": np.ascontiguousarray(A_T["q"][:, fsl]),
            "AkT": np.ascontiguousarray(A_T["k"][:, fsl]),
            "AvT": np.ascontiguousarray(A_T["v"][:, fsl]),
            "AoT": A_T["o"],
            "bq_c": np.ascontiguousarray(bias["q"][fsl].reshape(2, 128).T),
            "bk_c": np.ascontiguousarray(bias["k"][fsl].reshape(2, 128).T),
            "bv_r": np.ascontiguousarray(bias["v"][fsl].reshape(1, 256)),
            "bo_r": np.ascontiguousarray(bias["o"].reshape(1, D)),
            "Wg": Wg,
            "W1t": _tile_w(W1[c]),
            "W3t": _tile_w(W3[c]),
            "W2d": np.ascontiguousarray(W2[c]),
            "sw1t": sw1t, "sw3t": sw3t, "sw2d": sw2,
            "maskp": np.tile((np.arange(8, dtype=f32) < c).astype(f32), (8, 1)),
            "tok1": (c * TOWN + 1 + np.arange(128, dtype=f32)[:, None]
                     + 128.0 * np.arange(16, dtype=f32)[None, :]),
        }
        m.update(consts)
        in_maps.append(m)
    return in_maps


def kernel(**inputs):
    in_maps = _prep_inputs(inputs)
    nc, _ = build_nc(DP=99)
    res = run_bass_kernel_spmd(nc, in_maps, core_ids=list(range(NCORE)))
    out = np.concatenate([np.asarray(r["out_my"], np.float32) for r in res.results],
                         axis=0)
    return out.reshape(4, 4096, 2048)


# version 14
# speedup vs baseline: 1.1432x; 1.0471x over previous
import os
import numpy as np
import ml_dtypes
import concourse.bass as bass
import concourse.mybir as mybir
from concourse.bass_utils import run_bass_kernel_spmd
from concourse.aot_env import allow_ici_all_to_all
from contextlib import ExitStack

F32 = mybir.dt.float32
BF16 = mybir.dt.bfloat16
U32 = mybir.dt.uint32
AF = mybir.ActivationFunctionType
ALU = mybir.AluOpType
AX = mybir.AxisListType

D = 2048
TTOK = 16384
TOWN = 2048
HFF = 2816
NCORE = 8
CAP = 2048
SCALE = 128.0 ** -0.5
LORA_S = 0.5
EPS = 1e-5
CSHIFT = 4.0
SENT = 1.0e9
NDC = 16
NHC = 22


class Sem:
    def __init__(self, h):
        self.h = h
        self.n = 0

    def inc(self, instr, k=1):
        instr.then_inc(self.h, k)
        self.n += k
        return self.n


def build_nc(DP=99):
    SUB = int(os.environ.get("KSUB", "9"))
    KCUT = set(x for x in os.environ.get("KCUT", "").split(",") if x)
    nc = bass.Bass("TRN2", target_bir_lowering=False, debug=False,
                   dynamic_dma_scratch_size=16384,
                   detect_race_conditions=False)

    def din(name, shape, dt=F32):
        return nc.dram_tensor(name, shape, dt, kind="ExternalInput").ap()

    def dout(name, shape, dt=F32):
        return nc.dram_tensor(name, shape, dt, kind="ExternalOutput").ap()

    def dtmp(name, shape, dt):
        return nc.dram_tensor(name, shape, dt).ap()

    x_my = din("x_my", [TOWN, D])
    WqT = din("WqT", [D, 256]); WkT = din("WkT", [D, 256]); WvT = din("WvT", [D, 256])
    WoT = din("WoT", [D, D])
    Bq = din("Bq", [16, D]); Bk = din("Bk", [16, D]); Bv = din("Bv", [16, D]); Bo = din("Bo", [16, D])
    AqT = din("AqT", [16, 256]); AkT = din("AkT", [16, 256]); AvT = din("AvT", [16, 256])
    AoT = din("AoT", [16, D])
    bq_c = din("bq_c", [128, 2]); bk_c = din("bk_c", [128, 2])
    bv_r = din("bv_r", [1, 256]); bo_r = din("bo_r", [1, D])
    Wg = din("Wg", [D, 8])
    W1t = din("W1t", [NHC, NDC, 128, 128])
    W3t = din("W3t", [NHC, NDC, 128, 128])
    W2d = din("W2d", [HFF, D])
    sw1t = din("sw1t", [NHC, NDC, 128, 128])
    sw3t = din("sw3t", [NHC, NDC, 128, 128])
    sw2d = din("sw2d", [HFF, D])
    ident = din("ident", [128, 128], BF16)
    ident32 = din("ident32", [128, 128], F32)
    TRI = din("TRI", [128, 128], BF16)
    iota8 = din("iota8", [128, 8])
    iota1k = din("iota1k", [128, 8])
    onesrow = din("onesrow", [1, 128], BF16)
    ones816 = din("ones816", [8, 16])
    maskp = din("maskp", [8, 8])
    tok1 = din("tok1", [128, 16])

    out_my = dout("out_my", [TOWN, D])

    xn_tm_c = dtmp("xn_tm_c", [TOWN, D], BF16)
    xn_fm_c = dtmp("xn_fm_c", [D, TOWN], BF16)
    xn_fm_full = dtmp("xn_fm_full", [NCORE * D, TOWN], BF16)
    hist_c = dtmp("hist_c", [8, 1], F32)
    hist_full = dtmp("hist_full", [64, 1], F32)
    xs_c = dtmp("xs_c", [TTOK, D], BF16)
    xs_all = dtmp("xs_all", [TTOK, D], BF16)
    Kfm = dtmp("Kfm", [256, TTOK], BF16)
    Qfm = dtmp("Qfm", [256, TTOK], BF16)
    att_c = dtmp("att_c", [TTOK, 256], BF16)
    att_all = dtmp("att_all", [TTOK, 256], BF16)
    WoTe = dtmp("WoTe", [D, D], BF16)
    oe_c = dtmp("oe_c", [CAP, D], BF16)
    oe_full = dtmp("oe_full", [NCORE * CAP, D], BF16)
    shared_my = dtmp("shared_my", [TOWN, D], BF16)

    dbg = {}
    if DP < 99:
        if DP >= 1:
            dbg["d_xn_tm"] = dout("d_xn_tm", [TOWN, D], BF16)
            dbg["d_xn_fm"] = dout("d_xn_fm", [NCORE * D, TOWN], BF16)
        if DP >= 2:
            dbg["d_route"] = dout("d_route", [128, 16 * 6])
            dbg["d_xs"] = dout("d_xs", [TTOK, D], BF16)
            if DP == 2:
                dbg["d_xsc"] = dout("d_xsc", [TTOK, D], BF16)
                dbg["d_slotu"] = dout("d_slotu", [128, 16], U32)
                dbg["d_eq"] = dout("d_eq", [128, 128])
                dbg["d_vl"] = dout("d_vl", [128, 128])
                dbg["d_nm"] = dout("d_nm", [128, 32])
        if DP >= 3:
            dbg["d_Kfm"] = dout("d_Kfm", [256, TTOK], BF16)
            dbg["d_Qfm"] = dout("d_Qfm", [256, TTOK], BF16)
        if DP >= 4:
            dbg["d_att"] = dout("d_att", [TTOK, 256], BF16)
        if DP >= 5:
            dbg["d_oe"] = dout("d_oe", [NCORE * CAP, D], BF16)
        if DP == 5:
            dbg["d_xd"] = dout("d_xd", [2048, 1024], BF16)
            dbg["d_hh"] = dout("d_hh", [NHC * 128, 1024], BF16)
        if DP >= 6:
            dbg["d_shared"] = dout("d_shared", [TOWN, D], BF16)

    with ExitStack() as stk:
        def sb(name, shape, dt):
            return stk.enter_context(nc.sbuf_tensor(name, shape, dt))

        def psumt(name, shape, dt):
            return stk.enter_context(nc.psum_tensor(name, shape, dt))

        def S(name):
            return Sem(stk.enter_context(nc.semaphore(name)))

        AR = 90032
        arena = sb("arena", [128, AR], BF16)
        f32ar = sb("f32ar", [128, 6144], F32)
        ident_sb = sb("ident_sb", [128, 128], BF16)
        id32_sb = sb("id32_sb", [128, 128], F32)
        tri_sb = sb("tri_sb", [128, 128], BF16)
        iota8_sb = sb("iota8_sb", [128, 8], F32)
        iota1k_sb = sb("iota1k_sb", [128, 8], F32)
        ones_sb = sb("ones_sb", [1, 128], BF16)
        ones816_sb = sb("ones816_sb", [8, 16], F32)
        wg_sb = sb("wg_sb", [128, NDC * 8], BF16)
        bqc_sb = sb("bqc_sb", [128, 2], F32)
        bkc_sb = sb("bkc_sb", [128, 2], F32)
        bvr_sb = sb("bvr_sb", [1, 256], BF16)
        bor_sb = sb("bor_sb", [1, D], BF16)
        maskp_sb = sb("maskp_sb", [8, 8], F32)
        tok1_sb = sb("tok1_sb", [128, 16], F32)
        mean_all = sb("mean_all", [128, 16], F32)
        var_all = sb("var_all", [128, 16], F32)
        sd_all = sb("sd_all", [128, 16], F32)
        rstd_all = sb("rstd_all", [128, 16], F32)
        logits_all = sb("logits_all", [128, 128], F32)
        nm_all = sb("nm_all", [128, 16], F32)
        pmax_all = sb("pmax_all", [128, 16], F32)
        se_all = sb("se_all", [128, 16], F32)
        gate_sb = sb("gate_sb", [128, 16], F32)
        gs_sb = sb("gs_sb", [128, 16], F32)
        slotu_sb = sb("slotu_sb", [128, 16], U32)
        slotf_sb = sb("slotf_sb", [128, 16], F32)
        eidx_sb = sb("eidx_sb", [128, 16], F32)
        keep_sb = sb("keep_sb", [128, 16], F32)
        posd_sb = sb("posd_sb", [128, 16], F32)
        oh_all = sb("oh_all", [128, 128], BF16)
        ohf_sb = sb("ohf_sb", [128, 8], F32)
        rec_sb = sb("rec_sb", [128, 1], F32)
        csh_sb = sb("csh_sb", [128, 1], F32)

        xt_f32 = [f32ar[:, 0:2048], f32ar[:, 2048:4096]]
        out_ev = [f32ar[:, 4096 + i * 512: 4096 + (i + 1) * 512] for i in range(2)]
        tmp1_f = f32ar[:, 5120:5632]
        tmp2_f = f32ar[:, 5632:6144]
        ohT_all = f32ar[0:8, 0:2048]
        logits_all2 = f32ar[:, 2048:2176]
        logits_all3 = f32ar[:, 2176:2304]
        pos_all = f32ar[:, 2304:2432]
        junk8 = f32ar[:, 2432:2440]
        junk8b = f32ar[:, 2440:2448]
        histco = f32ar[0:8, 2448:2464]
        pref = f32ar[0:8, 2464:2480]
        offm1 = f32ar[0:8, 2480:2496]
        offmF = f32ar[0:8, 2496:2512]
        offc = f32ar[0:8, 2512:2513]
        hf_sb = f32ar[0:8, 2520:2528]
        hm_sb = f32ar[0:8, 2528:2536]
        r_c1 = f32ar[:, 2536:2537]
        stat6 = f32ar[:, 4096:4120]
        stat2 = f32ar[:, 4120:4122]
        r_c2 = f32ar[:, 2544:2545]
        r_c3 = f32ar[:, 2552:2553]

        O_VAUG = 0
        VAUG_SZ = 128 * 258
        O_FM = VAUG_SZ
        O_ET = VAUG_SZ
        O_KH = O_ET + 16384
        O_QP = O_KH + 8192
        O_STG = O_QP + 1024
        O_HH = 0
        O_XD = 22528
        O_WB = O_XD + 16384
        O_W2 = 22528
        O_AFM = 0
        O_WOE = 32768

        def ar(o, n):
            return arena[:, o:o + n]

        xnb_ev = [ar(33024, 2048), ar(35072, 2048)]
        fmstage = ar(37120, 4096)
        lor_b = arena[0:16, 41216:43264]
        aot_b = arena[0:16, 43264:45312]
        at_b = arena[0:16, 45312:45568]
        raw_b = [ar(45568, 256), ar(45824, 256)]
        rawo_b = [ar(61696, 2048), ar(63744, 2048)]
        woe_b = [ar(65792, 2048), ar(67840, 2048)]
        O_WEFF = 49408

        def weff(w, dc):
            o = O_WEFF + ({"q": 0, "k": 1, "v": 2}[w] * NDC + dc) * 256
            return arena[:, o:o + 256]

        kq_ev = [ar(69888 + i * 512, 512) for i in range(4)]
        xdtm = [ar(47104, 2048), ar(49152, 2048)]
        xsh = [ar(51200, 2048), ar(53248, 2048)]
        accS = ar(57344, 2048)
        s1_ev = [ar(55296, 512), ar(55808, 512)]
        oe_ev = [ar(56320, 512), ar(56832, 512)]
        moebuf = [ar(65536, 2048), ar(67584, 2048)]
        shbuf = [ar(69632, 2048), ar(71680, 2048)]

        PA = [psumt(f"PA{i}", [128, 512], F32) for i in range(4)]
        PB = [psumt(f"PB{i}", [128, 512], F32) for i in range(2)]
        PT_full = [psumt(f"PT{i}", [128, 1024], BF16) for i in range(2)]
        PT = [t[:, 0:128] for t in PT_full]

        cc = S("cc")
        g0 = S("g0")
        f_ba = S("f_ba"); f_raw = [S("f_raw0"), S("f_raw1")]
        f_mm = S("f_mm"); f_cmb = S("f_cmb")
        f_wst = [S("f_wst0"), S("f_wst1")]
        a_x = [S("a_x0"), S("a_x1")]; a_fr = S("a_fr"); a_ms = S("a_ms")
        m_sq = S("m_sq"); m_rs = S("m_rs")
        b_x = [S("b_x0"), S("b_x1")]; b_fr = S("b_fr"); b_xn = S("b_xn")
        b_fmc = S("b_fmc"); b_tr = S("b_tr"); b_lg = S("b_lg"); b_le = S("b_le")
        b_st = [S("b_st0"), S("b_st1")]; b_sf = [S("b_sf0"), S("b_sf1")]
        r_nm = S("r_nm"); r_ex = S("r_ex"); r_vA = S("r_vA"); r_mmA = S("r_mmA")
        r_hs = S("r_hs"); r_hl = S("r_hl"); r_off = S("r_off")
        r_mmB = S("r_mmB"); r_vB = S("r_vB"); g_gs = S("g_gs")
        p3_fm = [S("p3_fm0"), S("p3_fm1")]
        p3_mmA = S("p3_mmA"); p3_mmV = S("p3_mmV")
        p3_ev = S("p3_ev"); p3_vv = S("p3_vv"); p3_vm = S("p3_vm")
        p3_ks = [S(f"p3_ks{i}") for i in range(4)]
        q_kh = [S("q_kh0"), S("q_kh1")]; q_qp = [S("q_qp0"), S("q_qp1")]
        q_mmS = S("q_mmS"); q_mmA = S("q_mmA"); q_ex = S("q_ex")
        q_rc = S("q_rc"); q_av = S("q_av"); q_st = S("q_st")
        z_gs = S("z_gs"); z_as = S("z_as"); z_ms = S("z_ms")
        x_sc = [S("x_sc0"), S("x_sc1")]
        m_sl = [S("m_sl0"), S("m_sl1")]; m_mg = S("m_mg")
        n_ga = [S("n_ga0"), S("n_ga1")]; n_gx = S("n_gx")
        n_tr = S("n_tr"); n_tc = S("n_tc")
        n_w1 = [S("n_w10"), S("n_w11")]; n_hm = S("n_hm"); n_si = S("n_si")
        n_ht = S("n_ht"); n_w2 = S("n_w2"); n_m2 = S("n_m2"); n_oe = S("n_oe")
        dx_s = S("dx_s")
        n_os = [S("n_os0"), S("n_os1")]
        o_af = S("o_af"); o_wo = S("o_wo")
        o_x = [S("o_x0"), S("o_x1")]; o_sh = [S("o_sh0"), S("o_sh1")]
        o_mg = [S("o_mg0"), S("o_mg1")]
        o_mm = S("o_mm"); o_ev = S("o_ev"); o_st = [S("o_st0"), S("o_st1")]
        o_fr5 = S("o_fr5"); o_fr6 = S("o_fr6"); d_st = S("d_st")
        vch = S("vch")

        def VE(v, instr):
            vch.inc(instr)
            v.wait_ge(vch.h, vch.n)

        M = {}

        with nc.Block() as blk:
            @blk.gpsimd
            def _(g):
                for dst, src in [(ident_sb[:], ident), (id32_sb[:], ident32),
                                 (tri_sb[:], TRI), (iota8_sb[:], iota8),
                                 (iota1k_sb[:], iota1k), (ones_sb[:], onesrow),
                                 (ones816_sb[:], ones816),
                                 (bqc_sb[:], bq_c), (bkc_sb[:], bk_c),
                                 (bvr_sb[:], bv_r), (bor_sb[:], bo_r),
                                 (maskp_sb[:], maskp), (tok1_sb[:], tok1)]:
                    g0.inc(g.dma_start(out=dst, in_=src), 16)
                g0.inc(g.dma_start(
                    out=wg_sb[:].rearrange("p (dc e) -> p dc e", dc=NDC),
                    in_=Wg.rearrange("(dc p) e -> p dc e", p=128)), 16)
                g.memset(csh_sb[:], -CSHIFT)
                z_ms.inc(g.memset(ar(0, 2048), 0.0))
                g.wait_ge(z_ms.h, 1)
                for r in range(64):
                    g_gs.inc(g.dma_start(out=xs_c[r * 128:(r + 1) * 128, :],
                                         in_=ar(0, 2048)), 16)
        M["consts"] = g0.n
        M["xs_zero_g"] = g_gs.n

        with nc.Block() as blk:
            @blk.scalar
            def _(sc):
                sc.wait_ge(z_ms.h, 1)
                for r in range(64, 128):
                    z_as.inc(sc.dma_start(out=xs_c[r * 128:(r + 1) * 128, :],
                                          in_=ar(0, 2048)), 16)
        M["xs_zero_a"] = z_as.n

        with nc.Block() as blk:
            @blk.sync
            def _(s):
                for i in range(16):
                    if i >= 2:
                        s.wait_ge(a_fr.h, i - 1)
                    a_x[i % 2].inc(s.dma_start(out=xt_f32[i % 2],
                                               in_=x_my[i * 128:(i + 1) * 128, :]), 16)

            @blk.vector
            def _(v):
                for i in range(16):
                    v.wait_ge(a_x[i % 2].h, 16 * (i // 2 + 1))
                    for q4 in range(4):
                        ii = v.bn_stats(stat6[:, 6 * q4:6 * q4 + 6],
                                        xt_f32[i % 2][:, 512 * q4:512 * (q4 + 1)])
                    a_ms.inc(ii)
                    v.wait_ge(a_ms.h, a_ms.n)
                    a_ms.inc(v.bn_aggr(stat2[:], stat6[:]))
                    v.wait_ge(a_ms.h, a_ms.n)
                    v.tensor_copy(mean_all[:, i:i + 1], stat2[:, 0:1])
                    ts = v.tensor_scalar(var_all[:, i:i + 1], stat2[:, 1:2], EPS, None,
                                         op0=ALU.add)
                    if i == 15:
                        a_ms.inc(ts)
                    a_fr.inc(v.nop())
        M["stats"] = a_ms.n

        with nc.Block() as blk:
            @blk.scalar
            def _(sc):
                sc.wait_ge(a_ms.h, M["stats"])
                m_sq.inc(sc.activation(sd_all[:], var_all[:], AF.Sqrt))

            @blk.vector
            def _(v):
                v.wait_ge(m_sq.h, 1)
                m_rs.inc(v.reciprocal(rstd_all[:], sd_all[:]))
                v.wait_ge(m_rs.h, 1)
        M["rstd"] = m_rs.n

        for wi, (wname, WT_d, Bm, Am) in enumerate(
                [("q", WqT, Bq, AqT), ("k", WkT, Bk, AkT), ("v", WvT, Bv, AvT)]):
            bR = [f_raw[0].n, f_raw[1].n]
            bC = f_cmb.n
            bM = f_mm.n
            with nc.Block() as blk:
                @blk.gpsimd
                def _(g, Bm=Bm, Am=Am, WT_d=WT_d, bC=bC):
                    g.wait_ge(f_cmb.h, bC)
                    f_ba.inc(g.dma_start(out=lor_b[:, 0:D], in_=Bm), 16)
                    f_ba.inc(g.dma_start(out=at_b[:], in_=Am), 16)
                    for dc in range(NDC):
                        if dc >= 2:
                            g.wait_ge(f_cmb.h, bC + dc - 1)
                        f_raw[dc % 2].inc(g.dma_start(
                            out=raw_b[dc % 2][:], in_=WT_d[dc * 128:(dc + 1) * 128, :]), 16)
                M[f"ba_{wname}"] = f_ba.n

                @blk.tensor
                def _(t, wname=wname, bC=bC):
                    t.wait_ge(f_ba.h, M[f"ba_{wname}"])
                    for dc in range(NDC):
                        if dc >= 2:
                            t.wait_ge(f_cmb.h, bC + dc - 1)
                        f_mm.inc(t.matmul(PB[dc % 2][:, 0:256],
                                          lor_b[:, dc * 128:(dc + 1) * 128],
                                          at_b[:], start=True, stop=True))

                @blk.vector
                def _(v, wname=wname, bM=bM, bR=bR):
                    for dc in range(NDC):
                        v.wait_ge(f_mm.h, bM + dc + 1)
                        v.wait_ge(f_raw[dc % 2].h, bR[dc % 2] + 16 * (dc // 2 + 1))
                        f_cmb.inc(v.scalar_tensor_tensor(
                            weff(wname, dc), PB[dc % 2][:, 0:256], LORA_S,
                            raw_b[dc % 2][:], op0=ALU.mult, op1=ALU.add))
            M[f"fold_{wname}"] = f_cmb.n

        bR = [f_raw[0].n, f_raw[1].n]
        bC = f_cmb.n
        bM = f_mm.n
        bW = [f_wst[0].n, f_wst[1].n]
        with nc.Block() as blk:
            @blk.sync
            def _(s):
                for dc in range(NDC):
                    s.wait_ge(f_cmb.h, bC + 4 * (dc + 1))
                    f_wst[dc % 2].inc(s.dma_start(
                        out=WoTe[dc * 128:(dc + 1) * 128, :], in_=woe_b[dc % 2][:]), 16)

            @blk.gpsimd
            def _(g):
                g.wait_ge(f_cmb.h, bC)
                f_ba.inc(g.dma_start(out=lor_b[:, 0:D], in_=Bo), 16)
                f_ba.inc(g.dma_start(out=aot_b[:], in_=AoT), 16)
                M["ba_o"] = f_ba.n
                for dc in range(NDC):
                    if dc >= 2:
                        g.wait_ge(f_cmb.h, bC + 4 * (dc - 1))
                    f_raw[dc % 2].inc(g.dma_start(
                        out=rawo_b[dc % 2][:], in_=WoT[dc * 128:(dc + 1) * 128, :]), 16)

            @blk.tensor
            def _(t):
                t.wait_ge(f_ba.h, M["ba_o"])
                for dc in range(NDC):
                    for ot in range(4):
                        idx = dc * 4 + ot
                        if idx >= 2:
                            t.wait_ge(f_cmb.h, bC + idx - 1)
                        f_mm.inc(t.matmul(PB[idx % 2][:, 0:512],
                                          lor_b[:, dc * 128:(dc + 1) * 128],
                                          aot_b[:, ot * 512:(ot + 1) * 512],
                                          start=True, stop=True))

            @blk.vector
            def _(v):
                for dc in range(NDC):
                    if dc >= 2:
                        v.wait_ge(f_wst[dc % 2].h, bW[dc % 2] + 16 * (dc // 2 - 0))
                    for ot in range(4):
                        idx = dc * 4 + ot
                        v.wait_ge(f_mm.h, bM + idx + 1)
                        v.wait_ge(f_raw[dc % 2].h, bR[dc % 2] + 16 * (dc // 2 + 1))
                        f_cmb.inc(v.scalar_tensor_tensor(
                            woe_b[dc % 2][:, ot * 512:(ot + 1) * 512],
                            PB[idx % 2][:, 0:512], LORA_S,
                            rawo_b[dc % 2][:, ot * 512:(ot + 1) * 512],
                            op0=ALU.mult, op1=ALU.add))
        M["fold_o"] = f_cmb.n
        M["woTe_w0"] = f_wst[0].n
        M["woTe_w1"] = f_wst[1].n

        with nc.Block() as blk:
            @blk.sync
            def _(s):
                for i in range(16):
                    if i >= 2:
                        s.wait_ge(b_fr.h, i - 1)
                    b_x[i % 2].inc(s.dma_start(out=xt_f32[i % 2],
                                               in_=x_my[i * 128:(i + 1) * 128, :]), 16)
                    if "tm" not in KCUT:
                        s.wait_ge(b_xn.h, i + 1)
                        b_st[i % 2].inc(s.dma_start(out=xn_tm_c[i * 128:(i + 1) * 128, :],
                                                    in_=xnb_ev[i % 2][:]), 16)
                    if i % 2 == 1 and "fm" not in KCUT:
                        s.wait_ge(b_fmc.h, 16 * (i + 1))
                        b_sf[(i // 2) % 2].inc(s.dma_start(
                            out=xn_fm_c.rearrange("(dc p) t -> p dc t", p=128)
                                [:, :, (i - 1) * 128:(i + 1) * 128],
                            in_=fmstage[:].rearrange("p (dc t) -> p dc t", dc=NDC)), 16)

            @blk.vector
            def _(v):
                v.wait_ge(m_rs.h, M["rstd"])
                for i in range(16):
                    v.wait_ge(b_x[i % 2].h, 16 * (i // 2 + 1))
                    if i >= 2:
                        if "tm" not in KCUT:
                            v.wait_ge(b_st[i % 2].h, 16 * ((i - 2) // 2 + 1))
                        if "tr" not in KCUT:
                            v.wait_ge(b_tr.h, 16 * (i - 1))
                        P = (i - 2) // 2
                        if "fm" not in KCUT:
                            v.wait_ge(b_sf[0].h, 16 * (P // 2 + 1))
                            if P >= 1:
                                v.wait_ge(b_sf[1].h, 16 * ((P - 1) // 2 + 1))
                        if "lg" not in KCUT:
                            v.wait_ge(b_lg.h, i - 1)
                    b_xn.inc(v.tensor_scalar(xnb_ev[i % 2][:], xt_f32[i % 2],
                                             mean_all[:, i:i + 1], rstd_all[:, i:i + 1],
                                             op0=ALU.subtract, op1=ALU.mult))
                    b_fr.inc(v.nop())
                    if "tr" not in KCUT:
                        for dc in range(NDC):
                            g2 = i * 16 + dc
                            v.wait_ge(b_tr.h, g2 + 1)
                            b_fmc.inc(v.tensor_copy(
                                fmstage[:, dc * 256 + (i % 2) * 128:
                                        dc * 256 + (i % 2) * 128 + 128],
                                PT[dc % 2][:]))
                    if "lg" not in KCUT and "le" not in KCUT and "lev" not in KCUT:
                        v.wait_ge(b_lg.h, i + 1)
                        b_le.inc(v.tensor_copy(logits_all[:, 8 * i:8 * i + 8],
                                               PA[2 + i % 2][:, 0:8]))

            @blk.tensor
            def _(t):
                t.wait_ge(g0.h, M["consts"])
                for i in range(16):
                    t.wait_ge(b_xn.h, i + 1)
                    if "tr" not in KCUT:
                        for dc in range(NDC):
                            g2 = i * 16 + dc
                            if g2 >= 2:
                                t.wait_ge(b_fmc.h, g2 - 1)
                            b_tr.inc(t.transpose(PT[dc % 2][:],
                                                 xnb_ev[i % 2][:, dc * 128:(dc + 1) * 128],
                                                 ident_sb[:]))
                    if "lg" not in KCUT:
                        t.wait_ge(b_fmc.h, 16 * (i + 1))
                        if i >= 2 and "le" not in KCUT:
                            t.wait_ge(b_le.h, i - 1)
                        for dc in range(NDC):
                            mmi = t.matmul(PA[2 + i % 2][:, 0:8],
                                           fmstage[:, dc * 256 + (i % 2) * 128:
                                                   dc * 256 + (i % 2) * 128 + 128],
                                           wg_sb[:, dc * 8:(dc + 1) * 8],
                                           start=(dc == 0), stop=(dc == NDC - 1))
                        b_lg.inc(mmi)

            @blk.scalar
            def _(sc):
                if "lg" not in KCUT and "le" not in KCUT and "lev" in KCUT:
                    for i in range(16):
                        sc.wait_ge(b_lg.h, i + 1)
                        b_le.inc(sc.activation(logits_all[:, 8 * i:8 * i + 8],
                                               PA[2 + i % 2][:, 0:8], AF.Copy))

        with nc.Block() as blk:
            @blk.gpsimd
            def _(g):
                g.wait_ge(b_st[0].h, b_st[0].n)
                g.wait_ge(b_st[1].h, b_st[1].n)
                g.wait_ge(b_sf[0].h, b_sf[0].n)
                g.wait_ge(b_sf[1].h, b_sf[1].n)
                cc.inc(g.collective_compute(
                    "AllGather", ALU.bypass, replica_groups=[list(range(8))],
                    ins=[xn_fm_c], outs=[xn_fm_full]))
        M["cc_ag1"] = cc.n

        if DP >= 2:
            with nc.Block() as blk:
                @blk.vector
                def _(v):
                    v.wait_ge(b_le.h, 16)
                    for i in range(16):
                        last = v.reduce_max(nm_all[:, i:i + 1],
                                            logits_all[:, 8 * i:8 * i + 8],
                                            axis=AX.X, negate=True)
                    r_nm.inc(last)
                    v.wait_ge(r_nm.h, 1)
                    r_nm.inc(v.tensor_scalar(pmax_all[:], nm_all[:], -1.0, None,
                                             op0=ALU.mult))

                @blk.scalar
                def _(sc):
                    sc.wait_ge(r_nm.h, 2)
                    for i in range(16):
                        r_ex.inc(sc.activation(junk8[:], logits_all[:, 8 * i:8 * i + 8],
                                               AF.Exp, bias=nm_all[:, i:i + 1],
                                               accum_out=se_all[:, i:i + 1]))
            M["exp"] = r_ex.n

            with nc.Block() as blk:
                @blk.vector
                def _(v):
                    v.wait_ge(r_ex.h, M["exp"])
                    v.reciprocal(gate_sb[:], se_all[:])
                    for i in range(16):
                        if i >= 1:
                            v.wait_ge(r_mmA.h, 3 * (i - 1) + 3)
                        VE(v, v.tensor_scalar(junk8[:], logits_all[:, 8 * i:8 * i + 8],
                                              pmax_all[:, i:i + 1], None, op0=ALU.is_equal))
                        r_vA.inc(v.scalar_tensor_tensor(junk8b[:], junk8[:], -1000.0,
                                                        iota1k_sb[:],
                                                        op0=ALU.mult, op1=ALU.add))
                        v.wait_ge(r_vA.h, r_vA.n)
                        r_vA.inc(v.tensor_reduce(eidx_sb[:, i:i + 1], junk8b[:],
                                                 op=ALU.min, axis=AX.X))
                        v.wait_ge(r_vA.h, 5 * i + 2)
                        if DP == 2:
                            v.tensor_copy(logits_all2[:, 8 * i:8 * i + 8], junk8[:])
                            v.tensor_copy(logits_all3[:, 8 * i:8 * i + 8], junk8b[:])
                        VE(v, v.tensor_scalar(oh_all[:, 8 * i:8 * i + 8], iota8_sb[:],
                                              eidx_sb[:, i:i + 1], None, op0=ALU.is_equal))
                        r_vA.inc(v.tensor_copy(ohf_sb[:], oh_all[:, 8 * i:8 * i + 8]))
                        v.wait_ge(r_mmA.h, 3 * i + 1)
                        r_vA.inc(v.tensor_copy(pos_all[:, 8 * i:8 * i + 8],
                                               PA[i % 2][:, 0:8]))
                        v.wait_ge(r_mmA.h, 3 * i + 3)
                        v.tensor_copy(histco[:, i:i + 1], PA[2 + i % 2][0:8, 127:128])
                        r_vA.inc(v.tensor_copy(ohT_all[:, 128 * i:128 * (i + 1)],
                                               PB[i % 2][0:8, 0:128]))
                    r_vA.inc(v.nop())
                    v.wait_ge(r_vA.h, r_vA.n)
                    r_vA.inc(v.tensor_tensor_scan(pref[:], ones816_sb[:], histco[:],
                                                  0.0, op0=ALU.mult, op1=ALU.add))
                    v.wait_ge(r_vA.h, r_vA.n)

                @blk.tensor
                def _(t):
                    for i in range(16):
                        t.wait_ge(r_vA.h, 5 * i + 3)
                        if i >= 2:
                            t.wait_ge(r_vA.h, 5 * (i - 2) + 5)
                        r_mmA.inc(t.matmul(PA[i % 2][:, 0:8], tri_sb[:],
                                           oh_all[:, 8 * i:8 * i + 8],
                                           start=True, stop=True))
                        t.wait_ge(r_vA.h, 5 * i + 4)
                        r_mmA.inc(t.transpose(PA[2 + i % 2][0:8, 0:128],
                                              pos_all[:, 8 * i:8 * i + 8], id32_sb[:]))
                        r_mmA.inc(t.transpose(PB[i % 2][0:8, 0:128], ohf_sb[:],
                                              id32_sb[:]))
            M["p2a"] = r_vA.n

            with nc.Block() as blk:
                @blk.sync
                def _(s):
                    s.wait_ge(r_vA.h, M["p2a"])
                    r_hs.inc(s.dma_start(out=hist_c, in_=pref[:, 15:16]), 16)

                @blk.gpsimd
                def _(g):
                    g.wait_ge(r_hs.h, 16)
                    cc.inc(g.collective_compute(
                        "AllGather", ALU.bypass, replica_groups=[list(range(8))],
                        ins=[hist_c], outs=[hist_full]))
            M["cc_ag2"] = cc.n

            with nc.Block() as blk:
                @blk.sync
                def _(s):
                    s.wait_ge(cc.h, M["cc_ag2"])
                    with nc.allow_non_contiguous_dma("tiny 64-elem strided load"):
                        r_hl.inc(s.dma_start(
                            out=hf_sb[:].rearrange("e (j one) -> e j one", one=1),
                            in_=hist_full.rearrange("(j e) one -> e j one", e=8)), 16)

                @blk.vector
                def _(v):
                    v.wait_ge(r_hl.h, 16)
                    r_off.inc(v.tensor_tensor(hm_sb[:], hf_sb[:], maskp_sb[:], op=ALU.mult))
                    v.wait_ge(r_off.h, 1)
                    r_off.inc(v.tensor_reduce(offc[:], hm_sb[:], op=ALU.add, axis=AX.X))
                    v.wait_ge(r_off.h, 2)
                    VE(v, v.tensor_scalar(offm1[:], pref[:], offc[:], None, op0=ALU.add))
                    r_off.inc(v.tensor_tensor(offmF[:], offm1[:], histco[:],
                                              op=ALU.subtract))
            M["offm"] = r_off.n

            with nc.Block() as blk:
                @blk.tensor
                def _(t):
                    t.wait_ge(r_off.h, M["offm"])
                    for i in range(16):
                        if i >= 2:
                            t.wait_ge(r_vB.h, 3 * i - 3)
                        r_mmB.inc(t.matmul(PB[i % 2][:, 0:1],
                                           ohT_all[:, 128 * i:128 * (i + 1)],
                                           offmF[:, i:i + 1], start=True, stop=True))

                @blk.vector
                def _(v):
                    for i in range(16):
                        v.wait_ge(r_mmB.h, i + 1)
                        r_vB.inc(v.tensor_tensor(junk8[:], oh_all[:, 8 * i:8 * i + 8],
                                                 pos_all[:, 8 * i:8 * i + 8], op=ALU.mult))
                        v.wait_ge(r_vB.h, r_vB.n)
                        r_vB.inc(v.tensor_reduce(r_c1[:], junk8[:], op=ALU.add, axis=AX.X))
                        v.wait_ge(r_vB.h, r_vB.n)
                        VE(v, v.tensor_tensor(r_c2[:], r_c1[:], PB[i % 2][:, 0:1],
                                              op=ALU.add))
                        VE(v, v.tensor_scalar(posd_sb[:, i:i + 1], r_c2[:], -1.0, None,
                                              op0=ALU.add))
                        VE(v, v.tensor_scalar(keep_sb[:, i:i + 1], posd_sb[:, i:i + 1],
                                              float(CAP), None, op0=ALU.is_lt))
                        VE(v, v.tensor_scalar(r_c1[:], eidx_sb[:, i:i + 1], float(CAP),
                                              None, op0=ALU.mult))
                        VE(v, v.tensor_tensor(slotf_sb[:, i:i + 1], r_c1[:],
                                              posd_sb[:, i:i + 1], op=ALU.add))
                        VE(v, v.tensor_scalar(r_c2[:], keep_sb[:, i:i + 1], -1.0, 1.0,
                                              op0=ALU.mult, op1=ALU.add))
                        VE(v, v.tensor_tensor(r_c1[:], slotf_sb[:, i:i + 1],
                                              keep_sb[:, i:i + 1], op=ALU.mult))
                        VE(v, v.scalar_tensor_tensor(r_c3[:], r_c2[:], SENT,
                                                     r_c1[:], op0=ALU.mult, op1=ALU.add))
                        VE(v, v.tensor_copy(slotu_sb[:, i:i + 1], r_c3[:]))
                        r_vB.inc(v.tensor_tensor(gs_sb[:, i:i + 1], gate_sb[:, i:i + 1],
                                                 keep_sb[:, i:i + 1], op=ALU.mult))
            M["route"] = r_vB.n

            bBX = [b_x[0].n, b_x[1].n]
            bXS = [x_sc[0].n, x_sc[1].n]
            with nc.Block() as blk:
                @blk.sync
                def _(s):
                    s.wait_ge(b_st[0].h, b_st[0].n)
                    s.wait_ge(b_st[1].h, b_st[1].n)
                    for i in range(16):
                        if i >= 2:
                            s.wait_ge(x_sc[i % 2].h, bXS[i % 2] + 16 * ((i - 2) // 2 + 1))
                        b_x[i % 2].inc(s.dma_start(
                            out=xnb_ev[i % 2],
                            in_=xn_tm_c[i * 128:(i + 1) * 128, :]), 16)

                @blk.gpsimd
                def _(g):
                    g.wait_ge(r_vB.h, M["route"])
                    g.wait_ge(g_gs.h, M["xs_zero_g"])
                    g.wait_ge(z_as.h, M["xs_zero_a"])
                    for i in range(16):
                        g.wait_ge(b_x[i % 2].h, bBX[i % 2] + 16 * (i // 2 + 1))
                        x_sc[i % 2].inc(g.indirect_dma_start(
                            out=xs_c,
                            out_offset=bass.IndirectOffsetOnAxis(slotu_sb[:, i:i + 1], 0),
                            in_=xnb_ev[i % 2], in_offset=None,
                            bounds_check=TTOK - 1, oob_is_err=False), 16)

            def emit_a2a3():
                with nc.Block() as blk:
                    @blk.gpsimd
                    def _(g):
                        g.wait_ge(x_sc[0].h, x_sc[0].n)
                        g.wait_ge(x_sc[1].h, x_sc[1].n)
                        with allow_ici_all_to_all():
                            cc.inc(g.collective_compute(
                                "AllToAll", ALU.bypass, replica_groups=[list(range(8))],
                                ins=[xs_c], outs=[xs_all]))
                M["cc_a2a3"] = cc.n
            if DP == 2:
                emit_a2a3()

        if DP >= 3:
            def fmb(par):
                return O_FM + par * 8192

            with nc.Block() as blk:
                @blk.vector
                def _(v):
                    p3_vm.inc(v.memset(ar(O_VAUG, VAUG_SZ), 1.0))

                @blk.sync
                def _(s):
                    s.wait_ge(cc.h, M["cc_ag1"])
                    s.wait_ge(f_cmb.h, M["fold_o"])
                    for tt in range(32):
                        r, tl = tt // 4, tt % 4
                        if tt >= 2:
                            s.wait_ge(p3_mmA.h, 4 * (tt - 1))
                            s.wait_ge(p3_mmV.h, 4 * (tt - 1))
                        for dcg in range(4):
                            src = xn_fm_full[r * 2048 + dcg * 512: r * 2048 + (dcg + 1) * 512,
                                             tl * 512:(tl + 1) * 512]
                            p3_fm[tt % 2].inc(s.dma_start(
                                out=ar(fmb(tt % 2) + dcg * 2048, 2048).rearrange(
                                    "p (dc t) -> p dc t", dc=4),
                                in_=src.rearrange("(dc p) t -> p dc t", p=128)), 16)
                        for g in range(4):
                            pidx = (tt - 1) * 4 + g
                            if pidx >= 0:
                                s.wait_ge(p3_ev.h, pidx + 1)
                                pg = pidx % 4
                                dst = Kfm if pg < 2 else Qfm
                                hl = pg % 2
                                ptt = pidx // 4
                                p3_ks[pidx % 4].inc(s.dma_start(
                                    out=dst[hl * 128:(hl + 1) * 128,
                                            ptt * 512:(ptt + 1) * 512],
                                    in_=kq_ev[pidx % 4][:]), 16)
                    for g in range(4):
                        pidx = 31 * 4 + g
                        s.wait_ge(p3_ev.h, pidx + 1)
                        pg = pidx % 4
                        dst = Kfm if pg < 2 else Qfm
                        hl = pg % 2
                        p3_ks[pidx % 4].inc(s.dma_start(
                            out=dst[hl * 128:(hl + 1) * 128, 31 * 512:32 * 512],
                            in_=kq_ev[pidx % 4][:]), 16)

                @blk.tensor
                def _(t):
                    t.wait_ge(f_cmb.h, M["fold_o"])
                    t.wait_ge(p3_vm.h, 1)
                    for tt in range(32):
                        t.wait_ge(p3_fm[tt % 2].h, 64 * (tt // 2 + 1))
                        for g in range(4):
                            idx = tt * 4 + g
                            w = "k" if g < 2 else "q"
                            hl = g % 2
                            if idx >= 2:
                                t.wait_ge(p3_ev.h, idx - 1)
                            for dc in range(NDC):
                                mmi = t.matmul(PA[g % 2][:, 0:512],
                                               weff(w, dc)[:, hl * 128:(hl + 1) * 128],
                                               ar(fmb(tt % 2) + dc * 512, 512),
                                               start=(dc == 0), stop=(dc == NDC - 1))
                            p3_mmA.inc(mmi)
                        for tc in range(4):
                            idx = tt * 4 + tc
                            if idx >= 2:
                                t.wait_ge(p3_vv.h, idx - 1)
                            for dc in range(NDC):
                                t.matmul(PB[tc % 2][:, 0:256],
                                         ar(fmb(tt % 2) + dc * 512 + tc * 128, 128),
                                         weff("v", dc), start=(dc == 0), stop=False)
                            p3_mmV.inc(t.matmul(PB[tc % 2][:, 0:256], ones_sb[:],
                                                bvr_sb[:], start=False, stop=True))

                @blk.vector
                def _(v):
                    for tt in range(32):
                        for g in range(4):
                            idx = tt * 4 + g
                            bcol = (bkc_sb if g < 2 else bqc_sb)[:, g % 2:g % 2 + 1]
                            v.wait_ge(p3_mmA.h, idx + 1)
                            if idx >= 4:
                                v.wait_ge(p3_ks[idx % 4].h, 16 * (idx // 4))
                            p3_ev.inc(v.tensor_scalar(kq_ev[idx % 4][:], PA[g % 2][:],
                                                      bcol, None, op0=ALU.add))
                        for tc in range(4):
                            idx = tt * 4 + tc
                            v.wait_ge(p3_mmV.h, idx + 1)
                            off = O_VAUG + idx * 258
                            VE(v, v.tensor_copy(ar(off, 128), PB[tc % 2][:, 0:128]))
                            p3_vv.inc(v.tensor_copy(ar(off + 129, 128),
                                                    PB[tc % 2][:, 128:256]))
            M["p3_mmA"] = p3_mmA.n
            M["p3_ks"] = [p3_ks[i].n for i in range(4)]
            if DP == 3:
                emit_a2a3()

        if DP >= 4:
            def kh(par):
                return O_KH + par * 4096

            def qp(par):
                return O_QP + par * 512

            def et(par):
                return O_ET if par % 2 == 0 else 73648

            kh1_thr = q_kh[1].n + 16
            with nc.Block() as blk:
                @blk.gpsimd
                def _(g, kh1_thr=kh1_thr):
                    g.wait_ge(q_kh[1].h, kh1_thr)
                    g.wait_ge(x_sc[0].h, x_sc[0].n)
                    g.wait_ge(x_sc[1].h, x_sc[1].n)
                    with allow_ici_all_to_all():
                        cc.inc(g.collective_compute(
                            "AllToAll", ALU.bypass, replica_groups=[list(range(8))],
                            ins=[xs_c], outs=[xs_all]))
                    for b2 in range(4):
                        g.wait_ge(q_av.h, 64 * (b2 + 1))
                        for half in range(2):
                            q_st.inc(g.dma_start(
                                out=att_c[b2 * 4096 + half * 2048:
                                          b2 * 4096 + (half + 1) * 2048, :]
                                    .rearrange("(q p) f -> p q f", p=128),
                                in_=ar(O_STG + half * 4096, 4096).rearrange(
                                    "p (q f) -> p q f", f=256)), 16)

                @blk.sync
                def _(s):
                    for i4 in range(4):
                        s.wait_ge(p3_ks[i4].h, M["p3_ks"][i4])
                    for ph in range(8):
                        b, hl = ph // 2, ph % 2
                        if ph >= 2:
                            s.wait_ge(q_mmS.h, 256 * (ph - 1))
                        q_kh[ph % 2].inc(s.dma_start(
                            out=ar(kh(ph % 2), 4096),
                            in_=Kfm[hl * 128:(hl + 1) * 128, b * 4096:(b + 1) * 4096]), 16)
                        for qb in range(8):
                            gqb = ph * 8 + qb
                            if gqb >= 2:
                                s.wait_ge(q_mmS.h, 32 * (gqb - 1))
                            q_qp[gqb % 2].inc(s.dma_start(
                                out=ar(qp(gqb % 2), 512),
                                in_=Qfm[hl * 128:(hl + 1) * 128,
                                        b * 4096 + qb * 512: b * 4096 + (qb + 1) * 512]), 16)

                @blk.tensor
                def _(t):
                    def avmm(g):
                        phg = g // 8
                        bg, hlg = phg // 2, phg % 2
                        t.wait_ge(q_ex.h, 32 * (g + 1))
                        for qs in range(4):
                            gqs = g * 4 + qs
                            if gqs >= 2:
                                t.wait_ge(q_av.h, gqs - 1)
                            for kc in range(32):
                                gtc = bg * 32 + kc
                                mmi = t.matmul(PB[qs % 2][:, 0:129],
                                               ar(et(g % 2) + kc * 512 + qs * 128, 128),
                                               ar(O_VAUG + gtc * 258 + hlg * 129, 129),
                                               start=(kc == 0), stop=(kc == 31))
                            q_mmA.inc(mmi)

                    for ph in range(8):
                        b, hl = ph // 2, ph % 2
                        t.wait_ge(q_kh[ph % 2].h, 16 * (ph // 2 + 1))
                        for qb in range(8):
                            gqb = ph * 8 + qb
                            t.wait_ge(q_qp[gqb % 2].h, 16 * (gqb // 2 + 1))
                            for kc in range(32):
                                gsc = gqb * 32 + kc
                                if gsc >= 4:
                                    t.wait_ge(q_ex.h, gsc - 3)
                                q_mmS.inc(t.matmul(PA[kc % 4][:],
                                                   ar(kh(ph % 2) + kc * 128, 128),
                                                   ar(qp(gqb % 2), 512),
                                                   start=True, stop=True))
                            if gqb >= 1:
                                avmm(gqb - 1)
                    avmm(63)

                @blk.scalar
                def _(sc):
                    for ph in range(8):
                        for qb in range(8):
                            gqb = ph * 8 + qb
                            for kc in range(32):
                                gsc = gqb * 32 + kc
                                sc.wait_ge(q_mmS.h, gsc + 1)
                                if kc == 0 and gqb >= 2:
                                    sc.wait_ge(q_mmA.h, 4 * (gqb - 1))
                                q_ex.inc(sc.activation(
                                    ar(et(gqb % 2) + kc * 512, 512), PA[kc % 4][:],
                                    AF.Exp, bias=csh_sb[:], scale=SCALE))

                @blk.vector
                def _(v):
                    for ph in range(8):
                        b, hl = ph // 2, ph % 2
                        for qb in range(8):
                            for qs in range(4):
                                gqs = (ph * 8 + qb) * 4 + qs
                                qspp = qb * 4 + qs
                                if hl == 0 and qspp == 0 and b >= 1:
                                    v.wait_ge(q_st.h, 32 * b)
                                v.wait_ge(q_mmA.h, gqs + 1)
                                q_rc.inc(v.reciprocal(rec_sb[:],
                                                      PB[qs % 2][:, 128:129]))
                                v.wait_ge(q_rc.h, q_rc.n)
                                q_av.inc(v.tensor_scalar(
                                    ar(O_STG + qspp * 256 + hl * 128, 128),
                                    PB[qs % 2][:, 0:128], rec_sb[:], None,
                                    op0=ALU.mult))
            M["p4_av"] = q_mmA.n
            M["p4_st"] = q_st.n
            M["cc_a2a3"] = cc.n

            with nc.Block() as blk:
                @blk.gpsimd
                def _(g):
                    g.wait_ge(q_st.h, M["p4_st"])
                    with allow_ici_all_to_all():
                        cc.inc(g.collective_compute(
                            "AllToAll", ALU.bypass, replica_groups=[list(range(8))],
                            ins=[att_c], outs=[att_all]))
            M["cc_a2a4"] = cc.n

        if DP >= 5:
            def emit_ffn(tag, w1src, w3src, w2src, out_dram, moe, prev_gate):
                for half in range(2):
                    gate = prev_gate if half == 0 else M[f"{tag}_w2g_{half - 1}"]
                    bTR = n_tr.n
                    bTC = n_tc.n
                    bGA = [n_ga[0].n, n_ga[1].n]
                    bGX = n_gx.n
                    if moe:
                        bSL = [m_sl[0].n, m_sl[1].n]
                        bMG = m_mg.n
                        with nc.Block() as blk:
                            @blk.sync
                            def _(s, half=half, gate=gate, bSL=bSL, bMG=bMG):
                                s.wait_ge(cc.h, M["cc_a2a3"])
                                s.wait_ge(gate[0].h, gate[1])
                                for st_ in range(8):
                                    for j in range(8):
                                        gidx = st_ * 8 + j
                                        if gidx >= 2:
                                            s.wait_ge(m_mg.h, bMG + gidx - 1)
                                        m_sl[j % 2].inc(s.dma_start(
                                            out=xsh[j % 2],
                                            in_=xs_all[j * 2048 + half * 1024 + st_ * 128:
                                                       j * 2048 + half * 1024 + (st_ + 1) * 128,
                                                       :]), 16)

                            @blk.vector
                            def _(v, half=half, gate=gate, bSL=bSL, bTR=bTR):
                                v.wait_ge(gate[0].h, gate[1])

                                def copies(st2):
                                    for dc in range(NDC):
                                        g2 = st2 * 16 + dc
                                        v.wait_ge(n_tr.h, bTR + g2 + 1)
                                        n_tc.inc(v.tensor_copy(
                                            ar(O_XD + dc * 1024 + st2 * 128, 128),
                                            PT[dc % 2][:]))

                                for st_ in range(8):
                                    for j in range(8):
                                        v.wait_ge(m_sl[j % 2].h,
                                                  bSL[j % 2] + 16 * (st_ * 4 + j // 2 + 1))
                                        if j == 0:
                                            if st_ >= 2:
                                                v.wait_ge(n_tr.h, bTR + 16 * (st_ - 1))
                                            ii5 = v.tensor_copy(accS, xsh[0])
                                        else:
                                            dst = xdtm[st_ % 2] if j % 2 == 1 else accS
                                            srcacc = accS if j % 2 == 1 else xdtm[st_ % 2]
                                            ii5 = v.tensor_tensor(dst, srcacc,
                                                                  xsh[j % 2], op=ALU.add)
                                        m_mg.inc(ii5)
                                        v.wait_ge(m_mg.h, m_mg.n)
                                    if st_ >= 1:
                                        copies(st_ - 1)
                                copies(7)

                            @blk.tensor
                            def _(t, gate=gate, bMG=bMG, bTC=bTC):
                                t.wait_ge(gate[0].h, gate[1])
                                for st_ in range(8):
                                    t.wait_ge(m_mg.h, bMG + 8 * (st_ + 1))
                                    for dc in range(NDC):
                                        g2 = st_ * 16 + dc
                                        if g2 >= 2:
                                            t.wait_ge(n_tc.h, bTC + g2 - 1)
                                        n_tr.inc(t.transpose(
                                            PT[dc % 2][:],
                                            xdtm[st_ % 2][:, dc * 128:(dc + 1) * 128],
                                            ident_sb[:]))
                        M[f"{tag}_xd_{half}"] = ("v", n_tc.n)
                    else:
                        with nc.Block() as blk:
                            @blk.sync
                            def _(s, half=half, gate=gate):
                                s.wait_ge(gate[0].h, gate[1])
                                s.wait_ge(b_sf[0].h, b_sf[0].n)
                                s.wait_ge(b_sf[1].h, b_sf[1].n)
                                for dcg in range(4):
                                    n_gx.inc(s.dma_start(
                                        out=ar(O_XD + dcg * 4096, 4096).rearrange(
                                            "p (dc t) -> p dc t", dc=4),
                                        in_=xn_fm_c.rearrange("(dc p) t -> p dc t", p=128)
                                            [:, dcg * 4:(dcg + 1) * 4,
                                             half * 1024:(half + 1) * 1024]), 16)
                        M[f"{tag}_xd_{half}"] = ("s", n_gx.n)

                    if tag == "sh" and "shh" in KCUT:
                        M[f"{tag}_h_{half}"] = n_ht.n
                        M[f"{tag}_w2g_{half}"] = (n_gx, n_gx.n)
                        continue
                    bW1 = [n_w1[0].n, n_w1[1].n]
                    bHM = n_hm.n
                    bSI = n_si.n
                    bHT = n_ht.n
                    with nc.Block() as blk:
                        @blk.gpsimd
                        def _(g, w1src=w1src, w3src=w3src, gate=gate, bHT=bHT):
                            g.wait_ge(gate[0].h, gate[1])
                            for hc in range(NHC):
                                if hc >= 2:
                                    g.wait_ge(n_ht.h, bHT + 2 * (hc - 1))
                                for wj, wsrc in ((0, w1src), (1, w3src)):
                                    for dcg in range(4):
                                        n_w1[hc % 2].inc(g.dma_start(
                                            out=ar(O_WB + (hc % 2) * 4096 + wj * 2048
                                                   + dcg * 512, 512).rearrange(
                                                "p (dc h) -> p dc h", dc=4),
                                            in_=wsrc[hc, dcg * 4:(dcg + 1) * 4].rearrange(
                                                "dc p h -> p dc h")), 16)

                        @blk.tensor
                        def _(t, half=half, tag=tag, bW1=bW1, bHT=bHT):
                            kind, val = M[f"{tag}_xd_{half}"]
                            t.wait_ge((n_tc if kind == "v" else n_gx).h, val)
                            for hc in range(NHC):
                                t.wait_ge(n_w1[hc % 2].h, bW1[hc % 2] + 128 * (hc // 2 + 1))
                                for t2 in range(2):
                                    hidx = hc * 2 + t2
                                    if hidx >= 2:
                                        t.wait_ge(n_ht.h, bHT + hidx - 1)
                                    for wj in range(2):
                                        dst = PA[t2 % 2] if wj == 0 else PA[2 + t2 % 2]
                                        for dc in range(NDC):
                                            mmi = t.matmul(dst[:],
                                                           ar(O_WB + (hc % 2) * 4096
                                                              + wj * 2048 + dc * 128, 128),
                                                           ar(O_XD + dc * 1024 + t2 * 512, 512),
                                                           start=(dc == 0),
                                                           stop=(dc == NDC - 1))
                                    n_hm.inc(mmi)

                        @blk.scalar
                        def _(sc, bHM=bHM, bHT=bHT):
                            for hc in range(NHC):
                                for t2 in range(2):
                                    hidx = hc * 2 + t2
                                    sc.wait_ge(n_hm.h, bHM + hidx + 1)
                                    if hidx >= 2:
                                        sc.wait_ge(n_ht.h, bHT + hidx - 1)
                                    n_si.inc(sc.activation(s1_ev[t2 % 2][:],
                                                           PA[t2 % 2][:], AF.Silu))

                        @blk.vector
                        def _(v, bSI=bSI):
                            for hc in range(NHC):
                                for t2 in range(2):
                                    hidx = hc * 2 + t2
                                    v.wait_ge(n_si.h, bSI + hidx + 1)
                                    n_ht.inc(v.tensor_tensor(
                                        ar(O_HH + hc * 1024 + t2 * 512, 512),
                                        s1_ev[t2 % 2][:], PA[2 + t2 % 2][:],
                                        op=ALU.mult))

                        if DP == 5 and moe and half == 0:
                            @blk.sync
                            def _(s, bHT=bHT, half=half, tag=tag):
                                kind, val = M[f"{tag}_xd_{half}"]
                                s.wait_ge((n_tc if kind == "v" else n_gx).h, val)
                                for dc in range(NDC):
                                    dx_s.inc(s.dma_start(
                                        out=dbg["d_xd"][dc * 128:(dc + 1) * 128, :],
                                        in_=ar(O_XD + dc * 1024, 1024)), 16)
                                s.wait_ge(n_ht.h, bHT + 2 * NHC)
                                for hc in range(NHC):
                                    dx_s.inc(s.dma_start(
                                        out=dbg["d_hh"][hc * 128:(hc + 1) * 128, :],
                                        in_=ar(O_HH + hc * 1024, 1024)), 16)
                    M[f"{tag}_h_{half}"] = n_ht.n

                    if tag == "sh" and "shw2" in KCUT:
                        M[f"{tag}_w2g_{half}"] = (n_ht, n_ht.n)
                        continue
                    for oh in range(2):
                        bW2 = n_w2.n
                        bM2 = n_m2.n
                        bOE = n_oe.n
                        bOS = [n_os[0].n, n_os[1].n]
                        with nc.Block() as blk:
                            @blk.tensor
                            def _(t, bW2=bW2, bOE=bOE):
                                t.wait_ge(n_w2.h, bW2 + 16 * NHC)
                                for stl in range(8):
                                    for otl in range(2):
                                        idx = stl * 2 + otl
                                        if idx >= 2:
                                            t.wait_ge(n_oe.h, bOE + idx - 1)
                                        for hc in range(NHC):
                                            mmi = t.matmul(PA[idx % 2][:],
                                                           ar(O_HH + hc * 1024 + stl * 128, 128),
                                                           ar(O_W2 + hc * 1024 + otl * 512, 512),
                                                           start=(hc == 0),
                                                           stop=(hc == NHC - 1))
                                        n_m2.inc(mmi)

                            @blk.vector
                            def _(v, bM2=bM2, bOS=bOS):
                                for stl in range(8):
                                    for otl in range(2):
                                        idx = stl * 2 + otl
                                        v.wait_ge(n_m2.h, bM2 + idx + 1)
                                        if idx >= 2:
                                            v.wait_ge(n_os[idx % 2].h,
                                                      bOS[idx % 2] + 16 * (idx // 2))
                                        n_oe.inc(v.tensor_copy(oe_ev[idx % 2][:],
                                                               PA[idx % 2][:]))

                            @blk.gpsimd
                            def _(g, w2src=w2src, tag=tag, half=half, oh=oh, bM2=bM2):
                                g.wait_ge(n_ht.h, M[f"{tag}_h_{half}"])
                                if DP == 5 and moe and half == 0:
                                    g.wait_ge(dx_s.h, 16 * (NDC + NHC))
                                if oh == 1:
                                    g.wait_ge(n_m2.h, bM2)
                                for hc in range(NHC):
                                    n_w2.inc(g.dma_start(
                                        out=ar(O_W2 + hc * 1024, 1024),
                                        in_=w2src[hc * 128:(hc + 1) * 128,
                                                  oh * 1024:(oh + 1) * 1024]), 16)

                            @blk.sync
                            def _(s, out_dram=out_dram, half=half, oh=oh, bOE=bOE):
                                for stl in range(8):
                                    for otl in range(2):
                                        idx = stl * 2 + otl
                                        s.wait_ge(n_oe.h, bOE + idx + 1)
                                        n_os[idx % 2].inc(s.dma_start(
                                            out=out_dram[half * 1024 + stl * 128:
                                                         half * 1024 + (stl + 1) * 128,
                                                         oh * 1024 + otl * 512:
                                                         oh * 1024 + (otl + 1) * 512],
                                            in_=oe_ev[idx % 2][:]), 16)
                    M[f"{tag}_w2g_{half}"] = (n_m2, n_m2.n)
                M[f"{tag}_st"] = (n_os[0].n, n_os[1].n)

            if DP >= 4:
                first_gate = (q_mmA, M["p4_av"])
            else:
                first_gate = (p3_mmA, M.get("p3_mmA", 0)) if DP >= 3 else (cc, 0)
            emit_ffn("moe", W1t, W3t, W2d, oe_c, True, first_gate)

            with nc.Block() as blk:
                @blk.gpsimd
                def _(g):
                    g.wait_ge(n_os[0].h, M["moe_st"][0])
                    g.wait_ge(n_os[1].h, M["moe_st"][1])
                    cc.inc(g.collective_compute(
                        "AllGather", ALU.bypass, replica_groups=[list(range(8))],
                        ins=[oe_c], outs=[oe_full]))
            M["cc_ag5"] = cc.n

        if DP >= 6:
            emit_ffn("sh", sw1t, sw3t, sw2d, shared_my, False, M["moe_w2g_1"])

        if DP >= 7:
            with nc.Block() as blk:
                @blk.sync
                def _(s):
                    s.wait_ge(cc.h, M["cc_a2a4"])
                    g = M["sh_w2g_1"]
                    s.wait_ge(g[0].h, g[1])
                    for fc in range(16):
                        j, hl = fc // 2, fc % 2
                        o_af.inc(s.dma_start(
                            out=ar(O_AFM + fc * 2048, 2048),
                            in_=att_all[j * 2048:(j + 1) * 2048,
                                        hl * 128:(hl + 1) * 128],
                            transpose=True), 16)
                    s.wait_ge(f_wst[0].h, M["woTe_w0"])
                    s.wait_ge(f_wst[1].h, M["woTe_w1"])
                    for fc in range(16):
                        o_wo.inc(s.dma_start(
                            out=ar(O_WOE + fc * 2048, 2048),
                            in_=WoTe[fc * 128:(fc + 1) * 128, :]), 16)
                    for tc in range(16):
                        if tc >= 2:
                            s.wait_ge(o_fr6.h, tc - 1)
                        o_x[tc % 2].inc(s.dma_start(
                            out=xt_f32[tc % 2], in_=x_my[tc * 128:(tc + 1) * 128, :]), 16)
                        o_sh[tc % 2].inc(s.dma_start(
                            out=shbuf[tc % 2][:],
                            in_=shared_my[tc * 128:(tc + 1) * 128, :]), 16)
                        for ot in range(4):
                            idx = tc * 4 + ot
                            s.wait_ge(o_ev.h, idx + 1)
                            o_st[idx % 2].inc(s.dma_start(
                                out=out_my[tc * 128:(tc + 1) * 128,
                                           ot * 512:(ot + 1) * 512],
                                in_=out_ev[idx % 2]), 16)

                @blk.gpsimd
                def _(g):
                    g.wait_ge(cc.h, M["cc_ag5"])
                    for tc in range(16):
                        if tc >= 2:
                            g.wait_ge(o_fr5.h, tc - 1)
                        g.memset(moebuf[tc % 2][:], 0.0)
                        o_mg[tc % 2].inc(g.indirect_dma_start(
                            out=moebuf[tc % 2][:], out_offset=None,
                            in_=oe_full,
                            in_offset=bass.IndirectOffsetOnAxis(slotu_sb[:, tc:tc + 1], 0),
                            bounds_check=NCORE * CAP - 1, oob_is_err=False), 16)

                @blk.tensor
                def _(t):
                    t.wait_ge(o_af.h, 16 * 16)
                    t.wait_ge(o_wo.h, 16 * 16)
                    for tc in range(16):
                        for ot in range(4):
                            idx = tc * 4 + ot
                            if idx >= 2:
                                t.wait_ge(o_ev.h, idx - 1)
                            for fc in range(16):
                                t.matmul(PA[idx % 2][:],
                                         ar(O_AFM + fc * 2048 + tc * 128, 128),
                                         ar(O_WOE + fc * 2048 + ot * 512, 512),
                                         start=(fc == 0), stop=False)
                            o_mm.inc(t.matmul(PA[idx % 2][:], ones_sb[:],
                                              bor_sb[:, ot * 512:(ot + 1) * 512],
                                              start=False, stop=True))

                @blk.vector
                def _(v):
                    for tc in range(16):
                        v.wait_ge(o_mg[tc % 2].h, 16 * (tc // 2 + 1))
                        v.wait_ge(o_x[tc % 2].h, 16 * (tc // 2 + 1))
                        v.wait_ge(o_sh[tc % 2].h, 16 * (tc // 2 + 1))
                        for ot in range(4):
                            idx = tc * 4 + ot
                            v.wait_ge(o_mm.h, idx + 1)
                            if idx >= 2:
                                v.wait_ge(o_st[idx % 2].h, 16 * (idx // 2))
                            VE(v, v.scalar_tensor_tensor(
                                tmp1_f, moebuf[tc % 2][:, ot * 512:(ot + 1) * 512],
                                gs_sb[:, tc:tc + 1], PA[idx % 2][:],
                                op0=ALU.mult, op1=ALU.add))
                            VE(v, v.tensor_tensor(
                                tmp2_f, tmp1_f,
                                xt_f32[tc % 2][:, ot * 512:(ot + 1) * 512], op=ALU.add))
                            o_ev.inc(v.tensor_tensor(
                                out_ev[idx % 2], tmp2_f,
                                shbuf[tc % 2][:, ot * 512:(ot + 1) * 512], op=ALU.add))
                        o_fr5.inc(v.nop())
                        o_fr6.inc(v.nop())

        with nc.Block() as blk:
            @blk.sync
            def _(s):
                if DP < 99:
                    if DP >= 1:
                        s.wait_ge(cc.h, M["cc_ag1"])
                        s.wait_ge(b_st[0].h, b_st[0].n)
                        s.wait_ge(b_st[1].h, b_st[1].n)
                        d_st.inc(s.dma_start(out=dbg["d_xn_tm"], in_=xn_tm_c), 16)
                        d_st.inc(s.dma_start(out=dbg["d_xn_fm"], in_=xn_fm_full), 16)
                    if DP >= 2:
                        s.wait_ge(cc.h, M["cc_a2a3"])
                        d_st.inc(s.dma_start(out=dbg["d_xs"], in_=xs_all), 16)
                        if DP == 2:
                            d_st.inc(s.dma_start(out=dbg["d_xsc"], in_=xs_c), 16)
                            d_st.inc(s.dma_start(out=dbg["d_slotu"], in_=slotu_sb[:]), 16)
                            d_st.inc(s.dma_start(out=dbg["d_eq"], in_=logits_all2[:]), 16)
                            d_st.inc(s.dma_start(out=dbg["d_vl"], in_=logits_all3[:]), 16)
                            d_st.inc(s.dma_start(out=dbg["d_nm"][:, 0:16], in_=nm_all[:]), 16)
                            d_st.inc(s.dma_start(out=dbg["d_nm"][:, 16:32], in_=pmax_all[:]), 16)
                        for k, srcb in enumerate([eidx_sb, gate_sb, posd_sb, keep_sb,
                                                  slotf_sb, gs_sb]):
                            d_st.inc(s.dma_start(out=dbg["d_route"][:, k * 16:(k + 1) * 16],
                                                 in_=srcb[:]), 16)
                    if DP >= 3:
                        for i4 in range(4):
                            s.wait_ge(p3_ks[i4].h, M["p3_ks"][i4])
                        d_st.inc(s.dma_start(out=dbg["d_Kfm"], in_=Kfm), 16)
                        d_st.inc(s.dma_start(out=dbg["d_Qfm"], in_=Qfm), 16)
                    if DP >= 4:
                        s.wait_ge(cc.h, M["cc_a2a4"])
                        d_st.inc(s.dma_start(out=dbg["d_att"], in_=att_all), 16)
                    if DP >= 5:
                        s.wait_ge(cc.h, M["cc_ag5"])
                        d_st.inc(s.dma_start(out=dbg["d_oe"], in_=oe_full), 16)
                    if DP >= 6:
                        s.wait_ge(n_os[0].h, M["sh_st"][0])
                        s.wait_ge(n_os[1].h, M["sh_st"][1])
                        if "shdump" not in KCUT:
                            d_st.inc(s.dma_start(out=dbg["d_shared"], in_=shared_my), 16)
                    s.wait_ge(d_st.h, d_st.n)
                if DP >= 7:
                    s.wait_ge(o_st[0].h, o_st[0].n)
                    s.wait_ge(o_st[1].h, o_st[1].n)

            @blk.gpsimd
            def _(g):
                g.wait_ge(g_gs.h, g_gs.n)

    return nc, dbg


def _tile_w(W):
    return np.ascontiguousarray(
        np.asarray(W).reshape(NDC, 128, NHC, 128).transpose(2, 0, 1, 3))


def _prep_inputs(inputs):
    f32 = np.float32
    x = np.ascontiguousarray(np.asarray(inputs["L0_attn_x"], f32).reshape(TTOK, D))
    WT = {w: np.ascontiguousarray(np.asarray(inputs[f"L0_attn_W{w}"], f32).T)
          for w in "qkvo"}
    A_T = {w: np.ascontiguousarray(np.asarray(inputs[f"L0_attn_A{w}"], f32).T)
           for w in "qkvo"}
    Bm = {w: np.ascontiguousarray(np.asarray(inputs[f"L0_attn_B{w}"], f32))
          for w in "qkvo"}
    bias = {w: np.asarray(inputs[f"L0_attn_b{w}"], f32) for w in "qkvo"}
    Wg = np.ascontiguousarray(np.asarray(inputs["L0_ffn_Wg"], f32))
    W1 = np.asarray(inputs["L0_ffn_W1"], f32)
    W3 = np.asarray(inputs["L0_ffn_W3"], f32)
    W2 = np.asarray(inputs["L0_ffn_W2"], f32)
    sw1t = _tile_w(inputs["L0_ffn_sw1"])
    sw3t = _tile_w(inputs["L0_ffn_sw3"])
    sw2 = np.ascontiguousarray(np.asarray(inputs["L0_ffn_sw2"], f32))

    bf = ml_dtypes.bfloat16
    consts = {
        "ident": np.eye(128, dtype=f32).astype(bf),
        "ident32": np.eye(128, dtype=f32),
        "TRI": np.triu(np.ones((128, 128), f32)).astype(bf),
        "iota8": np.tile(np.arange(8, dtype=f32), (128, 1)),
        "iota1k": np.tile(np.arange(8, dtype=f32) + 1000.0, (128, 1)),
        "onesrow": np.ones((1, 128), f32).astype(bf),
        "ones816": np.ones((8, 16), f32),
    }

    in_maps = []
    for c in range(NCORE):
        fsl = slice(256 * c, 256 * (c + 1))
        m = {
            "x_my": np.ascontiguousarray(x[c * TOWN:(c + 1) * TOWN]),
            "WqT": np.ascontiguousarray(WT["q"][:, fsl]),
            "WkT": np.ascontiguousarray(WT["k"][:, fsl]),
            "WvT": np.ascontiguousarray(WT["v"][:, fsl]),
            "WoT": WT["o"],
            "Bq": Bm["q"], "Bk": Bm["k"], "Bv": Bm["v"], "Bo": Bm["o"],
            "AqT": np.ascontiguousarray(A_T["q"][:, fsl]),
            "AkT": np.ascontiguousarray(A_T["k"][:, fsl]),
            "AvT": np.ascontiguousarray(A_T["v"][:, fsl]),
            "AoT": A_T["o"],
            "bq_c": np.ascontiguousarray(bias["q"][fsl].reshape(2, 128).T),
            "bk_c": np.ascontiguousarray(bias["k"][fsl].reshape(2, 128).T),
            "bv_r": np.ascontiguousarray(bias["v"][fsl].reshape(1, 256)),
            "bo_r": np.ascontiguousarray(bias["o"].reshape(1, D)),
            "Wg": Wg,
            "W1t": _tile_w(W1[c]),
            "W3t": _tile_w(W3[c]),
            "W2d": np.ascontiguousarray(W2[c]),
            "sw1t": sw1t, "sw3t": sw3t, "sw2d": sw2,
            "maskp": np.tile((np.arange(8, dtype=f32) < c).astype(f32), (8, 1)),
            "tok1": (c * TOWN + 1 + np.arange(128, dtype=f32)[:, None]
                     + 128.0 * np.arange(16, dtype=f32)[None, :]),
        }
        m.update(consts)
        in_maps.append(m)
    return in_maps


def kernel(**inputs):
    in_maps = _prep_inputs(inputs)
    nc, _ = build_nc(DP=99)
    res = run_bass_kernel_spmd(nc, in_maps, core_ids=list(range(NCORE)))
    out = np.concatenate([np.asarray(r["out_my"], np.float32) for r in res.results],
                         axis=0)
    return out.reshape(4, 4096, 2048)


# version 16
# speedup vs baseline: 1.1597x; 1.0145x over previous
import os
import numpy as np
import ml_dtypes
import concourse.bass as bass
import concourse.mybir as mybir
from concourse.bass_utils import run_bass_kernel_spmd
from concourse.aot_env import allow_ici_all_to_all
from contextlib import ExitStack

F32 = mybir.dt.float32
BF16 = mybir.dt.bfloat16
U32 = mybir.dt.uint32
AF = mybir.ActivationFunctionType
ALU = mybir.AluOpType
AX = mybir.AxisListType

D = 2048
TTOK = 16384
TOWN = 2048
HFF = 2816
NCORE = 8
CAP = 2048
SCALE = 128.0 ** -0.5
LORA_S = 0.5
EPS = 1e-5
CSHIFT = 4.0
SENT = 1.0e9
NDC = 16
NHC = 22


class Sem:
    def __init__(self, h):
        self.h = h
        self.n = 0

    def inc(self, instr, k=1):
        instr.then_inc(self.h, k)
        self.n += k
        return self.n


def build_nc(DP=99):
    SUB = int(os.environ.get("KSUB", "9"))
    KCUT = set(x for x in os.environ.get("KCUT", "").split(",") if x)
    nc = bass.Bass("TRN2", target_bir_lowering=False, debug=False,
                   dynamic_dma_scratch_size=16384,
                   detect_race_conditions=False)

    def din(name, shape, dt=F32):
        return nc.dram_tensor(name, shape, dt, kind="ExternalInput").ap()

    def dout(name, shape, dt=F32):
        return nc.dram_tensor(name, shape, dt, kind="ExternalOutput").ap()

    def dtmp(name, shape, dt):
        return nc.dram_tensor(name, shape, dt).ap()

    x_my = din("x_my", [TOWN, D])
    WqT = din("WqT", [D, 256]); WkT = din("WkT", [D, 256]); WvT = din("WvT", [D, 256])
    WoT = din("WoT", [D, D])
    Bq = din("Bq", [16, D]); Bk = din("Bk", [16, D]); Bv = din("Bv", [16, D]); Bo = din("Bo", [16, D])
    AqT = din("AqT", [16, 256]); AkT = din("AkT", [16, 256]); AvT = din("AvT", [16, 256])
    AoT = din("AoT", [16, D])
    bq_c = din("bq_c", [128, 2]); bk_c = din("bk_c", [128, 2])
    bv_r = din("bv_r", [1, 256]); bo_r = din("bo_r", [1, D])
    Wg = din("Wg", [D, 8])
    W1t = din("W1t", [NHC, NDC, 128, 128])
    W3t = din("W3t", [NHC, NDC, 128, 128])
    W2d = din("W2d", [HFF, D])
    sw1t = din("sw1t", [NHC, NDC, 128, 128])
    sw3t = din("sw3t", [NHC, NDC, 128, 128])
    sw2d = din("sw2d", [HFF, D])
    ident = din("ident", [128, 128], BF16)
    ident32 = din("ident32", [128, 128], F32)
    TRI = din("TRI", [128, 128], BF16)
    iota8 = din("iota8", [128, 8])
    iota1k = din("iota1k", [128, 8])
    onesrow = din("onesrow", [1, 128], BF16)
    ones816 = din("ones816", [8, 16])
    maskp = din("maskp", [8, 8])
    tok1 = din("tok1", [128, 16])

    out_my = dout("out_my", [TOWN, D])

    xn_tm_c = dtmp("xn_tm_c", [TOWN, D], BF16)
    xn_fm_c = dtmp("xn_fm_c", [D, TOWN], BF16)
    xn_fm_full = dtmp("xn_fm_full", [NCORE * D, TOWN], BF16)
    hist_c = dtmp("hist_c", [8, 1], F32)
    hist_full = dtmp("hist_full", [64, 1], F32)
    xs_c = dtmp("xs_c", [TTOK, D], BF16)
    xs_all = dtmp("xs_all", [TTOK, D], BF16)
    Kfm = dtmp("Kfm", [256, TTOK], BF16)
    Qfm = dtmp("Qfm", [256, TTOK], BF16)
    att_c = dtmp("att_c", [TTOK, 256], BF16)
    att_all = dtmp("att_all", [TTOK, 256], BF16)
    WoTe = dtmp("WoTe", [D, D], BF16)
    oe_c = dtmp("oe_c", [CAP, D], BF16)
    oe_full = dtmp("oe_full", [NCORE * CAP, D], BF16)
    shared_my = dtmp("shared_my", [TOWN, D], BF16)

    dbg = {}
    if DP < 99:
        if DP >= 1:
            dbg["d_xn_tm"] = dout("d_xn_tm", [TOWN, D], BF16)
            dbg["d_xn_fm"] = dout("d_xn_fm", [NCORE * D, TOWN], BF16)
        if DP >= 2:
            dbg["d_route"] = dout("d_route", [128, 16 * 6])
            dbg["d_xs"] = dout("d_xs", [TTOK, D], BF16)
            if DP == 2:
                dbg["d_xsc"] = dout("d_xsc", [TTOK, D], BF16)
                dbg["d_slotu"] = dout("d_slotu", [128, 16], U32)
                dbg["d_eq"] = dout("d_eq", [128, 128])
                dbg["d_vl"] = dout("d_vl", [128, 128])
                dbg["d_nm"] = dout("d_nm", [128, 32])
        if DP >= 3:
            dbg["d_Kfm"] = dout("d_Kfm", [256, TTOK], BF16)
            dbg["d_Qfm"] = dout("d_Qfm", [256, TTOK], BF16)
        if DP >= 4:
            dbg["d_att"] = dout("d_att", [TTOK, 256], BF16)
        if DP >= 5:
            dbg["d_oe"] = dout("d_oe", [NCORE * CAP, D], BF16)
        if DP == 5:
            dbg["d_xd"] = dout("d_xd", [2048, 1024], BF16)
            dbg["d_hh"] = dout("d_hh", [NHC * 128, 1024], BF16)
        if DP >= 6:
            dbg["d_shared"] = dout("d_shared", [TOWN, D], BF16)

    with ExitStack() as stk:
        def sb(name, shape, dt):
            return stk.enter_context(nc.sbuf_tensor(name, shape, dt))

        def psumt(name, shape, dt):
            return stk.enter_context(nc.psum_tensor(name, shape, dt))

        def S(name):
            return Sem(stk.enter_context(nc.semaphore(name)))

        AR = 90032
        arena = sb("arena", [128, AR], BF16)
        f32ar = sb("f32ar", [128, 6144], F32)
        ident_sb = sb("ident_sb", [128, 128], BF16)
        id32_sb = sb("id32_sb", [128, 128], F32)
        tri_sb = sb("tri_sb", [128, 128], BF16)
        iota8_sb = sb("iota8_sb", [128, 8], F32)
        iota1k_sb = sb("iota1k_sb", [128, 8], F32)
        ones_sb = sb("ones_sb", [1, 128], BF16)
        ones816_sb = sb("ones816_sb", [8, 16], F32)
        wg_sb = sb("wg_sb", [128, NDC * 8], BF16)
        bqc_sb = sb("bqc_sb", [128, 2], F32)
        bkc_sb = sb("bkc_sb", [128, 2], F32)
        bvr_sb = sb("bvr_sb", [1, 256], BF16)
        bor_sb = sb("bor_sb", [1, D], BF16)
        maskp_sb = sb("maskp_sb", [8, 8], F32)
        tok1_sb = sb("tok1_sb", [128, 16], F32)
        mean_all = sb("mean_all", [128, 16], F32)
        var_all = sb("var_all", [128, 16], F32)
        sd_all = sb("sd_all", [128, 16], F32)
        rstd_all = sb("rstd_all", [128, 16], F32)
        logits_all = sb("logits_all", [128, 128], F32)
        nm_all = sb("nm_all", [128, 16], F32)
        pmax_all = sb("pmax_all", [128, 16], F32)
        se_all = sb("se_all", [128, 16], F32)
        gate_sb = sb("gate_sb", [128, 16], F32)
        gs_sb = sb("gs_sb", [128, 16], F32)
        slotu_sb = sb("slotu_sb", [128, 16], U32)
        slotf_sb = sb("slotf_sb", [128, 16], F32)
        eidx_sb = sb("eidx_sb", [128, 16], F32)
        keep_sb = sb("keep_sb", [128, 16], F32)
        posd_sb = sb("posd_sb", [128, 16], F32)
        oh_all = sb("oh_all", [128, 128], BF16)
        ohf_sb = sb("ohf_sb", [128, 8], F32)
        rec_sb = sb("rec_sb", [128, 1], F32)
        csh_sb = sb("csh_sb", [128, 1], F32)

        xt_f32 = [f32ar[:, 0:2048], f32ar[:, 2048:4096]]
        out_ev = [f32ar[:, 4096 + i * 512: 4096 + (i + 1) * 512] for i in range(2)]
        tmp1_f = f32ar[:, 5120:5632]
        tmp2_f = f32ar[:, 5632:6144]
        ohT_all = f32ar[0:8, 0:2048]
        logits_all2 = f32ar[:, 2048:2176]
        logits_all3 = f32ar[:, 2176:2304]
        pos_all = f32ar[:, 2304:2432]
        junk8 = f32ar[:, 2432:2440]
        junk8b = f32ar[:, 2440:2448]
        histco = f32ar[0:8, 2448:2464]
        pref = f32ar[0:8, 2464:2480]
        offm1 = f32ar[0:8, 2480:2496]
        offmF = f32ar[0:8, 2496:2512]
        offc = f32ar[0:8, 2512:2513]
        hf_sb = f32ar[0:8, 2520:2528]
        hm_sb = f32ar[0:8, 2528:2536]
        r_c1 = f32ar[:, 2536:2537]
        stat6 = f32ar[:, 4096:4120]
        stat2 = f32ar[:, 4120:4122]
        r_c2 = f32ar[:, 2544:2545]
        r_c3 = f32ar[:, 2552:2553]

        O_VAUG = 0
        VAUG_SZ = 128 * 258
        O_FM = VAUG_SZ
        O_ET = VAUG_SZ
        O_KH = O_ET + 16384
        O_QP = O_KH + 8192
        O_STG = O_QP + 1024
        O_HH = 0
        O_XD = 22528
        O_WB = O_XD + 16384
        O_W2 = 22528
        O_AFM = 0
        O_WOE = 32768

        def ar(o, n):
            return arena[:, o:o + n]

        xnb_ev = [ar(33024, 2048), ar(35072, 2048)]
        fmstage = ar(37120, 4096)
        lor_b = arena[0:16, 41216:43264]
        aot_b = arena[0:16, 43264:45312]
        at_b = arena[0:16, 45312:45568]
        raw_b = [ar(45568, 256), ar(45824, 256)]
        rawo_b = [ar(61696, 2048), ar(63744, 2048)]
        woe_b = [ar(65792, 2048), ar(67840, 2048)]
        O_WEFF = 49408

        def weff(w, dc):
            o = O_WEFF + ({"q": 0, "k": 1, "v": 2}[w] * NDC + dc) * 256
            return arena[:, o:o + 256]

        kq_ev = [ar(69888 + i * 512, 512) for i in range(4)]
        xdtm = [ar(47104, 2048), ar(49152, 2048)]
        xsh = [ar(51200, 2048), ar(53248, 2048)]
        accS = ar(57344, 2048)
        s1_ev = [ar(55296, 512), ar(55808, 512)]
        oe_ev = [ar(56320, 512), ar(56832, 512)]
        moebuf = [ar(65536, 2048), ar(67584, 2048)]
        shbuf = [ar(69632, 2048), ar(71680, 2048)]

        PA = [psumt(f"PA{i}", [128, 512], F32) for i in range(4)]
        PB = [psumt(f"PB{i}", [128, 512], F32) for i in range(2)]
        PT_full = [psumt(f"PT{i}", [128, 1024], BF16) for i in range(2)]
        PT = [t[:, 0:128] for t in PT_full]

        cc = S("cc")
        g0 = S("g0")
        f_ba = S("f_ba"); f_raw = [S("f_raw0"), S("f_raw1")]
        f_mm = S("f_mm"); f_cmb = S("f_cmb")
        f_wst = [S("f_wst0"), S("f_wst1")]
        a_x = [S("a_x0"), S("a_x1")]; a_fr = S("a_fr"); a_ms = S("a_ms")
        m_sq = S("m_sq"); m_rs = S("m_rs")
        b_x = [S("b_x0"), S("b_x1")]; b_fr = S("b_fr"); b_xn = S("b_xn")
        b_fmc = S("b_fmc"); b_tr = S("b_tr"); b_lg = S("b_lg"); b_le = S("b_le")
        b_st = [S("b_st0"), S("b_st1")]; b_sf = [S("b_sf0"), S("b_sf1")]
        r_nm = S("r_nm"); r_ex = S("r_ex"); r_vA = S("r_vA"); r_mmA = S("r_mmA")
        r_hs = S("r_hs"); r_hl = S("r_hl"); r_off = S("r_off")
        r_mmB = S("r_mmB"); r_vB = S("r_vB"); g_gs = S("g_gs")
        p3_fm = [S("p3_fm0"), S("p3_fm1")]
        p3_mmA = S("p3_mmA"); p3_mmV = S("p3_mmV")
        p3_ev = S("p3_ev"); p3_vv = S("p3_vv"); p3_vm = S("p3_vm")
        p3_ks = [S(f"p3_ks{i}") for i in range(4)]
        q_kh = [S("q_kh0"), S("q_kh1")]; q_qp = [S("q_qp0"), S("q_qp1")]
        q_mmS = S("q_mmS"); q_mmA = S("q_mmA"); q_ex = S("q_ex")
        q_rc = S("q_rc"); q_av = S("q_av"); q_st = S("q_st")
        z_gs = S("z_gs"); z_as = S("z_as"); z_ms = S("z_ms")
        x_sc = [S("x_sc0"), S("x_sc1")]
        m_sl = [S("m_sl0"), S("m_sl1")]; m_mg = S("m_mg")
        n_ga = [S("n_ga0"), S("n_ga1")]; n_gx = S("n_gx")
        n_tr = S("n_tr"); n_tc = S("n_tc")
        n_w1 = [S("n_w10"), S("n_w11")]; n_hm = S("n_hm"); n_si = S("n_si")
        n_ht = S("n_ht"); n_w2 = S("n_w2"); n_m2 = S("n_m2"); n_oe = S("n_oe")
        dx_s = S("dx_s")
        n_os = [S("n_os0"), S("n_os1")]
        o_af = S("o_af"); o_wo = S("o_wo")
        o_x = [S("o_x0"), S("o_x1")]; o_sh = [S("o_sh0"), S("o_sh1")]
        o_mg = [S("o_mg0"), S("o_mg1")]
        o_mm = S("o_mm"); o_ev = S("o_ev"); o_st = [S("o_st0"), S("o_st1")]
        o_fr5 = S("o_fr5"); o_fr6 = S("o_fr6"); d_st = S("d_st")
        vch = S("vch")

        def VE(v, instr):
            vch.inc(instr)
            v.wait_ge(vch.h, vch.n)

        M = {}

        with nc.Block() as blk:
            @blk.gpsimd
            def _(g):
                for dst, src in [(ident_sb[:], ident), (id32_sb[:], ident32),
                                 (tri_sb[:], TRI), (iota8_sb[:], iota8),
                                 (iota1k_sb[:], iota1k), (ones_sb[:], onesrow),
                                 (ones816_sb[:], ones816),
                                 (bqc_sb[:], bq_c), (bkc_sb[:], bk_c),
                                 (bvr_sb[:], bv_r), (bor_sb[:], bo_r),
                                 (maskp_sb[:], maskp), (tok1_sb[:], tok1)]:
                    g0.inc(g.dma_start(out=dst, in_=src), 16)
                g0.inc(g.dma_start(
                    out=wg_sb[:].rearrange("p (dc e) -> p dc e", dc=NDC),
                    in_=Wg.rearrange("(dc p) e -> p dc e", p=128)), 16)
                g.memset(csh_sb[:], -CSHIFT)
                z_ms.inc(g.memset(ar(0, 2048), 0.0))
                g.wait_ge(z_ms.h, 1)
                for r in range(64):
                    g_gs.inc(g.dma_start(out=xs_c[r * 128:(r + 1) * 128, :],
                                         in_=ar(0, 2048)), 16)
        M["consts"] = g0.n
        M["xs_zero_g"] = g_gs.n

        with nc.Block() as blk:
            @blk.scalar
            def _(sc):
                sc.wait_ge(z_ms.h, 1)
                for r in range(64, 128):
                    z_as.inc(sc.dma_start(out=xs_c[r * 128:(r + 1) * 128, :],
                                          in_=ar(0, 2048)), 16)
        M["xs_zero_a"] = z_as.n

        with nc.Block() as blk:
            @blk.sync
            def _(s):
                for i in range(16):
                    if i >= 2:
                        s.wait_ge(a_fr.h, i - 1)
                    a_x[i % 2].inc(s.dma_start(out=xt_f32[i % 2],
                                               in_=x_my[i * 128:(i + 1) * 128, :]), 16)

            @blk.vector
            def _(v):
                for i in range(16):
                    v.wait_ge(a_x[i % 2].h, 16 * (i // 2 + 1))
                    for q4 in range(4):
                        ii = v.bn_stats(stat6[:, 6 * q4:6 * q4 + 6],
                                        xt_f32[i % 2][:, 512 * q4:512 * (q4 + 1)])
                    a_ms.inc(ii)
                    v.wait_ge(a_ms.h, a_ms.n)
                    a_ms.inc(v.bn_aggr(stat2[:], stat6[:]))
                    v.wait_ge(a_ms.h, a_ms.n)
                    v.tensor_copy(mean_all[:, i:i + 1], stat2[:, 0:1])
                    ts = v.tensor_scalar(var_all[:, i:i + 1], stat2[:, 1:2], EPS, None,
                                         op0=ALU.add)
                    if i == 15:
                        a_ms.inc(ts)
                    a_fr.inc(v.nop())
        M["stats"] = a_ms.n

        with nc.Block() as blk:
            @blk.scalar
            def _(sc):
                sc.wait_ge(a_ms.h, M["stats"])
                m_sq.inc(sc.activation(sd_all[:], var_all[:], AF.Sqrt))

            @blk.vector
            def _(v):
                v.wait_ge(m_sq.h, 1)
                m_rs.inc(v.reciprocal(rstd_all[:], sd_all[:]))
                v.wait_ge(m_rs.h, 1)
        M["rstd"] = m_rs.n

        for wi, (wname, WT_d, Bm, Am) in enumerate(
                [("q", WqT, Bq, AqT), ("k", WkT, Bk, AkT), ("v", WvT, Bv, AvT)]):
            bR = [f_raw[0].n, f_raw[1].n]
            bC = f_cmb.n
            bM = f_mm.n
            with nc.Block() as blk:
                @blk.gpsimd
                def _(g, Bm=Bm, Am=Am, WT_d=WT_d, bC=bC):
                    g.wait_ge(f_cmb.h, bC)
                    f_ba.inc(g.dma_start(out=lor_b[:, 0:D], in_=Bm), 16)
                    f_ba.inc(g.dma_start(out=at_b[:], in_=Am), 16)
                    for dc in range(NDC):
                        if dc >= 2:
                            g.wait_ge(f_cmb.h, bC + dc - 1)
                        f_raw[dc % 2].inc(g.dma_start(
                            out=raw_b[dc % 2][:], in_=WT_d[dc * 128:(dc + 1) * 128, :]), 16)
                M[f"ba_{wname}"] = f_ba.n

                @blk.tensor
                def _(t, wname=wname, bC=bC):
                    t.wait_ge(f_ba.h, M[f"ba_{wname}"])
                    for dc in range(NDC):
                        if dc >= 2:
                            t.wait_ge(f_cmb.h, bC + dc - 1)
                        f_mm.inc(t.matmul(PB[dc % 2][:, 0:256],
                                          lor_b[:, dc * 128:(dc + 1) * 128],
                                          at_b[:], start=True, stop=True))

                @blk.vector
                def _(v, wname=wname, bM=bM, bR=bR):
                    for dc in range(NDC):
                        v.wait_ge(f_mm.h, bM + dc + 1)
                        v.wait_ge(f_raw[dc % 2].h, bR[dc % 2] + 16 * (dc // 2 + 1))
                        f_cmb.inc(v.scalar_tensor_tensor(
                            weff(wname, dc), PB[dc % 2][:, 0:256], LORA_S,
                            raw_b[dc % 2][:], op0=ALU.mult, op1=ALU.add))
            M[f"fold_{wname}"] = f_cmb.n

        bR = [f_raw[0].n, f_raw[1].n]
        bC = f_cmb.n
        bM = f_mm.n
        bW = [f_wst[0].n, f_wst[1].n]
        with nc.Block() as blk:
            @blk.sync
            def _(s):
                for dc in range(NDC):
                    s.wait_ge(f_cmb.h, bC + 4 * (dc + 1))
                    f_wst[dc % 2].inc(s.dma_start(
                        out=WoTe[dc * 128:(dc + 1) * 128, :], in_=woe_b[dc % 2][:]), 16)

            @blk.gpsimd
            def _(g):
                g.wait_ge(f_cmb.h, bC)
                f_ba.inc(g.dma_start(out=lor_b[:, 0:D], in_=Bo), 16)
                f_ba.inc(g.dma_start(out=aot_b[:], in_=AoT), 16)
                M["ba_o"] = f_ba.n
                for dc in range(NDC):
                    if dc >= 2:
                        g.wait_ge(f_cmb.h, bC + 4 * (dc - 1))
                    f_raw[dc % 2].inc(g.dma_start(
                        out=rawo_b[dc % 2][:], in_=WoT[dc * 128:(dc + 1) * 128, :]), 16)

            @blk.tensor
            def _(t):
                t.wait_ge(f_ba.h, M["ba_o"])
                for dc in range(NDC):
                    for ot in range(4):
                        idx = dc * 4 + ot
                        if idx >= 2:
                            t.wait_ge(f_cmb.h, bC + idx - 1)
                        f_mm.inc(t.matmul(PB[idx % 2][:, 0:512],
                                          lor_b[:, dc * 128:(dc + 1) * 128],
                                          aot_b[:, ot * 512:(ot + 1) * 512],
                                          start=True, stop=True))

            @blk.vector
            def _(v):
                for dc in range(NDC):
                    if dc >= 2:
                        v.wait_ge(f_wst[dc % 2].h, bW[dc % 2] + 16 * (dc // 2 - 0))
                    for ot in range(4):
                        idx = dc * 4 + ot
                        v.wait_ge(f_mm.h, bM + idx + 1)
                        v.wait_ge(f_raw[dc % 2].h, bR[dc % 2] + 16 * (dc // 2 + 1))
                        f_cmb.inc(v.scalar_tensor_tensor(
                            woe_b[dc % 2][:, ot * 512:(ot + 1) * 512],
                            PB[idx % 2][:, 0:512], LORA_S,
                            rawo_b[dc % 2][:, ot * 512:(ot + 1) * 512],
                            op0=ALU.mult, op1=ALU.add))
        M["fold_o"] = f_cmb.n
        M["woTe_w0"] = f_wst[0].n
        M["woTe_w1"] = f_wst[1].n

        with nc.Block() as blk:
            @blk.sync
            def _(s):
                for i in range(16):
                    if i >= 2:
                        s.wait_ge(b_fr.h, i - 1)
                    b_x[i % 2].inc(s.dma_start(out=xt_f32[i % 2],
                                               in_=x_my[i * 128:(i + 1) * 128, :]), 16)
                    if "tm" not in KCUT:
                        s.wait_ge(b_xn.h, i + 1)
                        b_st[i % 2].inc(s.dma_start(out=xn_tm_c[i * 128:(i + 1) * 128, :],
                                                    in_=xnb_ev[i % 2][:]), 16)
                    if i % 2 == 1 and "fm" not in KCUT:
                        s.wait_ge(b_fmc.h, 16 * (i + 1))
                        b_sf[(i // 2) % 2].inc(s.dma_start(
                            out=xn_fm_c.rearrange("(dc p) t -> p dc t", p=128)
                                [:, :, (i - 1) * 128:(i + 1) * 128],
                            in_=fmstage[:].rearrange("p (dc t) -> p dc t", dc=NDC)), 16)

            @blk.vector
            def _(v):
                v.wait_ge(m_rs.h, M["rstd"])
                for i in range(16):
                    v.wait_ge(b_x[i % 2].h, 16 * (i // 2 + 1))
                    if i >= 2:
                        if "tm" not in KCUT:
                            v.wait_ge(b_st[i % 2].h, 16 * ((i - 2) // 2 + 1))
                        if "tr" not in KCUT:
                            v.wait_ge(b_tr.h, 16 * (i - 1))
                        P = (i - 2) // 2
                        if "fm" not in KCUT:
                            v.wait_ge(b_sf[0].h, 16 * (P // 2 + 1))
                            if P >= 1:
                                v.wait_ge(b_sf[1].h, 16 * ((P - 1) // 2 + 1))
                        if "lg" not in KCUT:
                            v.wait_ge(b_lg.h, i - 1)
                    b_xn.inc(v.tensor_scalar(xnb_ev[i % 2][:], xt_f32[i % 2],
                                             mean_all[:, i:i + 1], rstd_all[:, i:i + 1],
                                             op0=ALU.subtract, op1=ALU.mult))
                    b_fr.inc(v.nop())
                    if "tr" not in KCUT:
                        for dc in range(NDC):
                            g2 = i * 16 + dc
                            v.wait_ge(b_tr.h, g2 + 1)
                            b_fmc.inc(v.tensor_copy(
                                fmstage[:, dc * 256 + (i % 2) * 128:
                                        dc * 256 + (i % 2) * 128 + 128],
                                PT[dc % 2][:]))
                    if "lg" not in KCUT and "le" not in KCUT and "lev" not in KCUT:
                        v.wait_ge(b_lg.h, i + 1)
                        b_le.inc(v.tensor_copy(logits_all[:, 8 * i:8 * i + 8],
                                               PA[2 + i % 2][:, 0:8]))

            @blk.tensor
            def _(t):
                t.wait_ge(g0.h, M["consts"])
                for i in range(16):
                    t.wait_ge(b_xn.h, i + 1)
                    if "tr" not in KCUT:
                        for dc in range(NDC):
                            g2 = i * 16 + dc
                            if g2 >= 2:
                                t.wait_ge(b_fmc.h, g2 - 1)
                            b_tr.inc(t.transpose(PT[dc % 2][:],
                                                 xnb_ev[i % 2][:, dc * 128:(dc + 1) * 128],
                                                 ident_sb[:]))
                    if "lg" not in KCUT:
                        t.wait_ge(b_fmc.h, 16 * (i + 1))
                        if i >= 2 and "le" not in KCUT:
                            t.wait_ge(b_le.h, i - 1)
                        for dc in range(NDC):
                            mmi = t.matmul(PA[2 + i % 2][:, 0:8],
                                           fmstage[:, dc * 256 + (i % 2) * 128:
                                                   dc * 256 + (i % 2) * 128 + 128],
                                           wg_sb[:, dc * 8:(dc + 1) * 8],
                                           start=(dc == 0), stop=(dc == NDC - 1))
                        b_lg.inc(mmi)

            @blk.scalar
            def _(sc):
                if "lg" not in KCUT and "le" not in KCUT and "lev" in KCUT:
                    for i in range(16):
                        sc.wait_ge(b_lg.h, i + 1)
                        b_le.inc(sc.activation(logits_all[:, 8 * i:8 * i + 8],
                                               PA[2 + i % 2][:, 0:8], AF.Copy))

        with nc.Block() as blk:
            @blk.gpsimd
            def _(g):
                g.wait_ge(b_st[0].h, b_st[0].n)
                g.wait_ge(b_st[1].h, b_st[1].n)
                g.wait_ge(b_sf[0].h, b_sf[0].n)
                g.wait_ge(b_sf[1].h, b_sf[1].n)
                cc.inc(g.collective_compute(
                    "AllGather", ALU.bypass, replica_groups=[list(range(8))],
                    ins=[xn_fm_c], outs=[xn_fm_full]))
        M["cc_ag1"] = cc.n

        if DP >= 2:
            with nc.Block() as blk:
                @blk.vector
                def _(v):
                    v.wait_ge(b_le.h, 16)
                    for i in range(16):
                        last = v.reduce_max(nm_all[:, i:i + 1],
                                            logits_all[:, 8 * i:8 * i + 8],
                                            axis=AX.X, negate=True)
                    r_nm.inc(last)
                    v.wait_ge(r_nm.h, 1)
                    r_nm.inc(v.tensor_scalar(pmax_all[:], nm_all[:], -1.0, None,
                                             op0=ALU.mult))

                @blk.scalar
                def _(sc):
                    sc.wait_ge(r_nm.h, 2)
                    for i in range(16):
                        r_ex.inc(sc.activation(junk8[:], logits_all[:, 8 * i:8 * i + 8],
                                               AF.Exp, bias=nm_all[:, i:i + 1],
                                               accum_out=se_all[:, i:i + 1]))
            M["exp"] = r_ex.n

            with nc.Block() as blk:
                @blk.vector
                def _(v):
                    v.wait_ge(r_ex.h, M["exp"])
                    v.reciprocal(gate_sb[:], se_all[:])
                    for i in range(16):
                        if i >= 1:
                            v.wait_ge(r_mmA.h, 3 * (i - 1) + 3)
                        VE(v, v.tensor_scalar(junk8[:], logits_all[:, 8 * i:8 * i + 8],
                                              pmax_all[:, i:i + 1], None, op0=ALU.is_equal))
                        r_vA.inc(v.scalar_tensor_tensor(junk8b[:], junk8[:], -1000.0,
                                                        iota1k_sb[:],
                                                        op0=ALU.mult, op1=ALU.add))
                        v.wait_ge(r_vA.h, r_vA.n)
                        r_vA.inc(v.tensor_reduce(eidx_sb[:, i:i + 1], junk8b[:],
                                                 op=ALU.min, axis=AX.X))
                        v.wait_ge(r_vA.h, 5 * i + 2)
                        if DP == 2:
                            v.tensor_copy(logits_all2[:, 8 * i:8 * i + 8], junk8[:])
                            v.tensor_copy(logits_all3[:, 8 * i:8 * i + 8], junk8b[:])
                        VE(v, v.tensor_scalar(oh_all[:, 8 * i:8 * i + 8], iota8_sb[:],
                                              eidx_sb[:, i:i + 1], None, op0=ALU.is_equal))
                        r_vA.inc(v.tensor_copy(ohf_sb[:], oh_all[:, 8 * i:8 * i + 8]))
                        v.wait_ge(r_mmA.h, 3 * i + 1)
                        r_vA.inc(v.tensor_copy(pos_all[:, 8 * i:8 * i + 8],
                                               PA[i % 2][:, 0:8]))
                        v.wait_ge(r_mmA.h, 3 * i + 3)
                        v.tensor_copy(histco[:, i:i + 1], PA[2 + i % 2][0:8, 127:128])
                        r_vA.inc(v.tensor_copy(ohT_all[:, 128 * i:128 * (i + 1)],
                                               PB[i % 2][0:8, 0:128]))
                    r_vA.inc(v.nop())
                    v.wait_ge(r_vA.h, r_vA.n)
                    r_vA.inc(v.tensor_tensor_scan(pref[:], ones816_sb[:], histco[:],
                                                  0.0, op0=ALU.mult, op1=ALU.add))
                    v.wait_ge(r_vA.h, r_vA.n)

                @blk.tensor
                def _(t):
                    for i in range(16):
                        t.wait_ge(r_vA.h, 5 * i + 3)
                        if i >= 2:
                            t.wait_ge(r_vA.h, 5 * (i - 2) + 5)
                        r_mmA.inc(t.matmul(PA[i % 2][:, 0:8], tri_sb[:],
                                           oh_all[:, 8 * i:8 * i + 8],
                                           start=True, stop=True))
                        t.wait_ge(r_vA.h, 5 * i + 4)
                        r_mmA.inc(t.transpose(PA[2 + i % 2][0:8, 0:128],
                                              pos_all[:, 8 * i:8 * i + 8], id32_sb[:]))
                        r_mmA.inc(t.transpose(PB[i % 2][0:8, 0:128], ohf_sb[:],
                                              id32_sb[:]))
            M["p2a"] = r_vA.n

            with nc.Block() as blk:
                @blk.sync
                def _(s):
                    s.wait_ge(r_vA.h, M["p2a"])
                    r_hs.inc(s.dma_start(out=hist_c, in_=pref[:, 15:16]), 16)

                @blk.gpsimd
                def _(g):
                    g.wait_ge(r_hs.h, 16)
                    cc.inc(g.collective_compute(
                        "AllGather", ALU.bypass, replica_groups=[list(range(8))],
                        ins=[hist_c], outs=[hist_full]))
            M["cc_ag2"] = cc.n

            with nc.Block() as blk:
                @blk.sync
                def _(s):
                    s.wait_ge(cc.h, M["cc_ag2"])
                    with nc.allow_non_contiguous_dma("tiny 64-elem strided load"):
                        r_hl.inc(s.dma_start(
                            out=hf_sb[:].rearrange("e (j one) -> e j one", one=1),
                            in_=hist_full.rearrange("(j e) one -> e j one", e=8)), 16)

                @blk.vector
                def _(v):
                    v.wait_ge(r_hl.h, 16)
                    r_off.inc(v.tensor_tensor(hm_sb[:], hf_sb[:], maskp_sb[:], op=ALU.mult))
                    v.wait_ge(r_off.h, 1)
                    r_off.inc(v.tensor_reduce(offc[:], hm_sb[:], op=ALU.add, axis=AX.X))
                    v.wait_ge(r_off.h, 2)
                    VE(v, v.tensor_scalar(offm1[:], pref[:], offc[:], None, op0=ALU.add))
                    r_off.inc(v.tensor_tensor(offmF[:], offm1[:], histco[:],
                                              op=ALU.subtract))
            M["offm"] = r_off.n

            with nc.Block() as blk:
                @blk.tensor
                def _(t):
                    t.wait_ge(r_off.h, M["offm"])
                    for i in range(16):
                        if i >= 2:
                            t.wait_ge(r_vB.h, 3 * i - 3)
                        r_mmB.inc(t.matmul(PB[i % 2][:, 0:1],
                                           ohT_all[:, 128 * i:128 * (i + 1)],
                                           offmF[:, i:i + 1], start=True, stop=True))

                @blk.vector
                def _(v):
                    for i in range(16):
                        v.wait_ge(r_mmB.h, i + 1)
                        r_vB.inc(v.tensor_tensor(junk8[:], oh_all[:, 8 * i:8 * i + 8],
                                                 pos_all[:, 8 * i:8 * i + 8], op=ALU.mult))
                        v.wait_ge(r_vB.h, r_vB.n)
                        r_vB.inc(v.tensor_reduce(r_c1[:], junk8[:], op=ALU.add, axis=AX.X))
                        v.wait_ge(r_vB.h, r_vB.n)
                        VE(v, v.tensor_tensor(r_c2[:], r_c1[:], PB[i % 2][:, 0:1],
                                              op=ALU.add))
                        VE(v, v.tensor_scalar(posd_sb[:, i:i + 1], r_c2[:], -1.0, None,
                                              op0=ALU.add))
                        VE(v, v.tensor_scalar(keep_sb[:, i:i + 1], posd_sb[:, i:i + 1],
                                              float(CAP), None, op0=ALU.is_lt))
                        VE(v, v.tensor_scalar(r_c1[:], eidx_sb[:, i:i + 1], float(CAP),
                                              None, op0=ALU.mult))
                        VE(v, v.tensor_tensor(slotf_sb[:, i:i + 1], r_c1[:],
                                              posd_sb[:, i:i + 1], op=ALU.add))
                        VE(v, v.tensor_scalar(r_c2[:], keep_sb[:, i:i + 1], -1.0, 1.0,
                                              op0=ALU.mult, op1=ALU.add))
                        VE(v, v.tensor_tensor(r_c1[:], slotf_sb[:, i:i + 1],
                                              keep_sb[:, i:i + 1], op=ALU.mult))
                        VE(v, v.scalar_tensor_tensor(r_c3[:], r_c2[:], SENT,
                                                     r_c1[:], op0=ALU.mult, op1=ALU.add))
                        VE(v, v.tensor_copy(slotu_sb[:, i:i + 1], r_c3[:]))
                        r_vB.inc(v.tensor_tensor(gs_sb[:, i:i + 1], gate_sb[:, i:i + 1],
                                                 keep_sb[:, i:i + 1], op=ALU.mult))
            M["route"] = r_vB.n

            bBX = [b_x[0].n, b_x[1].n]
            bXS = [x_sc[0].n, x_sc[1].n]
            with nc.Block() as blk:
                @blk.sync
                def _(s):
                    s.wait_ge(b_st[0].h, b_st[0].n)
                    s.wait_ge(b_st[1].h, b_st[1].n)
                    for i in range(16):
                        if i >= 2:
                            s.wait_ge(x_sc[i % 2].h, bXS[i % 2] + 16 * ((i - 2) // 2 + 1))
                        b_x[i % 2].inc(s.dma_start(
                            out=xnb_ev[i % 2],
                            in_=xn_tm_c[i * 128:(i + 1) * 128, :]), 16)

                @blk.gpsimd
                def _(g):
                    g.wait_ge(r_vB.h, M["route"])
                    g.wait_ge(g_gs.h, M["xs_zero_g"])
                    g.wait_ge(z_as.h, M["xs_zero_a"])
                    for i in range(16):
                        g.wait_ge(b_x[i % 2].h, bBX[i % 2] + 16 * (i // 2 + 1))
                        x_sc[i % 2].inc(g.indirect_dma_start(
                            out=xs_c,
                            out_offset=bass.IndirectOffsetOnAxis(slotu_sb[:, i:i + 1], 0),
                            in_=xnb_ev[i % 2], in_offset=None,
                            bounds_check=TTOK - 1, oob_is_err=False), 16)

            def emit_a2a3():
                with nc.Block() as blk:
                    @blk.gpsimd
                    def _(g):
                        g.wait_ge(x_sc[0].h, x_sc[0].n)
                        g.wait_ge(x_sc[1].h, x_sc[1].n)
                        with allow_ici_all_to_all():
                            cc.inc(g.collective_compute(
                                "AllToAll", ALU.bypass, replica_groups=[list(range(8))],
                                ins=[xs_c], outs=[xs_all]))
                M["cc_a2a3"] = cc.n
            if DP == 2:
                emit_a2a3()

        if DP >= 3:
            def fmb(par):
                return O_FM + par * 8192

            with nc.Block() as blk:
                @blk.vector
                def _(v):
                    p3_vm.inc(v.memset(ar(O_VAUG, VAUG_SZ), 1.0))

                @blk.sync
                def _(s):
                    s.wait_ge(cc.h, M["cc_ag1"])
                    s.wait_ge(f_cmb.h, M["fold_o"])
                    for tt in range(32):
                        r, tl = tt // 4, tt % 4
                        if tt >= 2:
                            s.wait_ge(p3_mmA.h, 4 * (tt - 1))
                            s.wait_ge(p3_mmV.h, 4 * (tt - 1))
                        for dcg in range(4):
                            src = xn_fm_full[r * 2048 + dcg * 512: r * 2048 + (dcg + 1) * 512,
                                             tl * 512:(tl + 1) * 512]
                            p3_fm[tt % 2].inc(s.dma_start(
                                out=ar(fmb(tt % 2) + dcg * 2048, 2048).rearrange(
                                    "p (dc t) -> p dc t", dc=4),
                                in_=src.rearrange("(dc p) t -> p dc t", p=128)), 16)
                        for g in range(4):
                            pidx = (tt - 1) * 4 + g
                            if pidx >= 0:
                                s.wait_ge(p3_ev.h, pidx + 1)
                                pg = pidx % 4
                                dst = Kfm if pg < 2 else Qfm
                                hl = pg % 2
                                ptt = pidx // 4
                                p3_ks[pidx % 4].inc(s.dma_start(
                                    out=dst[hl * 128:(hl + 1) * 128,
                                            ptt * 512:(ptt + 1) * 512],
                                    in_=kq_ev[pidx % 4][:]), 16)
                    for g in range(4):
                        pidx = 31 * 4 + g
                        s.wait_ge(p3_ev.h, pidx + 1)
                        pg = pidx % 4
                        dst = Kfm if pg < 2 else Qfm
                        hl = pg % 2
                        p3_ks[pidx % 4].inc(s.dma_start(
                            out=dst[hl * 128:(hl + 1) * 128, 31 * 512:32 * 512],
                            in_=kq_ev[pidx % 4][:]), 16)

                @blk.tensor
                def _(t):
                    t.wait_ge(f_cmb.h, M["fold_o"])
                    t.wait_ge(p3_vm.h, 1)
                    for tt in range(32):
                        t.wait_ge(p3_fm[tt % 2].h, 64 * (tt // 2 + 1))
                        for g in range(4):
                            idx = tt * 4 + g
                            w = "k" if g < 2 else "q"
                            hl = g % 2
                            if idx >= 2:
                                t.wait_ge(p3_ev.h, idx - 1)
                            for dc in range(NDC):
                                mmi = t.matmul(PA[g % 2][:, 0:512],
                                               weff(w, dc)[:, hl * 128:(hl + 1) * 128],
                                               ar(fmb(tt % 2) + dc * 512, 512),
                                               start=(dc == 0), stop=(dc == NDC - 1))
                            p3_mmA.inc(mmi)
                        for tc in range(4):
                            idx = tt * 4 + tc
                            if idx >= 2:
                                t.wait_ge(p3_vv.h, idx - 1)
                            for dc in range(NDC):
                                t.matmul(PB[tc % 2][:, 0:256],
                                         ar(fmb(tt % 2) + dc * 512 + tc * 128, 128),
                                         weff("v", dc), start=(dc == 0), stop=False)
                            p3_mmV.inc(t.matmul(PB[tc % 2][:, 0:256], ones_sb[:],
                                                bvr_sb[:], start=False, stop=True))

                @blk.vector
                def _(v):
                    for tt in range(32):
                        for g in range(4):
                            idx = tt * 4 + g
                            bcol = (bkc_sb if g < 2 else bqc_sb)[:, g % 2:g % 2 + 1]
                            v.wait_ge(p3_mmA.h, idx + 1)
                            if idx >= 4:
                                v.wait_ge(p3_ks[idx % 4].h, 16 * (idx // 4))
                            p3_ev.inc(v.tensor_scalar(kq_ev[idx % 4][:], PA[g % 2][:],
                                                      bcol, None, op0=ALU.add))
                        for tc in range(4):
                            idx = tt * 4 + tc
                            v.wait_ge(p3_mmV.h, idx + 1)
                            off = O_VAUG + idx * 258
                            VE(v, v.tensor_copy(ar(off, 128), PB[tc % 2][:, 0:128]))
                            p3_vv.inc(v.tensor_copy(ar(off + 129, 128),
                                                    PB[tc % 2][:, 128:256]))
            M["p3_mmA"] = p3_mmA.n
            M["p3_ks"] = [p3_ks[i].n for i in range(4)]
            if DP == 3:
                emit_a2a3()

        if DP >= 4:
            def kh(par):
                return O_KH + par * 4096

            def qp(par):
                return O_QP + par * 512

            def et(par):
                return O_ET if par % 2 == 0 else 73648

            kh1_thr = q_kh[1].n + 16
            with nc.Block() as blk:
                @blk.gpsimd
                def _(g, kh1_thr=kh1_thr):
                    g.wait_ge(q_kh[1].h, kh1_thr)
                    g.wait_ge(x_sc[0].h, x_sc[0].n)
                    g.wait_ge(x_sc[1].h, x_sc[1].n)
                    with allow_ici_all_to_all():
                        cc.inc(g.collective_compute(
                            "AllToAll", ALU.bypass, replica_groups=[list(range(8))],
                            ins=[xs_c], outs=[xs_all]))
                    for b2 in range(4):
                        g.wait_ge(q_av.h, 64 * (b2 + 1))
                        for half in range(2):
                            q_st.inc(g.dma_start(
                                out=att_c[b2 * 4096 + half * 2048:
                                          b2 * 4096 + (half + 1) * 2048, :]
                                    .rearrange("(q p) f -> p q f", p=128),
                                in_=ar(O_STG + half * 4096, 4096).rearrange(
                                    "p (q f) -> p q f", f=256)), 16)

                @blk.sync
                def _(s):
                    for i4 in range(4):
                        s.wait_ge(p3_ks[i4].h, M["p3_ks"][i4])
                    for ph in range(8):
                        b, hl = ph // 2, ph % 2
                        if ph >= 2:
                            s.wait_ge(q_mmS.h, 256 * (ph - 1))
                        q_kh[ph % 2].inc(s.dma_start(
                            out=ar(kh(ph % 2), 4096),
                            in_=Kfm[hl * 128:(hl + 1) * 128, b * 4096:(b + 1) * 4096]), 16)
                        for qb in range(8):
                            gqb = ph * 8 + qb
                            if gqb >= 2:
                                s.wait_ge(q_mmS.h, 32 * (gqb - 1))
                            q_qp[gqb % 2].inc(s.dma_start(
                                out=ar(qp(gqb % 2), 512),
                                in_=Qfm[hl * 128:(hl + 1) * 128,
                                        b * 4096 + qb * 512: b * 4096 + (qb + 1) * 512]), 16)

                @blk.tensor
                def _(t):
                    def avmm(g):
                        phg = g // 8
                        bg, hlg = phg // 2, phg % 2
                        t.wait_ge(q_ex.h, 32 * (g + 1))
                        for qs in range(4):
                            gqs = g * 4 + qs
                            if gqs >= 2:
                                t.wait_ge(q_av.h, gqs - 1)
                            for kc in range(32):
                                gtc = bg * 32 + kc
                                mmi = t.matmul(PB[qs % 2][:, 0:129],
                                               ar(et(g % 2) + kc * 512 + qs * 128, 128),
                                               ar(O_VAUG + gtc * 258 + hlg * 129, 129),
                                               start=(kc == 0), stop=(kc == 31))
                            q_mmA.inc(mmi)

                    for ph in range(8):
                        b, hl = ph // 2, ph % 2
                        t.wait_ge(q_kh[ph % 2].h, 16 * (ph // 2 + 1))
                        for qb in range(8):
                            gqb = ph * 8 + qb
                            t.wait_ge(q_qp[gqb % 2].h, 16 * (gqb // 2 + 1))
                            for kc in range(32):
                                gsc = gqb * 32 + kc
                                if gsc >= 4:
                                    t.wait_ge(q_ex.h, gsc - 3)
                                q_mmS.inc(t.matmul(PA[kc % 4][:],
                                                   ar(kh(ph % 2) + kc * 128, 128),
                                                   ar(qp(gqb % 2), 512),
                                                   start=True, stop=True))
                            if gqb >= 1:
                                avmm(gqb - 1)
                    avmm(63)

                @blk.scalar
                def _(sc):
                    for ph in range(8):
                        for qb in range(8):
                            gqb = ph * 8 + qb
                            for kc in range(32):
                                gsc = gqb * 32 + kc
                                sc.wait_ge(q_mmS.h, gsc + 1)
                                if kc == 0 and gqb >= 2:
                                    sc.wait_ge(q_mmA.h, 4 * (gqb - 1))
                                q_ex.inc(sc.activation(
                                    ar(et(gqb % 2) + kc * 512, 512), PA[kc % 4][:],
                                    AF.Exp, bias=csh_sb[:], scale=SCALE))

                @blk.vector
                def _(v):
                    for ph in range(8):
                        b, hl = ph // 2, ph % 2
                        for qb in range(8):
                            for qs in range(4):
                                gqs = (ph * 8 + qb) * 4 + qs
                                qspp = qb * 4 + qs
                                if hl == 0 and qspp == 0 and b >= 1:
                                    v.wait_ge(q_st.h, 32 * b)
                                v.wait_ge(q_mmA.h, gqs + 1)
                                q_rc.inc(v.reciprocal(rec_sb[:],
                                                      PB[qs % 2][:, 128:129]))
                                v.wait_ge(q_rc.h, q_rc.n)
                                q_av.inc(v.tensor_scalar(
                                    ar(O_STG + qspp * 256 + hl * 128, 128),
                                    PB[qs % 2][:, 0:128], rec_sb[:], None,
                                    op0=ALU.mult))
            M["p4_av"] = q_mmA.n
            M["p4_st"] = q_st.n
            M["cc_a2a3"] = cc.n

            with nc.Block() as blk:
                @blk.gpsimd
                def _(g):
                    g.wait_ge(q_st.h, M["p4_st"])
                    with allow_ici_all_to_all():
                        cc.inc(g.collective_compute(
                            "AllToAll", ALU.bypass, replica_groups=[list(range(8))],
                            ins=[att_c], outs=[att_all]))
            M["cc_a2a4"] = cc.n

        if DP >= 5:
            def emit_ffn(tag, w1src, w3src, w2src, out_dram, moe, prev_gate,
                         hook_after_h0=None):
                for half in range(2):
                    gate = prev_gate if half == 0 else M[f"{tag}_w2g_{half - 1}"]
                    bTR = n_tr.n
                    bTC = n_tc.n
                    bGA = [n_ga[0].n, n_ga[1].n]
                    bGX = n_gx.n
                    if moe:
                        bSL = [m_sl[0].n, m_sl[1].n]
                        bMG = m_mg.n
                        with nc.Block() as blk:
                            @blk.sync
                            def _(s, half=half, gate=gate, bSL=bSL, bMG=bMG):
                                s.wait_ge(cc.h, M["cc_a2a3"])
                                s.wait_ge(gate[0].h, gate[1])
                                for st_ in range(8):
                                    for j in range(8):
                                        gidx = st_ * 8 + j
                                        if gidx >= 2:
                                            s.wait_ge(m_mg.h, bMG + gidx - 1)
                                        m_sl[j % 2].inc(s.dma_start(
                                            out=xsh[j % 2],
                                            in_=xs_all[j * 2048 + half * 1024 + st_ * 128:
                                                       j * 2048 + half * 1024 + (st_ + 1) * 128,
                                                       :]), 16)

                            @blk.vector
                            def _(v, half=half, gate=gate, bSL=bSL, bTR=bTR):
                                v.wait_ge(gate[0].h, gate[1])

                                def copies(st2):
                                    for dc in range(NDC):
                                        g2 = st2 * 16 + dc
                                        v.wait_ge(n_tr.h, bTR + g2 + 1)
                                        n_tc.inc(v.tensor_copy(
                                            ar(O_XD + dc * 1024 + st2 * 128, 128),
                                            PT[dc % 2][:]))

                                for st_ in range(8):
                                    for j in range(8):
                                        v.wait_ge(m_sl[j % 2].h,
                                                  bSL[j % 2] + 16 * (st_ * 4 + j // 2 + 1))
                                        if j == 0:
                                            if st_ >= 2:
                                                v.wait_ge(n_tr.h, bTR + 16 * (st_ - 1))
                                            ii5 = v.tensor_copy(accS, xsh[0])
                                        else:
                                            dst = xdtm[st_ % 2] if j % 2 == 1 else accS
                                            srcacc = accS if j % 2 == 1 else xdtm[st_ % 2]
                                            ii5 = v.tensor_tensor(dst, srcacc,
                                                                  xsh[j % 2], op=ALU.add)
                                        m_mg.inc(ii5)
                                        v.wait_ge(m_mg.h, m_mg.n)
                                    if st_ >= 1:
                                        copies(st_ - 1)
                                copies(7)

                            @blk.tensor
                            def _(t, gate=gate, bMG=bMG, bTC=bTC):
                                t.wait_ge(gate[0].h, gate[1])
                                for st_ in range(8):
                                    t.wait_ge(m_mg.h, bMG + 8 * (st_ + 1))
                                    for dc in range(NDC):
                                        g2 = st_ * 16 + dc
                                        if g2 >= 2:
                                            t.wait_ge(n_tc.h, bTC + g2 - 1)
                                        n_tr.inc(t.transpose(
                                            PT[dc % 2][:],
                                            xdtm[st_ % 2][:, dc * 128:(dc + 1) * 128],
                                            ident_sb[:]))
                        M[f"{tag}_xd_{half}"] = ("v", n_tc.n)
                    else:
                        with nc.Block() as blk:
                            @blk.sync
                            def _(s, half=half, gate=gate):
                                s.wait_ge(gate[0].h, gate[1])
                                s.wait_ge(b_sf[0].h, b_sf[0].n)
                                s.wait_ge(b_sf[1].h, b_sf[1].n)
                                for dcg in range(4):
                                    n_gx.inc(s.dma_start(
                                        out=ar(O_XD + dcg * 4096, 4096).rearrange(
                                            "p (dc t) -> p dc t", dc=4),
                                        in_=xn_fm_c.rearrange("(dc p) t -> p dc t", p=128)
                                            [:, dcg * 4:(dcg + 1) * 4,
                                             half * 1024:(half + 1) * 1024]), 16)
                        M[f"{tag}_xd_{half}"] = ("s", n_gx.n)

                    if tag == "sh" and "shh" in KCUT:
                        M[f"{tag}_h_{half}"] = n_ht.n
                        M[f"{tag}_w2g_{half}"] = (n_gx, n_gx.n)
                        continue
                    bW1 = [n_w1[0].n, n_w1[1].n]
                    bHM = n_hm.n
                    bSI = n_si.n
                    bHT = n_ht.n
                    with nc.Block() as blk:
                        @blk.gpsimd
                        def _(g, w1src=w1src, w3src=w3src, gate=gate, bHT=bHT):
                            g.wait_ge(gate[0].h, gate[1])
                            for hc in range(NHC):
                                if hc >= 2:
                                    g.wait_ge(n_ht.h, bHT + 2 * (hc - 1))
                                for wj, wsrc in ((0, w1src), (1, w3src)):
                                    for dcg in range(4):
                                        n_w1[hc % 2].inc(g.dma_start(
                                            out=ar(O_WB + (hc % 2) * 4096 + wj * 2048
                                                   + dcg * 512, 512).rearrange(
                                                "p (dc h) -> p dc h", dc=4),
                                            in_=wsrc[hc, dcg * 4:(dcg + 1) * 4].rearrange(
                                                "dc p h -> p dc h")), 16)

                        @blk.tensor
                        def _(t, half=half, tag=tag, bW1=bW1, bHT=bHT):
                            kind, val = M[f"{tag}_xd_{half}"]
                            t.wait_ge((n_tc if kind == "v" else n_gx).h, val)
                            for hc in range(NHC):
                                t.wait_ge(n_w1[hc % 2].h, bW1[hc % 2] + 128 * (hc // 2 + 1))
                                for t2 in range(2):
                                    hidx = hc * 2 + t2
                                    if hidx >= 2:
                                        t.wait_ge(n_ht.h, bHT + hidx - 1)
                                    for wj in range(2):
                                        dst = PA[t2 % 2] if wj == 0 else PA[2 + t2 % 2]
                                        for dc in range(NDC):
                                            mmi = t.matmul(dst[:],
                                                           ar(O_WB + (hc % 2) * 4096
                                                              + wj * 2048 + dc * 128, 128),
                                                           ar(O_XD + dc * 1024 + t2 * 512, 512),
                                                           start=(dc == 0),
                                                           stop=(dc == NDC - 1))
                                    n_hm.inc(mmi)

                        @blk.scalar
                        def _(sc, bHM=bHM, bHT=bHT):
                            for hc in range(NHC):
                                for t2 in range(2):
                                    hidx = hc * 2 + t2
                                    sc.wait_ge(n_hm.h, bHM + hidx + 1)
                                    if hidx >= 2:
                                        sc.wait_ge(n_ht.h, bHT + hidx - 1)
                                    n_si.inc(sc.activation(s1_ev[t2 % 2][:],
                                                           PA[t2 % 2][:], AF.Silu))

                        @blk.vector
                        def _(v, bSI=bSI):
                            for hc in range(NHC):
                                for t2 in range(2):
                                    hidx = hc * 2 + t2
                                    v.wait_ge(n_si.h, bSI + hidx + 1)
                                    n_ht.inc(v.tensor_tensor(
                                        ar(O_HH + hc * 1024 + t2 * 512, 512),
                                        s1_ev[t2 % 2][:], PA[2 + t2 % 2][:],
                                        op=ALU.mult))

                        if DP == 5 and moe and half == 0:
                            @blk.sync
                            def _(s, bHT=bHT, half=half, tag=tag):
                                kind, val = M[f"{tag}_xd_{half}"]
                                s.wait_ge((n_tc if kind == "v" else n_gx).h, val)
                                for dc in range(NDC):
                                    dx_s.inc(s.dma_start(
                                        out=dbg["d_xd"][dc * 128:(dc + 1) * 128, :],
                                        in_=ar(O_XD + dc * 1024, 1024)), 16)
                                s.wait_ge(n_ht.h, bHT + 2 * NHC)
                                for hc in range(NHC):
                                    dx_s.inc(s.dma_start(
                                        out=dbg["d_hh"][hc * 128:(hc + 1) * 128, :],
                                        in_=ar(O_HH + hc * 1024, 1024)), 16)
                    M[f"{tag}_h_{half}"] = n_ht.n
                    if half == 0 and hook_after_h0 is not None:
                        hook_after_h0()

                    if tag == "sh" and "shw2" in KCUT:
                        M[f"{tag}_w2g_{half}"] = (n_ht, n_ht.n)
                        continue
                    for oh in range(2):
                        bW2 = n_w2.n
                        bM2 = n_m2.n
                        bOE = n_oe.n
                        bOS = [n_os[0].n, n_os[1].n]
                        with nc.Block() as blk:
                            @blk.tensor
                            def _(t, bW2=bW2, bOE=bOE):
                                t.wait_ge(n_w2.h, bW2 + 16 * NHC)
                                for stl in range(8):
                                    for otl in range(2):
                                        idx = stl * 2 + otl
                                        if idx >= 2:
                                            t.wait_ge(n_oe.h, bOE + idx - 1)
                                        for hc in range(NHC):
                                            mmi = t.matmul(PA[idx % 2][:],
                                                           ar(O_HH + hc * 1024 + stl * 128, 128),
                                                           ar(O_W2 + hc * 1024 + otl * 512, 512),
                                                           start=(hc == 0),
                                                           stop=(hc == NHC - 1))
                                        n_m2.inc(mmi)

                            @blk.vector
                            def _(v, bM2=bM2, bOS=bOS):
                                for stl in range(8):
                                    for otl in range(2):
                                        idx = stl * 2 + otl
                                        v.wait_ge(n_m2.h, bM2 + idx + 1)
                                        if idx >= 2:
                                            v.wait_ge(n_os[idx % 2].h,
                                                      bOS[idx % 2] + 16 * (idx // 2))
                                        n_oe.inc(v.tensor_copy(oe_ev[idx % 2][:],
                                                               PA[idx % 2][:]))

                            @blk.gpsimd
                            def _(g, w2src=w2src, tag=tag, half=half, oh=oh, bM2=bM2):
                                g.wait_ge(n_ht.h, M[f"{tag}_h_{half}"])
                                if DP == 5 and moe and half == 0:
                                    g.wait_ge(dx_s.h, 16 * (NDC + NHC))
                                if oh == 1:
                                    g.wait_ge(n_m2.h, bM2)
                                for hc in range(NHC):
                                    n_w2.inc(g.dma_start(
                                        out=ar(O_W2 + hc * 1024, 1024),
                                        in_=w2src[hc * 128:(hc + 1) * 128,
                                                  oh * 1024:(oh + 1) * 1024]), 16)

                            @blk.sync
                            def _(s, out_dram=out_dram, half=half, oh=oh, bOE=bOE):
                                for stl in range(8):
                                    for otl in range(2):
                                        idx = stl * 2 + otl
                                        s.wait_ge(n_oe.h, bOE + idx + 1)
                                        n_os[idx % 2].inc(s.dma_start(
                                            out=out_dram[half * 1024 + stl * 128:
                                                         half * 1024 + (stl + 1) * 128,
                                                         oh * 1024 + otl * 512:
                                                         oh * 1024 + (otl + 1) * 512],
                                            in_=oe_ev[idx % 2][:]), 16)
                    M[f"{tag}_w2g_{half}"] = (n_m2, n_m2.n)
                M[f"{tag}_st"] = (n_os[0].n, n_os[1].n)

            if DP >= 4:
                first_gate = (q_mmA, M["p4_av"])
            else:
                first_gate = (p3_mmA, M.get("p3_mmA", 0)) if DP >= 3 else (cc, 0)
            emit_ffn("moe", W1t, W3t, W2d, oe_c, True, first_gate)

            def emit_ag5():
                with nc.Block() as blk:
                    @blk.gpsimd
                    def _(g):
                        g.wait_ge(n_os[0].h, M["moe_st"][0])
                        g.wait_ge(n_os[1].h, M["moe_st"][1])
                        cc.inc(g.collective_compute(
                            "AllGather", ALU.bypass, replica_groups=[list(range(8))],
                            ins=[oe_c], outs=[oe_full]))
                M["cc_ag5"] = cc.n
            if DP == 5:
                emit_ag5()

        if DP >= 6:
            emit_ffn("sh", sw1t, sw3t, sw2d, shared_my, False, M["moe_w2g_1"],
                     hook_after_h0=emit_ag5)

        if DP >= 7:
            with nc.Block() as blk:
                @blk.sync
                def _(s):
                    s.wait_ge(cc.h, M["cc_a2a4"])
                    g = M["sh_w2g_1"]
                    s.wait_ge(g[0].h, g[1])
                    for fc in range(16):
                        j, hl = fc // 2, fc % 2
                        o_af.inc(s.dma_start(
                            out=ar(O_AFM + fc * 2048, 2048),
                            in_=att_all[j * 2048:(j + 1) * 2048,
                                        hl * 128:(hl + 1) * 128],
                            transpose=True), 16)
                    s.wait_ge(f_wst[0].h, M["woTe_w0"])
                    s.wait_ge(f_wst[1].h, M["woTe_w1"])
                    for fc in range(16):
                        o_wo.inc(s.dma_start(
                            out=ar(O_WOE + fc * 2048, 2048),
                            in_=WoTe[fc * 128:(fc + 1) * 128, :]), 16)
                    for tc in range(16):
                        if tc >= 2:
                            s.wait_ge(o_fr6.h, tc - 1)
                        o_x[tc % 2].inc(s.dma_start(
                            out=xt_f32[tc % 2], in_=x_my[tc * 128:(tc + 1) * 128, :]), 16)
                        o_sh[tc % 2].inc(s.dma_start(
                            out=shbuf[tc % 2][:],
                            in_=shared_my[tc * 128:(tc + 1) * 128, :]), 16)
                        for ot in range(4):
                            idx = tc * 4 + ot
                            s.wait_ge(o_ev.h, idx + 1)
                            o_st[idx % 2].inc(s.dma_start(
                                out=out_my[tc * 128:(tc + 1) * 128,
                                           ot * 512:(ot + 1) * 512],
                                in_=out_ev[idx % 2]), 16)

                @blk.gpsimd
                def _(g):
                    g.wait_ge(cc.h, M["cc_ag5"])
                    for tc in range(16):
                        if tc >= 2:
                            g.wait_ge(o_fr5.h, tc - 1)
                        g.memset(moebuf[tc % 2][:], 0.0)
                        o_mg[tc % 2].inc(g.indirect_dma_start(
                            out=moebuf[tc % 2][:], out_offset=None,
                            in_=oe_full,
                            in_offset=bass.IndirectOffsetOnAxis(slotu_sb[:, tc:tc + 1], 0),
                            bounds_check=NCORE * CAP - 1, oob_is_err=False), 16)

                @blk.tensor
                def _(t):
                    t.wait_ge(o_af.h, 16 * 16)
                    t.wait_ge(o_wo.h, 16 * 16)
                    for tc in range(16):
                        for ot in range(4):
                            idx = tc * 4 + ot
                            if idx >= 2:
                                t.wait_ge(o_ev.h, idx - 1)
                            for fc in range(16):
                                t.matmul(PA[idx % 2][:],
                                         ar(O_AFM + fc * 2048 + tc * 128, 128),
                                         ar(O_WOE + fc * 2048 + ot * 512, 512),
                                         start=(fc == 0), stop=False)
                            o_mm.inc(t.matmul(PA[idx % 2][:], ones_sb[:],
                                              bor_sb[:, ot * 512:(ot + 1) * 512],
                                              start=False, stop=True))

                @blk.vector
                def _(v):
                    for tc in range(16):
                        v.wait_ge(o_mg[tc % 2].h, 16 * (tc // 2 + 1))
                        v.wait_ge(o_x[tc % 2].h, 16 * (tc // 2 + 1))
                        v.wait_ge(o_sh[tc % 2].h, 16 * (tc // 2 + 1))
                        for ot in range(4):
                            idx = tc * 4 + ot
                            v.wait_ge(o_mm.h, idx + 1)
                            if idx >= 2:
                                v.wait_ge(o_st[idx % 2].h, 16 * (idx // 2))
                            VE(v, v.scalar_tensor_tensor(
                                tmp1_f, moebuf[tc % 2][:, ot * 512:(ot + 1) * 512],
                                gs_sb[:, tc:tc + 1], PA[idx % 2][:],
                                op0=ALU.mult, op1=ALU.add))
                            VE(v, v.tensor_tensor(
                                tmp2_f, tmp1_f,
                                xt_f32[tc % 2][:, ot * 512:(ot + 1) * 512], op=ALU.add))
                            o_ev.inc(v.tensor_tensor(
                                out_ev[idx % 2], tmp2_f,
                                shbuf[tc % 2][:, ot * 512:(ot + 1) * 512], op=ALU.add))
                        o_fr5.inc(v.nop())
                        o_fr6.inc(v.nop())

        with nc.Block() as blk:
            @blk.sync
            def _(s):
                if DP < 99:
                    if DP >= 1:
                        s.wait_ge(cc.h, M["cc_ag1"])
                        s.wait_ge(b_st[0].h, b_st[0].n)
                        s.wait_ge(b_st[1].h, b_st[1].n)
                        d_st.inc(s.dma_start(out=dbg["d_xn_tm"], in_=xn_tm_c), 16)
                        d_st.inc(s.dma_start(out=dbg["d_xn_fm"], in_=xn_fm_full), 16)
                    if DP >= 2:
                        s.wait_ge(cc.h, M["cc_a2a3"])
                        d_st.inc(s.dma_start(out=dbg["d_xs"], in_=xs_all), 16)
                        if DP == 2:
                            d_st.inc(s.dma_start(out=dbg["d_xsc"], in_=xs_c), 16)
                            d_st.inc(s.dma_start(out=dbg["d_slotu"], in_=slotu_sb[:]), 16)
                            d_st.inc(s.dma_start(out=dbg["d_eq"], in_=logits_all2[:]), 16)
                            d_st.inc(s.dma_start(out=dbg["d_vl"], in_=logits_all3[:]), 16)
                            d_st.inc(s.dma_start(out=dbg["d_nm"][:, 0:16], in_=nm_all[:]), 16)
                            d_st.inc(s.dma_start(out=dbg["d_nm"][:, 16:32], in_=pmax_all[:]), 16)
                        for k, srcb in enumerate([eidx_sb, gate_sb, posd_sb, keep_sb,
                                                  slotf_sb, gs_sb]):
                            d_st.inc(s.dma_start(out=dbg["d_route"][:, k * 16:(k + 1) * 16],
                                                 in_=srcb[:]), 16)
                    if DP >= 3:
                        for i4 in range(4):
                            s.wait_ge(p3_ks[i4].h, M["p3_ks"][i4])
                        d_st.inc(s.dma_start(out=dbg["d_Kfm"], in_=Kfm), 16)
                        d_st.inc(s.dma_start(out=dbg["d_Qfm"], in_=Qfm), 16)
                    if DP >= 4:
                        s.wait_ge(cc.h, M["cc_a2a4"])
                        d_st.inc(s.dma_start(out=dbg["d_att"], in_=att_all), 16)
                    if DP >= 5:
                        s.wait_ge(cc.h, M["cc_ag5"])
                        d_st.inc(s.dma_start(out=dbg["d_oe"], in_=oe_full), 16)
                    if DP >= 6:
                        s.wait_ge(n_os[0].h, M["sh_st"][0])
                        s.wait_ge(n_os[1].h, M["sh_st"][1])
                        if "shdump" not in KCUT:
                            d_st.inc(s.dma_start(out=dbg["d_shared"], in_=shared_my), 16)
                    s.wait_ge(d_st.h, d_st.n)
                if DP >= 7:
                    s.wait_ge(o_st[0].h, o_st[0].n)
                    s.wait_ge(o_st[1].h, o_st[1].n)

            @blk.gpsimd
            def _(g):
                g.wait_ge(g_gs.h, g_gs.n)

    return nc, dbg


def _tile_w(W):
    return np.ascontiguousarray(
        np.asarray(W).reshape(NDC, 128, NHC, 128).transpose(2, 0, 1, 3))


def _prep_inputs(inputs):
    f32 = np.float32
    x = np.ascontiguousarray(np.asarray(inputs["L0_attn_x"], f32).reshape(TTOK, D))
    WT = {w: np.ascontiguousarray(np.asarray(inputs[f"L0_attn_W{w}"], f32).T)
          for w in "qkvo"}
    A_T = {w: np.ascontiguousarray(np.asarray(inputs[f"L0_attn_A{w}"], f32).T)
           for w in "qkvo"}
    Bm = {w: np.ascontiguousarray(np.asarray(inputs[f"L0_attn_B{w}"], f32))
          for w in "qkvo"}
    bias = {w: np.asarray(inputs[f"L0_attn_b{w}"], f32) for w in "qkvo"}
    Wg = np.ascontiguousarray(np.asarray(inputs["L0_ffn_Wg"], f32))
    W1 = np.asarray(inputs["L0_ffn_W1"], f32)
    W3 = np.asarray(inputs["L0_ffn_W3"], f32)
    W2 = np.asarray(inputs["L0_ffn_W2"], f32)
    sw1t = _tile_w(inputs["L0_ffn_sw1"])
    sw3t = _tile_w(inputs["L0_ffn_sw3"])
    sw2 = np.ascontiguousarray(np.asarray(inputs["L0_ffn_sw2"], f32))

    bf = ml_dtypes.bfloat16
    consts = {
        "ident": np.eye(128, dtype=f32).astype(bf),
        "ident32": np.eye(128, dtype=f32),
        "TRI": np.triu(np.ones((128, 128), f32)).astype(bf),
        "iota8": np.tile(np.arange(8, dtype=f32), (128, 1)),
        "iota1k": np.tile(np.arange(8, dtype=f32) + 1000.0, (128, 1)),
        "onesrow": np.ones((1, 128), f32).astype(bf),
        "ones816": np.ones((8, 16), f32),
    }

    in_maps = []
    for c in range(NCORE):
        fsl = slice(256 * c, 256 * (c + 1))
        m = {
            "x_my": np.ascontiguousarray(x[c * TOWN:(c + 1) * TOWN]),
            "WqT": np.ascontiguousarray(WT["q"][:, fsl]),
            "WkT": np.ascontiguousarray(WT["k"][:, fsl]),
            "WvT": np.ascontiguousarray(WT["v"][:, fsl]),
            "WoT": WT["o"],
            "Bq": Bm["q"], "Bk": Bm["k"], "Bv": Bm["v"], "Bo": Bm["o"],
            "AqT": np.ascontiguousarray(A_T["q"][:, fsl]),
            "AkT": np.ascontiguousarray(A_T["k"][:, fsl]),
            "AvT": np.ascontiguousarray(A_T["v"][:, fsl]),
            "AoT": A_T["o"],
            "bq_c": np.ascontiguousarray(bias["q"][fsl].reshape(2, 128).T),
            "bk_c": np.ascontiguousarray(bias["k"][fsl].reshape(2, 128).T),
            "bv_r": np.ascontiguousarray(bias["v"][fsl].reshape(1, 256)),
            "bo_r": np.ascontiguousarray(bias["o"].reshape(1, D)),
            "Wg": Wg,
            "W1t": _tile_w(W1[c]),
            "W3t": _tile_w(W3[c]),
            "W2d": np.ascontiguousarray(W2[c]),
            "sw1t": sw1t, "sw3t": sw3t, "sw2d": sw2,
            "maskp": np.tile((np.arange(8, dtype=f32) < c).astype(f32), (8, 1)),
            "tok1": (c * TOWN + 1 + np.arange(128, dtype=f32)[:, None]
                     + 128.0 * np.arange(16, dtype=f32)[None, :]),
        }
        m.update(consts)
        in_maps.append(m)
    return in_maps


def kernel(**inputs):
    in_maps = _prep_inputs(inputs)
    nc, _ = build_nc(DP=99)
    res = run_bass_kernel_spmd(nc, in_maps, core_ids=list(range(NCORE)))
    out = np.concatenate([np.asarray(r["out_my"], np.float32) for r in res.results],
                         axis=0)
    return out.reshape(4, 4096, 2048)


# version 17 (final)
# speedup vs baseline: 1.1741x; 1.0124x over previous
import os
import numpy as np
import ml_dtypes
import concourse.bass as bass
import concourse.mybir as mybir
from concourse.bass_utils import run_bass_kernel_spmd
from concourse.aot_env import allow_ici_all_to_all
from contextlib import ExitStack

F32 = mybir.dt.float32
BF16 = mybir.dt.bfloat16
U32 = mybir.dt.uint32
AF = mybir.ActivationFunctionType
ALU = mybir.AluOpType
AX = mybir.AxisListType

D = 2048
TTOK = 16384
TOWN = 2048
HFF = 2816
NCORE = 8
CAP = 2048
SCALE = 128.0 ** -0.5
LORA_S = 0.5
EPS = 1e-5
CSHIFT = 4.0
SENT = 1.0e9
NDC = 16
NHC = 22


class Sem:
    def __init__(self, h):
        self.h = h
        self.n = 0

    def inc(self, instr, k=1):
        instr.then_inc(self.h, k)
        self.n += k
        return self.n


def build_nc(DP=99):
    SUB = int(os.environ.get("KSUB", "9"))
    KCUT = set(x for x in os.environ.get("KCUT", "").split(",") if x)
    nc = bass.Bass("TRN2", target_bir_lowering=False, debug=False,
                   dynamic_dma_scratch_size=16384,
                   detect_race_conditions=False)

    def din(name, shape, dt=F32):
        return nc.dram_tensor(name, shape, dt, kind="ExternalInput").ap()

    def dout(name, shape, dt=F32):
        return nc.dram_tensor(name, shape, dt, kind="ExternalOutput").ap()

    def dtmp(name, shape, dt):
        return nc.dram_tensor(name, shape, dt).ap()

    x_my = din("x_my", [TOWN, D])
    WqT = din("WqT", [D, 256]); WkT = din("WkT", [D, 256]); WvT = din("WvT", [D, 256])
    WoT = din("WoT", [D, D])
    Bq = din("Bq", [16, D]); Bk = din("Bk", [16, D]); Bv = din("Bv", [16, D]); Bo = din("Bo", [16, D])
    AqT = din("AqT", [16, 256]); AkT = din("AkT", [16, 256]); AvT = din("AvT", [16, 256])
    AoT = din("AoT", [16, D])
    bq_c = din("bq_c", [128, 2]); bk_c = din("bk_c", [128, 2])
    bv_r = din("bv_r", [1, 256]); bo_r = din("bo_r", [1, D])
    Wg = din("Wg", [D, 8])
    W1t = din("W1t", [NHC, NDC, 128, 128])
    W3t = din("W3t", [NHC, NDC, 128, 128])
    W2d = din("W2d", [HFF, D])
    sw1t = din("sw1t", [NHC, NDC, 128, 128])
    sw3t = din("sw3t", [NHC, NDC, 128, 128])
    sw2d = din("sw2d", [HFF, D])
    ident = din("ident", [128, 128], BF16)
    ident32 = din("ident32", [128, 128], F32)
    TRI = din("TRI", [128, 128], BF16)
    iota8 = din("iota8", [128, 8])
    iota1k = din("iota1k", [128, 8])
    onesrow = din("onesrow", [1, 128], BF16)
    ones816 = din("ones816", [8, 16])
    maskp = din("maskp", [8, 8])
    tok1 = din("tok1", [128, 16])

    out_my = dout("out_my", [TOWN, D])

    xn_tm_c = dtmp("xn_tm_c", [TOWN, D], BF16)
    xn_fm_c = dtmp("xn_fm_c", [D, TOWN], BF16)
    xn_fm_full = dtmp("xn_fm_full", [NCORE * D, TOWN], BF16)
    hist_c = dtmp("hist_c", [8, 1], F32)
    hist_full = dtmp("hist_full", [64, 1], F32)
    xs_c = dtmp("xs_c", [TTOK, D], BF16)
    xs_all = dtmp("xs_all", [TTOK, D], BF16)
    Kfm = dtmp("Kfm", [256, TTOK], BF16)
    Qfm = dtmp("Qfm", [256, TTOK], BF16)
    att_c = dtmp("att_c", [TTOK, 256], BF16)
    att_all = dtmp("att_all", [TTOK, 256], BF16)
    WoTe = dtmp("WoTe", [D, D], BF16)
    oe_c = dtmp("oe_c", [CAP, D], BF16)
    oe_full = dtmp("oe_full", [NCORE * CAP, D], BF16)
    shared_my = dtmp("shared_my", [TOWN, D], BF16)

    dbg = {}
    if DP < 99:
        if DP >= 1:
            dbg["d_xn_tm"] = dout("d_xn_tm", [TOWN, D], BF16)
            dbg["d_xn_fm"] = dout("d_xn_fm", [NCORE * D, TOWN], BF16)
        if DP >= 2:
            dbg["d_route"] = dout("d_route", [128, 16 * 6])
            dbg["d_xs"] = dout("d_xs", [TTOK, D], BF16)
            if DP == 2:
                dbg["d_xsc"] = dout("d_xsc", [TTOK, D], BF16)
                dbg["d_slotu"] = dout("d_slotu", [128, 16], U32)
                dbg["d_eq"] = dout("d_eq", [128, 128])
                dbg["d_vl"] = dout("d_vl", [128, 128])
                dbg["d_nm"] = dout("d_nm", [128, 32])
        if DP >= 3:
            dbg["d_Kfm"] = dout("d_Kfm", [256, TTOK], BF16)
            dbg["d_Qfm"] = dout("d_Qfm", [256, TTOK], BF16)
        if DP >= 4:
            dbg["d_att"] = dout("d_att", [TTOK, 256], BF16)
        if DP >= 5:
            dbg["d_oe"] = dout("d_oe", [NCORE * CAP, D], BF16)
        if DP == 5:
            dbg["d_xd"] = dout("d_xd", [2048, 1024], BF16)
            dbg["d_hh"] = dout("d_hh", [NHC * 128, 1024], BF16)
        if DP >= 6:
            dbg["d_shared"] = dout("d_shared", [TOWN, D], BF16)

    with ExitStack() as stk:
        def sb(name, shape, dt):
            return stk.enter_context(nc.sbuf_tensor(name, shape, dt))

        def psumt(name, shape, dt):
            return stk.enter_context(nc.psum_tensor(name, shape, dt))

        def S(name):
            return Sem(stk.enter_context(nc.semaphore(name)))

        AR = 90032
        arena = sb("arena", [128, AR], BF16)
        f32ar = sb("f32ar", [128, 6144], F32)
        ident_sb = sb("ident_sb", [128, 128], BF16)
        id32_sb = sb("id32_sb", [128, 128], F32)
        tri_sb = sb("tri_sb", [128, 128], BF16)
        iota8_sb = sb("iota8_sb", [128, 8], F32)
        iota1k_sb = sb("iota1k_sb", [128, 8], F32)
        ones_sb = sb("ones_sb", [1, 128], BF16)
        ones816_sb = sb("ones816_sb", [8, 16], F32)
        wg_sb = sb("wg_sb", [128, NDC * 8], BF16)
        bqc_sb = sb("bqc_sb", [128, 2], F32)
        bkc_sb = sb("bkc_sb", [128, 2], F32)
        bvr_sb = sb("bvr_sb", [1, 256], BF16)
        bor_sb = sb("bor_sb", [1, D], BF16)
        maskp_sb = sb("maskp_sb", [8, 8], F32)
        tok1_sb = sb("tok1_sb", [128, 16], F32)
        mean_all = sb("mean_all", [128, 16], F32)
        var_all = sb("var_all", [128, 16], F32)
        sd_all = sb("sd_all", [128, 16], F32)
        rstd_all = sb("rstd_all", [128, 16], F32)
        logits_all = sb("logits_all", [128, 128], F32)
        nm_all = sb("nm_all", [128, 16], F32)
        pmax_all = sb("pmax_all", [128, 16], F32)
        se_all = sb("se_all", [128, 16], F32)
        gate_sb = sb("gate_sb", [128, 16], F32)
        gs_sb = sb("gs_sb", [128, 16], F32)
        slotu_sb = sb("slotu_sb", [128, 16], U32)
        slotf_sb = sb("slotf_sb", [128, 16], F32)
        eidx_sb = sb("eidx_sb", [128, 16], F32)
        keep_sb = sb("keep_sb", [128, 16], F32)
        posd_sb = sb("posd_sb", [128, 16], F32)
        oh_all = sb("oh_all", [128, 128], BF16)
        ohf_sb = sb("ohf_sb", [128, 8], F32)
        rec_sb = sb("rec_sb", [128, 1], F32)
        csh_sb = sb("csh_sb", [128, 1], F32)

        xt_f32 = [f32ar[:, 0:2048], f32ar[:, 2048:4096]]
        out_ev = [f32ar[:, 4096 + i * 512: 4096 + (i + 1) * 512] for i in range(2)]
        tmp1_f = f32ar[:, 5120:5632]
        tmp2_f = f32ar[:, 5632:6144]
        ohT_all = f32ar[0:8, 0:2048]
        logits_all2 = f32ar[:, 2048:2176]
        logits_all3 = f32ar[:, 2176:2304]
        pos_all = f32ar[:, 2304:2432]
        junk8 = f32ar[:, 2432:2440]
        junk8b = f32ar[:, 2440:2448]
        histco = f32ar[0:8, 2448:2464]
        pref = f32ar[0:8, 2464:2480]
        offm1 = f32ar[0:8, 2480:2496]
        offmF = f32ar[0:8, 2496:2512]
        offc = f32ar[0:8, 2512:2513]
        hf_sb = f32ar[0:8, 2520:2528]
        hm_sb = f32ar[0:8, 2528:2536]
        r_c1 = f32ar[:, 2536:2537]
        stat6 = f32ar[:, 4096:4120]
        stat2 = f32ar[:, 4120:4122]
        r_c2 = f32ar[:, 2544:2545]
        r_c3 = f32ar[:, 2552:2553]

        O_VAUG = 0
        VAUG_SZ = 128 * 258
        O_FM = VAUG_SZ
        O_ET = VAUG_SZ
        O_KH = O_ET + 16384
        O_QP = O_KH + 8192
        O_STG = O_QP + 1024
        O_HH = 0
        O_XD = 22528
        O_WB = O_XD + 16384
        O_W2 = 22528
        O_AFM = 0
        O_WOE = 32768

        def ar(o, n):
            return arena[:, o:o + n]

        xnb_ev = [ar(33024, 2048), ar(35072, 2048)]
        fmstage = ar(37120, 4096)
        lor_b = arena[0:16, 41216:43264]
        aot_b = arena[0:16, 43264:45312]
        at_b = arena[0:16, 45312:45568]
        raw_b = [ar(45568, 256), ar(45824, 256)]
        rawo_b = [ar(61696, 2048), ar(63744, 2048)]
        woe_b = [ar(65792, 2048), ar(67840, 2048)]
        O_WEFF = 49408

        def weff(w, dc):
            o = O_WEFF + ({"q": 0, "k": 1, "v": 2}[w] * NDC + dc) * 256
            return arena[:, o:o + 256]

        kq_ev = [ar(69888 + i * 512, 512) for i in range(4)]
        xdtm = [ar(47104, 2048), ar(49152, 2048)]
        xsh = [ar(51200, 2048), ar(53248, 2048)]
        accS = ar(57344, 2048)
        s1_ev = [ar(55296, 512), ar(55808, 512)]
        oe_ev = [ar(56320, 512), ar(56832, 512)]
        moebuf = [ar(65536, 2048), ar(67584, 2048)]
        shbuf = [ar(69632, 2048), ar(71680, 2048)]

        PA = [psumt(f"PA{i}", [128, 512], F32) for i in range(4)]
        PB = [psumt(f"PB{i}", [128, 512], F32) for i in range(2)]
        PT_full = [psumt(f"PT{i}", [128, 1024], BF16) for i in range(2)]
        PT = [t[:, 0:128] for t in PT_full]

        cc = S("cc")
        g0 = S("g0")
        f_ba = S("f_ba"); f_raw = [S("f_raw0"), S("f_raw1")]
        f_mm = S("f_mm"); f_cmb = S("f_cmb")
        f_wst = [S("f_wst0"), S("f_wst1")]
        a_x = [S("a_x0"), S("a_x1")]; a_fr = S("a_fr"); a_ms = S("a_ms")
        m_sq = S("m_sq"); m_rs = S("m_rs")
        b_x = [S("b_x0"), S("b_x1")]; b_fr = S("b_fr"); b_xn = S("b_xn")
        b_fmc = S("b_fmc"); b_tr = S("b_tr"); b_lg = S("b_lg"); b_le = S("b_le")
        b_st = [S("b_st0"), S("b_st1")]; b_sf = [S("b_sf0"), S("b_sf1")]
        r_nm = S("r_nm"); r_ex = S("r_ex"); r_vA = S("r_vA"); r_mmA = S("r_mmA")
        r_hs = S("r_hs"); r_hl = S("r_hl"); r_off = S("r_off")
        r_mmB = S("r_mmB"); r_vB = S("r_vB"); g_gs = S("g_gs")
        p3_fm = [S("p3_fm0"), S("p3_fm1")]
        p3_mmA = S("p3_mmA"); p3_mmV = S("p3_mmV")
        p3_ev = S("p3_ev"); p3_vv = S("p3_vv"); p3_vm = S("p3_vm")
        p3_ks = [S(f"p3_ks{i}") for i in range(4)]
        q_kh = [S("q_kh0"), S("q_kh1")]; q_qp = [S("q_qp0"), S("q_qp1")]
        q_mmS = S("q_mmS"); q_mmA = S("q_mmA"); q_ex = S("q_ex")
        q_rc = S("q_rc"); q_av = S("q_av"); q_st = S("q_st")
        z_gs = S("z_gs"); z_as = S("z_as"); z_ms = S("z_ms")
        x_sc = [S("x_sc0"), S("x_sc1")]
        m_sl = [S("m_sl0"), S("m_sl1")]; m_mg = S("m_mg")
        n_ga = [S("n_ga0"), S("n_ga1")]; n_gx = S("n_gx")
        n_tr = S("n_tr"); n_tc = S("n_tc")
        n_w1 = [S("n_w10"), S("n_w11")]; n_hm = S("n_hm"); n_si = S("n_si")
        n_ht = S("n_ht"); n_w2 = S("n_w2"); n_m2 = S("n_m2"); n_oe = S("n_oe")
        dx_s = S("dx_s")
        n_os = [S("n_os0"), S("n_os1")]
        o_af = S("o_af"); o_wo = S("o_wo")
        o_x = [S("o_x0"), S("o_x1")]; o_sh = [S("o_sh0"), S("o_sh1")]
        o_mg = [S("o_mg0"), S("o_mg1")]
        o_mm = S("o_mm"); o_ev = S("o_ev"); o_st = [S("o_st0"), S("o_st1")]
        o_fr5 = S("o_fr5"); o_fr6 = S("o_fr6"); d_st = S("d_st")
        vch = S("vch")

        def VE(v, instr):
            vch.inc(instr)
            v.wait_ge(vch.h, vch.n)

        M = {}

        with nc.Block() as blk:
            @blk.gpsimd
            def _(g):
                for dst, src in [(ident_sb[:], ident), (id32_sb[:], ident32),
                                 (tri_sb[:], TRI), (iota8_sb[:], iota8),
                                 (iota1k_sb[:], iota1k), (ones_sb[:], onesrow),
                                 (ones816_sb[:], ones816),
                                 (bqc_sb[:], bq_c), (bkc_sb[:], bk_c),
                                 (bvr_sb[:], bv_r), (bor_sb[:], bo_r),
                                 (maskp_sb[:], maskp), (tok1_sb[:], tok1)]:
                    g0.inc(g.dma_start(out=dst, in_=src), 16)
                g0.inc(g.dma_start(
                    out=wg_sb[:].rearrange("p (dc e) -> p dc e", dc=NDC),
                    in_=Wg.rearrange("(dc p) e -> p dc e", p=128)), 16)
                g.memset(csh_sb[:], -CSHIFT)
                z_ms.inc(g.memset(ar(0, 2048), 0.0))
                g.wait_ge(z_ms.h, 1)
                for r in range(64):
                    g_gs.inc(g.dma_start(out=xs_c[r * 128:(r + 1) * 128, :],
                                         in_=ar(0, 2048)), 16)
        M["consts"] = g0.n
        M["xs_zero_g"] = g_gs.n

        with nc.Block() as blk:
            @blk.scalar
            def _(sc):
                sc.wait_ge(z_ms.h, 1)
                for r in range(64, 128):
                    z_as.inc(sc.dma_start(out=xs_c[r * 128:(r + 1) * 128, :],
                                          in_=ar(0, 2048)), 16)
        M["xs_zero_a"] = z_as.n

        with nc.Block() as blk:
            @blk.sync
            def _(s):
                for i in range(16):
                    if i >= 2:
                        s.wait_ge(a_fr.h, i - 1)
                    a_x[i % 2].inc(s.dma_start(out=xt_f32[i % 2],
                                               in_=x_my[i * 128:(i + 1) * 128, :]), 16)

            @blk.vector
            def _(v):
                for i in range(16):
                    v.wait_ge(a_x[i % 2].h, 16 * (i // 2 + 1))
                    for q4 in range(4):
                        ii = v.bn_stats(stat6[:, 6 * q4:6 * q4 + 6],
                                        xt_f32[i % 2][:, 512 * q4:512 * (q4 + 1)])
                    a_ms.inc(ii)
                    v.wait_ge(a_ms.h, a_ms.n)
                    a_ms.inc(v.bn_aggr(stat2[:], stat6[:]))
                    v.wait_ge(a_ms.h, a_ms.n)
                    v.tensor_copy(mean_all[:, i:i + 1], stat2[:, 0:1])
                    ts = v.tensor_scalar(var_all[:, i:i + 1], stat2[:, 1:2], EPS, None,
                                         op0=ALU.add)
                    if i == 15:
                        a_ms.inc(ts)
                    a_fr.inc(v.nop())
        M["stats"] = a_ms.n

        with nc.Block() as blk:
            @blk.scalar
            def _(sc):
                sc.wait_ge(a_ms.h, M["stats"])
                m_sq.inc(sc.activation(sd_all[:], var_all[:], AF.Sqrt))

            @blk.vector
            def _(v):
                v.wait_ge(m_sq.h, 1)
                m_rs.inc(v.reciprocal(rstd_all[:], sd_all[:]))
                v.wait_ge(m_rs.h, 1)
        M["rstd"] = m_rs.n

        for wi, (wname, WT_d, Bm, Am) in enumerate(
                [("q", WqT, Bq, AqT), ("k", WkT, Bk, AkT), ("v", WvT, Bv, AvT)]):
            bR = [f_raw[0].n, f_raw[1].n]
            bC = f_cmb.n
            bM = f_mm.n
            with nc.Block() as blk:
                @blk.gpsimd
                def _(g, Bm=Bm, Am=Am, WT_d=WT_d, bC=bC):
                    g.wait_ge(f_cmb.h, bC)
                    f_ba.inc(g.dma_start(out=lor_b[:, 0:D], in_=Bm), 16)
                    f_ba.inc(g.dma_start(out=at_b[:], in_=Am), 16)
                    for dc in range(NDC):
                        if dc >= 2:
                            g.wait_ge(f_cmb.h, bC + dc - 1)
                        f_raw[dc % 2].inc(g.dma_start(
                            out=raw_b[dc % 2][:], in_=WT_d[dc * 128:(dc + 1) * 128, :]), 16)
                M[f"ba_{wname}"] = f_ba.n

                @blk.tensor
                def _(t, wname=wname, bC=bC):
                    t.wait_ge(f_ba.h, M[f"ba_{wname}"])
                    for dc in range(NDC):
                        if dc >= 2:
                            t.wait_ge(f_cmb.h, bC + dc - 1)
                        f_mm.inc(t.matmul(PB[dc % 2][:, 0:256],
                                          lor_b[:, dc * 128:(dc + 1) * 128],
                                          at_b[:], start=True, stop=True))

                @blk.vector
                def _(v, wname=wname, bM=bM, bR=bR):
                    for dc in range(NDC):
                        v.wait_ge(f_mm.h, bM + dc + 1)
                        v.wait_ge(f_raw[dc % 2].h, bR[dc % 2] + 16 * (dc // 2 + 1))
                        f_cmb.inc(v.scalar_tensor_tensor(
                            weff(wname, dc), PB[dc % 2][:, 0:256], LORA_S,
                            raw_b[dc % 2][:], op0=ALU.mult, op1=ALU.add))
            M[f"fold_{wname}"] = f_cmb.n

        bR = [f_raw[0].n, f_raw[1].n]
        bC = f_cmb.n
        bM = f_mm.n
        bW = [f_wst[0].n, f_wst[1].n]
        with nc.Block() as blk:
            @blk.sync
            def _(s):
                for dc in range(NDC):
                    s.wait_ge(f_cmb.h, bC + 4 * (dc + 1))
                    f_wst[dc % 2].inc(s.dma_start(
                        out=WoTe[dc * 128:(dc + 1) * 128, :], in_=woe_b[dc % 2][:]), 16)

            @blk.gpsimd
            def _(g):
                g.wait_ge(f_cmb.h, bC)
                f_ba.inc(g.dma_start(out=lor_b[:, 0:D], in_=Bo), 16)
                f_ba.inc(g.dma_start(out=aot_b[:], in_=AoT), 16)
                M["ba_o"] = f_ba.n
                for dc in range(NDC):
                    if dc >= 2:
                        g.wait_ge(f_cmb.h, bC + 4 * (dc - 1))
                    f_raw[dc % 2].inc(g.dma_start(
                        out=rawo_b[dc % 2][:], in_=WoT[dc * 128:(dc + 1) * 128, :]), 16)

            @blk.tensor
            def _(t):
                t.wait_ge(f_ba.h, M["ba_o"])
                for dc in range(NDC):
                    for ot in range(4):
                        idx = dc * 4 + ot
                        if idx >= 2:
                            t.wait_ge(f_cmb.h, bC + idx - 1)
                        f_mm.inc(t.matmul(PB[idx % 2][:, 0:512],
                                          lor_b[:, dc * 128:(dc + 1) * 128],
                                          aot_b[:, ot * 512:(ot + 1) * 512],
                                          start=True, stop=True))

            @blk.vector
            def _(v):
                for dc in range(NDC):
                    if dc >= 2:
                        v.wait_ge(f_wst[dc % 2].h, bW[dc % 2] + 16 * (dc // 2 - 0))
                    for ot in range(4):
                        idx = dc * 4 + ot
                        v.wait_ge(f_mm.h, bM + idx + 1)
                        v.wait_ge(f_raw[dc % 2].h, bR[dc % 2] + 16 * (dc // 2 + 1))
                        f_cmb.inc(v.scalar_tensor_tensor(
                            woe_b[dc % 2][:, ot * 512:(ot + 1) * 512],
                            PB[idx % 2][:, 0:512], LORA_S,
                            rawo_b[dc % 2][:, ot * 512:(ot + 1) * 512],
                            op0=ALU.mult, op1=ALU.add))
        M["fold_o"] = f_cmb.n
        M["woTe_w0"] = f_wst[0].n
        M["woTe_w1"] = f_wst[1].n

        with nc.Block() as blk:
            @blk.sync
            def _(s):
                for i in range(16):
                    if i >= 2:
                        s.wait_ge(b_fr.h, i - 1)
                    b_x[i % 2].inc(s.dma_start(out=xt_f32[i % 2],
                                               in_=x_my[i * 128:(i + 1) * 128, :]), 16)
                    if "tm" not in KCUT:
                        s.wait_ge(b_xn.h, i + 1)
                        b_st[i % 2].inc(s.dma_start(out=xn_tm_c[i * 128:(i + 1) * 128, :],
                                                    in_=xnb_ev[i % 2][:]), 16)
                    if i % 2 == 1 and "fm" not in KCUT:
                        s.wait_ge(b_fmc.h, 16 * (i + 1))
                        b_sf[(i // 2) % 2].inc(s.dma_start(
                            out=xn_fm_c.rearrange("(dc p) t -> p dc t", p=128)
                                [:, :, (i - 1) * 128:(i + 1) * 128],
                            in_=fmstage[:].rearrange("p (dc t) -> p dc t", dc=NDC)), 16)

            @blk.vector
            def _(v):
                v.wait_ge(m_rs.h, M["rstd"])
                for i in range(16):
                    v.wait_ge(b_x[i % 2].h, 16 * (i // 2 + 1))
                    if i >= 2:
                        if "tm" not in KCUT:
                            v.wait_ge(b_st[i % 2].h, 16 * ((i - 2) // 2 + 1))
                        if "tr" not in KCUT:
                            v.wait_ge(b_tr.h, 16 * (i - 1))
                        P = (i - 2) // 2
                        if "fm" not in KCUT:
                            v.wait_ge(b_sf[0].h, 16 * (P // 2 + 1))
                            if P >= 1:
                                v.wait_ge(b_sf[1].h, 16 * ((P - 1) // 2 + 1))
                        if "lg" not in KCUT:
                            v.wait_ge(b_lg.h, i - 1)
                    b_xn.inc(v.tensor_scalar(xnb_ev[i % 2][:], xt_f32[i % 2],
                                             mean_all[:, i:i + 1], rstd_all[:, i:i + 1],
                                             op0=ALU.subtract, op1=ALU.mult))
                    b_fr.inc(v.nop())
                    if "tr" not in KCUT:
                        for dc in range(NDC):
                            g2 = i * 16 + dc
                            v.wait_ge(b_tr.h, g2 + 1)
                            b_fmc.inc(v.tensor_copy(
                                fmstage[:, dc * 256 + (i % 2) * 128:
                                        dc * 256 + (i % 2) * 128 + 128],
                                PT[dc % 2][:]))
                    if "lg" not in KCUT and "le" not in KCUT and "lev" not in KCUT:
                        v.wait_ge(b_lg.h, i + 1)
                        b_le.inc(v.tensor_copy(logits_all[:, 8 * i:8 * i + 8],
                                               PA[2 + i % 2][:, 0:8]))

            @blk.tensor
            def _(t):
                t.wait_ge(g0.h, M["consts"])
                for i in range(16):
                    t.wait_ge(b_xn.h, i + 1)
                    if "tr" not in KCUT:
                        for dc in range(NDC):
                            g2 = i * 16 + dc
                            if g2 >= 2:
                                t.wait_ge(b_fmc.h, g2 - 1)
                            b_tr.inc(t.transpose(PT[dc % 2][:],
                                                 xnb_ev[i % 2][:, dc * 128:(dc + 1) * 128],
                                                 ident_sb[:]))
                    if "lg" not in KCUT:
                        t.wait_ge(b_fmc.h, 16 * (i + 1))
                        if i >= 2 and "le" not in KCUT:
                            t.wait_ge(b_le.h, i - 1)
                        for dc in range(NDC):
                            mmi = t.matmul(PA[2 + i % 2][:, 0:8],
                                           fmstage[:, dc * 256 + (i % 2) * 128:
                                                   dc * 256 + (i % 2) * 128 + 128],
                                           wg_sb[:, dc * 8:(dc + 1) * 8],
                                           start=(dc == 0), stop=(dc == NDC - 1))
                        b_lg.inc(mmi)

            @blk.scalar
            def _(sc):
                if "lg" not in KCUT and "le" not in KCUT and "lev" in KCUT:
                    for i in range(16):
                        sc.wait_ge(b_lg.h, i + 1)
                        b_le.inc(sc.activation(logits_all[:, 8 * i:8 * i + 8],
                                               PA[2 + i % 2][:, 0:8], AF.Copy))

        with nc.Block() as blk:
            @blk.gpsimd
            def _(g):
                g.wait_ge(b_st[0].h, b_st[0].n)
                g.wait_ge(b_st[1].h, b_st[1].n)
                g.wait_ge(b_sf[0].h, b_sf[0].n)
                g.wait_ge(b_sf[1].h, b_sf[1].n)
                cc.inc(g.collective_compute(
                    "AllGather", ALU.bypass, replica_groups=[list(range(8))],
                    ins=[xn_fm_c], outs=[xn_fm_full]))
        M["cc_ag1"] = cc.n

        if DP >= 2:
            with nc.Block() as blk:
                @blk.vector
                def _(v):
                    v.wait_ge(b_le.h, 16)
                    for i in range(16):
                        last = v.reduce_max(nm_all[:, i:i + 1],
                                            logits_all[:, 8 * i:8 * i + 8],
                                            axis=AX.X, negate=True)
                    r_nm.inc(last)
                    v.wait_ge(r_nm.h, 1)
                    r_nm.inc(v.tensor_scalar(pmax_all[:], nm_all[:], -1.0, None,
                                             op0=ALU.mult))

                @blk.scalar
                def _(sc):
                    sc.wait_ge(r_nm.h, 2)
                    for i in range(16):
                        r_ex.inc(sc.activation(junk8[:], logits_all[:, 8 * i:8 * i + 8],
                                               AF.Exp, bias=nm_all[:, i:i + 1],
                                               accum_out=se_all[:, i:i + 1]))
            M["exp"] = r_ex.n

            with nc.Block() as blk:
                @blk.vector
                def _(v):
                    v.wait_ge(r_ex.h, M["exp"])
                    v.reciprocal(gate_sb[:], se_all[:])
                    for i in range(16):
                        if i >= 1:
                            v.wait_ge(r_mmA.h, 3 * (i - 1) + 3)
                        VE(v, v.tensor_scalar(junk8[:], logits_all[:, 8 * i:8 * i + 8],
                                              pmax_all[:, i:i + 1], None, op0=ALU.is_equal))
                        r_vA.inc(v.scalar_tensor_tensor(junk8b[:], junk8[:], -1000.0,
                                                        iota1k_sb[:],
                                                        op0=ALU.mult, op1=ALU.add))
                        v.wait_ge(r_vA.h, r_vA.n)
                        r_vA.inc(v.tensor_reduce(eidx_sb[:, i:i + 1], junk8b[:],
                                                 op=ALU.min, axis=AX.X))
                        v.wait_ge(r_vA.h, 5 * i + 2)
                        if DP == 2:
                            v.tensor_copy(logits_all2[:, 8 * i:8 * i + 8], junk8[:])
                            v.tensor_copy(logits_all3[:, 8 * i:8 * i + 8], junk8b[:])
                        VE(v, v.tensor_scalar(oh_all[:, 8 * i:8 * i + 8], iota8_sb[:],
                                              eidx_sb[:, i:i + 1], None, op0=ALU.is_equal))
                        r_vA.inc(v.tensor_copy(ohf_sb[:], oh_all[:, 8 * i:8 * i + 8]))
                        v.wait_ge(r_mmA.h, 3 * i + 1)
                        r_vA.inc(v.tensor_copy(pos_all[:, 8 * i:8 * i + 8],
                                               PA[i % 2][:, 0:8]))
                        v.wait_ge(r_mmA.h, 3 * i + 3)
                        v.tensor_copy(histco[:, i:i + 1], PA[2 + i % 2][0:8, 127:128])
                        r_vA.inc(v.tensor_copy(ohT_all[:, 128 * i:128 * (i + 1)],
                                               PB[i % 2][0:8, 0:128]))
                    r_vA.inc(v.nop())
                    v.wait_ge(r_vA.h, r_vA.n)
                    r_vA.inc(v.tensor_tensor_scan(pref[:], ones816_sb[:], histco[:],
                                                  0.0, op0=ALU.mult, op1=ALU.add))
                    v.wait_ge(r_vA.h, r_vA.n)

                @blk.tensor
                def _(t):
                    for i in range(16):
                        t.wait_ge(r_vA.h, 5 * i + 3)
                        if i >= 2:
                            t.wait_ge(r_vA.h, 5 * (i - 2) + 5)
                        r_mmA.inc(t.matmul(PA[i % 2][:, 0:8], tri_sb[:],
                                           oh_all[:, 8 * i:8 * i + 8],
                                           start=True, stop=True))
                        t.wait_ge(r_vA.h, 5 * i + 4)
                        r_mmA.inc(t.transpose(PA[2 + i % 2][0:8, 0:128],
                                              pos_all[:, 8 * i:8 * i + 8], id32_sb[:]))
                        r_mmA.inc(t.transpose(PB[i % 2][0:8, 0:128], ohf_sb[:],
                                              id32_sb[:]))
            M["p2a"] = r_vA.n

            with nc.Block() as blk:
                @blk.sync
                def _(s):
                    s.wait_ge(r_vA.h, M["p2a"])
                    r_hs.inc(s.dma_start(out=hist_c, in_=pref[:, 15:16]), 16)

                @blk.gpsimd
                def _(g):
                    g.wait_ge(r_hs.h, 16)
                    cc.inc(g.collective_compute(
                        "AllGather", ALU.bypass, replica_groups=[list(range(8))],
                        ins=[hist_c], outs=[hist_full]))
            M["cc_ag2"] = cc.n

            with nc.Block() as blk:
                @blk.sync
                def _(s):
                    s.wait_ge(cc.h, M["cc_ag2"])
                    with nc.allow_non_contiguous_dma("tiny 64-elem strided load"):
                        r_hl.inc(s.dma_start(
                            out=hf_sb[:].rearrange("e (j one) -> e j one", one=1),
                            in_=hist_full.rearrange("(j e) one -> e j one", e=8)), 16)

                @blk.vector
                def _(v):
                    v.wait_ge(r_hl.h, 16)
                    r_off.inc(v.tensor_tensor(hm_sb[:], hf_sb[:], maskp_sb[:], op=ALU.mult))
                    v.wait_ge(r_off.h, 1)
                    r_off.inc(v.tensor_reduce(offc[:], hm_sb[:], op=ALU.add, axis=AX.X))
                    v.wait_ge(r_off.h, 2)
                    VE(v, v.tensor_scalar(offm1[:], pref[:], offc[:], None, op0=ALU.add))
                    r_off.inc(v.tensor_tensor(offmF[:], offm1[:], histco[:],
                                              op=ALU.subtract))
            M["offm"] = r_off.n

            with nc.Block() as blk:
                @blk.tensor
                def _(t):
                    t.wait_ge(r_off.h, M["offm"])
                    for i in range(16):
                        if i >= 2:
                            t.wait_ge(r_vB.h, 3 * i - 3)
                        r_mmB.inc(t.matmul(PB[i % 2][:, 0:1],
                                           ohT_all[:, 128 * i:128 * (i + 1)],
                                           offmF[:, i:i + 1], start=True, stop=True))

                @blk.vector
                def _(v):
                    for i in range(16):
                        v.wait_ge(r_mmB.h, i + 1)
                        r_vB.inc(v.tensor_tensor(junk8[:], oh_all[:, 8 * i:8 * i + 8],
                                                 pos_all[:, 8 * i:8 * i + 8], op=ALU.mult))
                        v.wait_ge(r_vB.h, r_vB.n)
                        r_vB.inc(v.tensor_reduce(r_c1[:], junk8[:], op=ALU.add, axis=AX.X))
                        v.wait_ge(r_vB.h, r_vB.n)
                        VE(v, v.tensor_tensor(r_c2[:], r_c1[:], PB[i % 2][:, 0:1],
                                              op=ALU.add))
                        VE(v, v.tensor_scalar(posd_sb[:, i:i + 1], r_c2[:], -1.0, None,
                                              op0=ALU.add))
                        VE(v, v.tensor_scalar(keep_sb[:, i:i + 1], posd_sb[:, i:i + 1],
                                              float(CAP), None, op0=ALU.is_lt))
                        VE(v, v.tensor_scalar(r_c1[:], eidx_sb[:, i:i + 1], float(CAP),
                                              None, op0=ALU.mult))
                        VE(v, v.tensor_tensor(slotf_sb[:, i:i + 1], r_c1[:],
                                              posd_sb[:, i:i + 1], op=ALU.add))
                        VE(v, v.tensor_scalar(r_c2[:], keep_sb[:, i:i + 1], -1.0, 1.0,
                                              op0=ALU.mult, op1=ALU.add))
                        VE(v, v.tensor_tensor(r_c1[:], slotf_sb[:, i:i + 1],
                                              keep_sb[:, i:i + 1], op=ALU.mult))
                        VE(v, v.scalar_tensor_tensor(r_c3[:], r_c2[:], SENT,
                                                     r_c1[:], op0=ALU.mult, op1=ALU.add))
                        VE(v, v.tensor_copy(slotu_sb[:, i:i + 1], r_c3[:]))
                        r_vB.inc(v.tensor_tensor(gs_sb[:, i:i + 1], gate_sb[:, i:i + 1],
                                                 keep_sb[:, i:i + 1], op=ALU.mult))
            M["route"] = r_vB.n

            bBX = [b_x[0].n, b_x[1].n]
            bXS = [x_sc[0].n, x_sc[1].n]
            with nc.Block() as blk:
                @blk.sync
                def _(s):
                    s.wait_ge(b_st[0].h, b_st[0].n)
                    s.wait_ge(b_st[1].h, b_st[1].n)
                    for i in range(16):
                        if i >= 2:
                            s.wait_ge(x_sc[i % 2].h, bXS[i % 2] + 16 * ((i - 2) // 2 + 1))
                        b_x[i % 2].inc(s.dma_start(
                            out=xnb_ev[i % 2],
                            in_=xn_tm_c[i * 128:(i + 1) * 128, :]), 16)

                @blk.gpsimd
                def _(g):
                    g.wait_ge(r_vB.h, M["route"])
                    g.wait_ge(g_gs.h, M["xs_zero_g"])
                    g.wait_ge(z_as.h, M["xs_zero_a"])
                    for i in range(16):
                        g.wait_ge(b_x[i % 2].h, bBX[i % 2] + 16 * (i // 2 + 1))
                        x_sc[i % 2].inc(g.indirect_dma_start(
                            out=xs_c,
                            out_offset=bass.IndirectOffsetOnAxis(slotu_sb[:, i:i + 1], 0),
                            in_=xnb_ev[i % 2], in_offset=None,
                            bounds_check=TTOK - 1, oob_is_err=False), 16)

            def emit_a2a3():
                with nc.Block() as blk:
                    @blk.gpsimd
                    def _(g):
                        g.wait_ge(x_sc[0].h, x_sc[0].n)
                        g.wait_ge(x_sc[1].h, x_sc[1].n)
                        with allow_ici_all_to_all():
                            cc.inc(g.collective_compute(
                                "AllToAll", ALU.bypass, replica_groups=[list(range(8))],
                                ins=[xs_c], outs=[xs_all]))
                M["cc_a2a3"] = cc.n
            if DP == 2:
                emit_a2a3()

        if DP >= 3:
            def fmb(par):
                return O_FM + par * 8192

            with nc.Block() as blk:
                @blk.vector
                def _(v):
                    p3_vm.inc(v.memset(ar(O_VAUG, VAUG_SZ), 1.0))

                @blk.sync
                def _(s):
                    s.wait_ge(cc.h, M["cc_ag1"])
                    s.wait_ge(f_cmb.h, M["fold_o"])
                    for tt in range(32):
                        r, tl = tt // 4, tt % 4
                        if tt >= 2:
                            s.wait_ge(p3_mmA.h, 4 * (tt - 1))
                            s.wait_ge(p3_mmV.h, 4 * (tt - 1))
                        for dcg in range(4):
                            src = xn_fm_full[r * 2048 + dcg * 512: r * 2048 + (dcg + 1) * 512,
                                             tl * 512:(tl + 1) * 512]
                            p3_fm[tt % 2].inc(s.dma_start(
                                out=ar(fmb(tt % 2) + dcg * 2048, 2048).rearrange(
                                    "p (dc t) -> p dc t", dc=4),
                                in_=src.rearrange("(dc p) t -> p dc t", p=128)), 16)
                        for g in range(4):
                            pidx = (tt - 1) * 4 + g
                            if pidx >= 0:
                                s.wait_ge(p3_ev.h, pidx + 1)
                                pg = pidx % 4
                                dst = Kfm if pg < 2 else Qfm
                                hl = pg % 2
                                ptt = pidx // 4
                                p3_ks[pidx % 4].inc(s.dma_start(
                                    out=dst[hl * 128:(hl + 1) * 128,
                                            ptt * 512:(ptt + 1) * 512],
                                    in_=kq_ev[pidx % 4][:]), 16)
                    for g in range(4):
                        pidx = 31 * 4 + g
                        s.wait_ge(p3_ev.h, pidx + 1)
                        pg = pidx % 4
                        dst = Kfm if pg < 2 else Qfm
                        hl = pg % 2
                        p3_ks[pidx % 4].inc(s.dma_start(
                            out=dst[hl * 128:(hl + 1) * 128, 31 * 512:32 * 512],
                            in_=kq_ev[pidx % 4][:]), 16)

                @blk.tensor
                def _(t):
                    t.wait_ge(f_cmb.h, M["fold_o"])
                    t.wait_ge(p3_vm.h, 1)
                    for tt in range(32):
                        t.wait_ge(p3_fm[tt % 2].h, 64 * (tt // 2 + 1))
                        for g in range(4):
                            idx = tt * 4 + g
                            w = "k" if g < 2 else "q"
                            hl = g % 2
                            if idx >= 2:
                                t.wait_ge(p3_ev.h, idx - 1)
                            for dc in range(NDC):
                                mmi = t.matmul(PA[g % 2][:, 0:512],
                                               weff(w, dc)[:, hl * 128:(hl + 1) * 128],
                                               ar(fmb(tt % 2) + dc * 512, 512),
                                               start=(dc == 0), stop=(dc == NDC - 1))
                            p3_mmA.inc(mmi)
                        for tc in range(4):
                            idx = tt * 4 + tc
                            if idx >= 2:
                                t.wait_ge(p3_vv.h, idx - 1)
                            for dc in range(NDC):
                                t.matmul(PB[tc % 2][:, 0:256],
                                         ar(fmb(tt % 2) + dc * 512 + tc * 128, 128),
                                         weff("v", dc), start=(dc == 0), stop=False)
                            p3_mmV.inc(t.matmul(PB[tc % 2][:, 0:256], ones_sb[:],
                                                bvr_sb[:], start=False, stop=True))

                @blk.vector
                def _(v):
                    for tt in range(32):
                        for g in range(4):
                            idx = tt * 4 + g
                            bcol = (bkc_sb if g < 2 else bqc_sb)[:, g % 2:g % 2 + 1]
                            v.wait_ge(p3_mmA.h, idx + 1)
                            if idx >= 4:
                                v.wait_ge(p3_ks[idx % 4].h, 16 * (idx // 4))
                            p3_ev.inc(v.tensor_scalar(kq_ev[idx % 4][:], PA[g % 2][:],
                                                      bcol, None, op0=ALU.add))
                        for tc in range(4):
                            idx = tt * 4 + tc
                            v.wait_ge(p3_mmV.h, idx + 1)
                            off = O_VAUG + idx * 258
                            VE(v, v.tensor_copy(ar(off, 128), PB[tc % 2][:, 0:128]))
                            p3_vv.inc(v.tensor_copy(ar(off + 129, 128),
                                                    PB[tc % 2][:, 128:256]))
            M["p3_mmA"] = p3_mmA.n
            M["p3_ks"] = [p3_ks[i].n for i in range(4)]
            if DP == 3:
                emit_a2a3()

        if DP >= 4:
            def kh(par):
                p3_ = par % 3
                return 66816 if p3_ == 2 else O_KH + p3_ * 4096

            def qp(par):
                return 70912 + (par % 4) * 512

            def et(par):
                return O_ET if par % 2 == 0 else 73648

            kh1_thr = q_kh[1].n + 16
            kh0_thr = q_kh[0].n + 32
            with nc.Block() as blk:
                @blk.gpsimd
                def _(g, kh1_thr=kh1_thr, kh0_thr=kh0_thr):
                    g.wait_ge(q_kh[1].h, kh1_thr)
                    g.wait_ge(q_kh[0].h, kh0_thr)
                    g.wait_ge(x_sc[0].h, x_sc[0].n)
                    g.wait_ge(x_sc[1].h, x_sc[1].n)
                    with allow_ici_all_to_all():
                        cc.inc(g.collective_compute(
                            "AllToAll", ALU.bypass, replica_groups=[list(range(8))],
                            ins=[xs_c], outs=[xs_all]))
                    for b2 in range(4):
                        g.wait_ge(q_av.h, 64 * (b2 + 1))
                        for half in range(2):
                            q_st.inc(g.dma_start(
                                out=att_c[b2 * 4096 + half * 2048:
                                          b2 * 4096 + (half + 1) * 2048, :]
                                    .rearrange("(q p) f -> p q f", p=128),
                                in_=ar(O_STG + half * 4096, 4096).rearrange(
                                    "p (q f) -> p q f", f=256)), 16)

                @blk.sync
                def _(s):
                    for i4 in range(4):
                        s.wait_ge(p3_ks[i4].h, M["p3_ks"][i4])
                    for ph in range(8):
                        b, hl = ph // 2, ph % 2
                        if ph >= 3:
                            s.wait_ge(q_mmS.h, 256 * (ph - 2))
                        q_kh[ph % 2].inc(s.dma_start(
                            out=ar(kh(ph), 4096),
                            in_=Kfm[hl * 128:(hl + 1) * 128, b * 4096:(b + 1) * 4096]), 16)
                        for qb in range(8):
                            gqb = ph * 8 + qb
                            if gqb >= 4:
                                s.wait_ge(q_mmS.h, 32 * (gqb - 3))
                            q_qp[gqb % 2].inc(s.dma_start(
                                out=ar(qp(gqb), 512),
                                in_=Qfm[hl * 128:(hl + 1) * 128,
                                        b * 4096 + qb * 512: b * 4096 + (qb + 1) * 512]), 16)

                @blk.tensor
                def _(t):
                    def avmm(g):
                        phg = g // 8
                        bg, hlg = phg // 2, phg % 2
                        t.wait_ge(q_ex.h, 32 * (g + 1))
                        for qs in range(4):
                            gqs = g * 4 + qs
                            if gqs >= 2:
                                t.wait_ge(q_av.h, gqs - 1)
                            for kc in range(32):
                                gtc = bg * 32 + kc
                                mmi = t.matmul(PB[qs % 2][:, 0:129],
                                               ar(et(g % 2) + kc * 512 + qs * 128, 128),
                                               ar(O_VAUG + gtc * 258 + hlg * 129, 129),
                                               start=(kc == 0), stop=(kc == 31))
                            q_mmA.inc(mmi)

                    for ph in range(8):
                        b, hl = ph // 2, ph % 2
                        t.wait_ge(q_kh[ph % 2].h, 16 * (ph // 2 + 1))
                        for qb in range(8):
                            gqb = ph * 8 + qb
                            t.wait_ge(q_qp[gqb % 2].h, 16 * (gqb // 2 + 1))
                            for kc in range(32):
                                gsc = gqb * 32 + kc
                                if gsc >= 4:
                                    t.wait_ge(q_ex.h, gsc - 3)
                                q_mmS.inc(t.matmul(PA[kc % 4][:],
                                                   ar(kh(ph) + kc * 128, 128),
                                                   ar(qp(gqb), 512),
                                                   start=True, stop=True))
                            if gqb >= 1:
                                avmm(gqb - 1)
                    avmm(63)

                @blk.scalar
                def _(sc):
                    for ph in range(8):
                        for qb in range(8):
                            gqb = ph * 8 + qb
                            for kc in range(32):
                                gsc = gqb * 32 + kc
                                sc.wait_ge(q_mmS.h, gsc + 1)
                                if kc == 0 and gqb >= 2:
                                    sc.wait_ge(q_mmA.h, 4 * (gqb - 1))
                                q_ex.inc(sc.activation(
                                    ar(et(gqb % 2) + kc * 512, 512), PA[kc % 4][:],
                                    AF.Exp, bias=csh_sb[:], scale=SCALE))

                @blk.vector
                def _(v):
                    for ph in range(8):
                        b, hl = ph // 2, ph % 2
                        for qb in range(8):
                            for qs in range(4):
                                gqs = (ph * 8 + qb) * 4 + qs
                                qspp = qb * 4 + qs
                                if hl == 0 and qspp == 0 and b >= 1:
                                    v.wait_ge(q_st.h, 32 * b)
                                v.wait_ge(q_mmA.h, gqs + 1)
                                q_rc.inc(v.reciprocal(rec_sb[:],
                                                      PB[qs % 2][:, 128:129]))
                                v.wait_ge(q_rc.h, q_rc.n)
                                q_av.inc(v.tensor_scalar(
                                    ar(O_STG + qspp * 256 + hl * 128, 128),
                                    PB[qs % 2][:, 0:128], rec_sb[:], None,
                                    op0=ALU.mult))
            M["p4_av"] = q_mmA.n
            M["p4_st"] = q_st.n
            M["cc_a2a3"] = cc.n

            with nc.Block() as blk:
                @blk.gpsimd
                def _(g):
                    g.wait_ge(q_st.h, M["p4_st"])
                    with allow_ici_all_to_all():
                        cc.inc(g.collective_compute(
                            "AllToAll", ALU.bypass, replica_groups=[list(range(8))],
                            ins=[att_c], outs=[att_all]))
            M["cc_a2a4"] = cc.n

        if DP >= 5:
            def emit_ffn(tag, w1src, w3src, w2src, out_dram, moe, prev_gate,
                         hook_after_h0=None):
                for half in range(2):
                    gate = prev_gate if half == 0 else M[f"{tag}_w2g_{half - 1}"]
                    bTR = n_tr.n
                    bTC = n_tc.n
                    bGA = [n_ga[0].n, n_ga[1].n]
                    bGX = n_gx.n
                    if moe:
                        bSL = [m_sl[0].n, m_sl[1].n]
                        bMG = m_mg.n
                        with nc.Block() as blk:
                            @blk.sync
                            def _(s, half=half, gate=gate, bSL=bSL, bMG=bMG):
                                s.wait_ge(cc.h, M["cc_a2a3"])
                                s.wait_ge(gate[0].h, gate[1])
                                for st_ in range(8):
                                    for j in range(8):
                                        gidx = st_ * 8 + j
                                        if gidx >= 2:
                                            s.wait_ge(m_mg.h, bMG + gidx - 1)
                                        m_sl[j % 2].inc(s.dma_start(
                                            out=xsh[j % 2],
                                            in_=xs_all[j * 2048 + half * 1024 + st_ * 128:
                                                       j * 2048 + half * 1024 + (st_ + 1) * 128,
                                                       :]), 16)

                            @blk.vector
                            def _(v, half=half, gate=gate, bSL=bSL, bTR=bTR):
                                v.wait_ge(gate[0].h, gate[1])

                                def copies(st2):
                                    for dc in range(NDC):
                                        g2 = st2 * 16 + dc
                                        v.wait_ge(n_tr.h, bTR + g2 + 1)
                                        n_tc.inc(v.tensor_copy(
                                            ar(O_XD + dc * 1024 + st2 * 128, 128),
                                            PT[dc % 2][:]))

                                for st_ in range(8):
                                    for j in range(8):
                                        v.wait_ge(m_sl[j % 2].h,
                                                  bSL[j % 2] + 16 * (st_ * 4 + j // 2 + 1))
                                        if j == 0:
                                            if st_ >= 2:
                                                v.wait_ge(n_tr.h, bTR + 16 * (st_ - 1))
                                            ii5 = v.tensor_copy(accS, xsh[0])
                                        else:
                                            dst = xdtm[st_ % 2] if j % 2 == 1 else accS
                                            srcacc = accS if j % 2 == 1 else xdtm[st_ % 2]
                                            ii5 = v.tensor_tensor(dst, srcacc,
                                                                  xsh[j % 2], op=ALU.add)
                                        m_mg.inc(ii5)
                                        v.wait_ge(m_mg.h, m_mg.n)
                                    if st_ >= 1:
                                        copies(st_ - 1)
                                copies(7)

                            @blk.tensor
                            def _(t, gate=gate, bMG=bMG, bTC=bTC):
                                t.wait_ge(gate[0].h, gate[1])
                                for st_ in range(8):
                                    t.wait_ge(m_mg.h, bMG + 8 * (st_ + 1))
                                    for dc in range(NDC):
                                        g2 = st_ * 16 + dc
                                        if g2 >= 2:
                                            t.wait_ge(n_tc.h, bTC + g2 - 1)
                                        n_tr.inc(t.transpose(
                                            PT[dc % 2][:],
                                            xdtm[st_ % 2][:, dc * 128:(dc + 1) * 128],
                                            ident_sb[:]))
                        M[f"{tag}_xd_{half}"] = ("v", n_tc.n)
                    else:
                        with nc.Block() as blk:
                            @blk.sync
                            def _(s, half=half, gate=gate):
                                s.wait_ge(gate[0].h, gate[1])
                                s.wait_ge(b_sf[0].h, b_sf[0].n)
                                s.wait_ge(b_sf[1].h, b_sf[1].n)
                                for dcg in range(4):
                                    n_gx.inc(s.dma_start(
                                        out=ar(O_XD + dcg * 4096, 4096).rearrange(
                                            "p (dc t) -> p dc t", dc=4),
                                        in_=xn_fm_c.rearrange("(dc p) t -> p dc t", p=128)
                                            [:, dcg * 4:(dcg + 1) * 4,
                                             half * 1024:(half + 1) * 1024]), 16)
                        M[f"{tag}_xd_{half}"] = ("s", n_gx.n)

                    if tag == "sh" and "shh" in KCUT:
                        M[f"{tag}_h_{half}"] = n_ht.n
                        M[f"{tag}_w2g_{half}"] = (n_gx, n_gx.n)
                        continue
                    bW1 = [n_w1[0].n, n_w1[1].n]
                    bHM = n_hm.n
                    bSI = n_si.n
                    bHT = n_ht.n
                    with nc.Block() as blk:
                        @blk.gpsimd
                        def _(g, w1src=w1src, w3src=w3src, gate=gate, bHT=bHT):
                            g.wait_ge(gate[0].h, gate[1])
                            for hc in range(NHC):
                                if hc >= 2:
                                    g.wait_ge(n_ht.h, bHT + 2 * (hc - 1))
                                for wj, wsrc in ((0, w1src), (1, w3src)):
                                    for dcg in range(4):
                                        n_w1[hc % 2].inc(g.dma_start(
                                            out=ar(O_WB + (hc % 2) * 4096 + wj * 2048
                                                   + dcg * 512, 512).rearrange(
                                                "p (dc h) -> p dc h", dc=4),
                                            in_=wsrc[hc, dcg * 4:(dcg + 1) * 4].rearrange(
                                                "dc p h -> p dc h")), 16)

                        @blk.tensor
                        def _(t, half=half, tag=tag, bW1=bW1, bHT=bHT):
                            kind, val = M[f"{tag}_xd_{half}"]
                            t.wait_ge((n_tc if kind == "v" else n_gx).h, val)
                            for hc in range(NHC):
                                t.wait_ge(n_w1[hc % 2].h, bW1[hc % 2] + 128 * (hc // 2 + 1))
                                for t2 in range(2):
                                    hidx = hc * 2 + t2
                                    if hidx >= 2:
                                        t.wait_ge(n_ht.h, bHT + hidx - 1)
                                    for wj in range(2):
                                        dst = PA[t2 % 2] if wj == 0 else PA[2 + t2 % 2]
                                        for dc in range(NDC):
                                            mmi = t.matmul(dst[:],
                                                           ar(O_WB + (hc % 2) * 4096
                                                              + wj * 2048 + dc * 128, 128),
                                                           ar(O_XD + dc * 1024 + t2 * 512, 512),
                                                           start=(dc == 0),
                                                           stop=(dc == NDC - 1))
                                    n_hm.inc(mmi)

                        @blk.scalar
                        def _(sc, bHM=bHM, bHT=bHT):
                            for hc in range(NHC):
                                for t2 in range(2):
                                    hidx = hc * 2 + t2
                                    sc.wait_ge(n_hm.h, bHM + hidx + 1)
                                    if hidx >= 2:
                                        sc.wait_ge(n_ht.h, bHT + hidx - 1)
                                    n_si.inc(sc.activation(s1_ev[t2 % 2][:],
                                                           PA[t2 % 2][:], AF.Silu))

                        @blk.vector
                        def _(v, bSI=bSI):
                            for hc in range(NHC):
                                for t2 in range(2):
                                    hidx = hc * 2 + t2
                                    v.wait_ge(n_si.h, bSI + hidx + 1)
                                    n_ht.inc(v.tensor_tensor(
                                        ar(O_HH + hc * 1024 + t2 * 512, 512),
                                        s1_ev[t2 % 2][:], PA[2 + t2 % 2][:],
                                        op=ALU.mult))

                        if DP == 5 and moe and half == 0:
                            @blk.sync
                            def _(s, bHT=bHT, half=half, tag=tag):
                                kind, val = M[f"{tag}_xd_{half}"]
                                s.wait_ge((n_tc if kind == "v" else n_gx).h, val)
                                for dc in range(NDC):
                                    dx_s.inc(s.dma_start(
                                        out=dbg["d_xd"][dc * 128:(dc + 1) * 128, :],
                                        in_=ar(O_XD + dc * 1024, 1024)), 16)
                                s.wait_ge(n_ht.h, bHT + 2 * NHC)
                                for hc in range(NHC):
                                    dx_s.inc(s.dma_start(
                                        out=dbg["d_hh"][hc * 128:(hc + 1) * 128, :],
                                        in_=ar(O_HH + hc * 1024, 1024)), 16)
                    M[f"{tag}_h_{half}"] = n_ht.n
                    if half == 0 and hook_after_h0 is not None:
                        hook_after_h0()

                    if tag == "sh" and "shw2" in KCUT:
                        M[f"{tag}_w2g_{half}"] = (n_ht, n_ht.n)
                        continue
                    for oh in range(2):
                        bW2 = n_w2.n
                        bM2 = n_m2.n
                        bOE = n_oe.n
                        bOS = [n_os[0].n, n_os[1].n]
                        with nc.Block() as blk:
                            @blk.tensor
                            def _(t, bW2=bW2, bOE=bOE):
                                t.wait_ge(n_w2.h, bW2 + 16 * NHC)
                                for stl in range(8):
                                    for otl in range(2):
                                        idx = stl * 2 + otl
                                        if idx >= 2:
                                            t.wait_ge(n_oe.h, bOE + idx - 1)
                                        for hc in range(NHC):
                                            mmi = t.matmul(PA[idx % 2][:],
                                                           ar(O_HH + hc * 1024 + stl * 128, 128),
                                                           ar(O_W2 + hc * 1024 + otl * 512, 512),
                                                           start=(hc == 0),
                                                           stop=(hc == NHC - 1))
                                        n_m2.inc(mmi)

                            @blk.vector
                            def _(v, bM2=bM2, bOS=bOS):
                                for stl in range(8):
                                    for otl in range(2):
                                        idx = stl * 2 + otl
                                        v.wait_ge(n_m2.h, bM2 + idx + 1)
                                        if idx >= 2:
                                            v.wait_ge(n_os[idx % 2].h,
                                                      bOS[idx % 2] + 16 * (idx // 2))
                                        n_oe.inc(v.tensor_copy(oe_ev[idx % 2][:],
                                                               PA[idx % 2][:]))

                            @blk.gpsimd
                            def _(g, w2src=w2src, tag=tag, half=half, oh=oh, bM2=bM2):
                                g.wait_ge(n_ht.h, M[f"{tag}_h_{half}"])
                                if DP == 5 and moe and half == 0:
                                    g.wait_ge(dx_s.h, 16 * (NDC + NHC))
                                if oh == 1:
                                    g.wait_ge(n_m2.h, bM2)
                                for hc in range(NHC):
                                    n_w2.inc(g.dma_start(
                                        out=ar(O_W2 + hc * 1024, 1024),
                                        in_=w2src[hc * 128:(hc + 1) * 128,
                                                  oh * 1024:(oh + 1) * 1024]), 16)

                            @blk.sync
                            def _(s, out_dram=out_dram, half=half, oh=oh, bOE=bOE):
                                for stl in range(8):
                                    for otl in range(2):
                                        idx = stl * 2 + otl
                                        s.wait_ge(n_oe.h, bOE + idx + 1)
                                        n_os[idx % 2].inc(s.dma_start(
                                            out=out_dram[half * 1024 + stl * 128:
                                                         half * 1024 + (stl + 1) * 128,
                                                         oh * 1024 + otl * 512:
                                                         oh * 1024 + (otl + 1) * 512],
                                            in_=oe_ev[idx % 2][:]), 16)
                    M[f"{tag}_w2g_{half}"] = (n_m2, n_m2.n)
                M[f"{tag}_st"] = (n_os[0].n, n_os[1].n)

            if DP >= 4:
                first_gate = (q_mmA, M["p4_av"])
            else:
                first_gate = (p3_mmA, M.get("p3_mmA", 0)) if DP >= 3 else (cc, 0)
            emit_ffn("moe", W1t, W3t, W2d, oe_c, True, first_gate)

            def emit_ag5():
                with nc.Block() as blk:
                    @blk.gpsimd
                    def _(g):
                        g.wait_ge(n_os[0].h, M["moe_st"][0])
                        g.wait_ge(n_os[1].h, M["moe_st"][1])
                        cc.inc(g.collective_compute(
                            "AllGather", ALU.bypass, replica_groups=[list(range(8))],
                            ins=[oe_c], outs=[oe_full]))
                M["cc_ag5"] = cc.n
            if DP == 5:
                emit_ag5()

        if DP >= 6:
            emit_ffn("sh", sw1t, sw3t, sw2d, shared_my, False, M["moe_w2g_1"],
                     hook_after_h0=emit_ag5)

        if DP >= 7:
            with nc.Block() as blk:
                @blk.sync
                def _(s):
                    s.wait_ge(cc.h, M["cc_a2a4"])
                    g = M["sh_w2g_1"]
                    s.wait_ge(g[0].h, g[1])
                    for fc in range(16):
                        j, hl = fc // 2, fc % 2
                        o_af.inc(s.dma_start(
                            out=ar(O_AFM + fc * 2048, 2048),
                            in_=att_all[j * 2048:(j + 1) * 2048,
                                        hl * 128:(hl + 1) * 128],
                            transpose=True), 16)
                    s.wait_ge(f_wst[0].h, M["woTe_w0"])
                    s.wait_ge(f_wst[1].h, M["woTe_w1"])
                    for fc in range(16):
                        o_wo.inc(s.dma_start(
                            out=ar(O_WOE + fc * 2048, 2048),
                            in_=WoTe[fc * 128:(fc + 1) * 128, :]), 16)
                    for tc in range(16):
                        if tc >= 2:
                            s.wait_ge(o_fr6.h, tc - 1)
                        o_x[tc % 2].inc(s.dma_start(
                            out=xt_f32[tc % 2], in_=x_my[tc * 128:(tc + 1) * 128, :]), 16)
                        o_sh[tc % 2].inc(s.dma_start(
                            out=shbuf[tc % 2][:],
                            in_=shared_my[tc * 128:(tc + 1) * 128, :]), 16)
                        for ot in range(4):
                            idx = tc * 4 + ot
                            s.wait_ge(o_ev.h, idx + 1)
                            o_st[idx % 2].inc(s.dma_start(
                                out=out_my[tc * 128:(tc + 1) * 128,
                                           ot * 512:(ot + 1) * 512],
                                in_=out_ev[idx % 2]), 16)

                @blk.gpsimd
                def _(g):
                    g.wait_ge(cc.h, M["cc_ag5"])
                    for tc in range(16):
                        if tc >= 2:
                            g.wait_ge(o_fr5.h, tc - 1)
                        g.memset(moebuf[tc % 2][:], 0.0)
                        o_mg[tc % 2].inc(g.indirect_dma_start(
                            out=moebuf[tc % 2][:], out_offset=None,
                            in_=oe_full,
                            in_offset=bass.IndirectOffsetOnAxis(slotu_sb[:, tc:tc + 1], 0),
                            bounds_check=NCORE * CAP - 1, oob_is_err=False), 16)

                @blk.tensor
                def _(t):
                    t.wait_ge(o_af.h, 16 * 16)
                    t.wait_ge(o_wo.h, 16 * 16)
                    for tc in range(16):
                        for ot in range(4):
                            idx = tc * 4 + ot
                            if idx >= 2:
                                t.wait_ge(o_ev.h, idx - 1)
                            for fc in range(16):
                                t.matmul(PA[idx % 2][:],
                                         ar(O_AFM + fc * 2048 + tc * 128, 128),
                                         ar(O_WOE + fc * 2048 + ot * 512, 512),
                                         start=(fc == 0), stop=False)
                            o_mm.inc(t.matmul(PA[idx % 2][:], ones_sb[:],
                                              bor_sb[:, ot * 512:(ot + 1) * 512],
                                              start=False, stop=True))

                @blk.vector
                def _(v):
                    for tc in range(16):
                        v.wait_ge(o_mg[tc % 2].h, 16 * (tc // 2 + 1))
                        v.wait_ge(o_x[tc % 2].h, 16 * (tc // 2 + 1))
                        v.wait_ge(o_sh[tc % 2].h, 16 * (tc // 2 + 1))
                        for ot in range(4):
                            idx = tc * 4 + ot
                            v.wait_ge(o_mm.h, idx + 1)
                            if idx >= 2:
                                v.wait_ge(o_st[idx % 2].h, 16 * (idx // 2))
                            VE(v, v.scalar_tensor_tensor(
                                tmp1_f, moebuf[tc % 2][:, ot * 512:(ot + 1) * 512],
                                gs_sb[:, tc:tc + 1], PA[idx % 2][:],
                                op0=ALU.mult, op1=ALU.add))
                            VE(v, v.tensor_tensor(
                                tmp2_f, tmp1_f,
                                xt_f32[tc % 2][:, ot * 512:(ot + 1) * 512], op=ALU.add))
                            o_ev.inc(v.tensor_tensor(
                                out_ev[idx % 2], tmp2_f,
                                shbuf[tc % 2][:, ot * 512:(ot + 1) * 512], op=ALU.add))
                        o_fr5.inc(v.nop())
                        o_fr6.inc(v.nop())

        with nc.Block() as blk:
            @blk.sync
            def _(s):
                if DP < 99:
                    if DP >= 1:
                        s.wait_ge(cc.h, M["cc_ag1"])
                        s.wait_ge(b_st[0].h, b_st[0].n)
                        s.wait_ge(b_st[1].h, b_st[1].n)
                        d_st.inc(s.dma_start(out=dbg["d_xn_tm"], in_=xn_tm_c), 16)
                        d_st.inc(s.dma_start(out=dbg["d_xn_fm"], in_=xn_fm_full), 16)
                    if DP >= 2:
                        s.wait_ge(cc.h, M["cc_a2a3"])
                        d_st.inc(s.dma_start(out=dbg["d_xs"], in_=xs_all), 16)
                        if DP == 2:
                            d_st.inc(s.dma_start(out=dbg["d_xsc"], in_=xs_c), 16)
                            d_st.inc(s.dma_start(out=dbg["d_slotu"], in_=slotu_sb[:]), 16)
                            d_st.inc(s.dma_start(out=dbg["d_eq"], in_=logits_all2[:]), 16)
                            d_st.inc(s.dma_start(out=dbg["d_vl"], in_=logits_all3[:]), 16)
                            d_st.inc(s.dma_start(out=dbg["d_nm"][:, 0:16], in_=nm_all[:]), 16)
                            d_st.inc(s.dma_start(out=dbg["d_nm"][:, 16:32], in_=pmax_all[:]), 16)
                        for k, srcb in enumerate([eidx_sb, gate_sb, posd_sb, keep_sb,
                                                  slotf_sb, gs_sb]):
                            d_st.inc(s.dma_start(out=dbg["d_route"][:, k * 16:(k + 1) * 16],
                                                 in_=srcb[:]), 16)
                    if DP >= 3:
                        for i4 in range(4):
                            s.wait_ge(p3_ks[i4].h, M["p3_ks"][i4])
                        d_st.inc(s.dma_start(out=dbg["d_Kfm"], in_=Kfm), 16)
                        d_st.inc(s.dma_start(out=dbg["d_Qfm"], in_=Qfm), 16)
                    if DP >= 4:
                        s.wait_ge(cc.h, M["cc_a2a4"])
                        d_st.inc(s.dma_start(out=dbg["d_att"], in_=att_all), 16)
                    if DP >= 5:
                        s.wait_ge(cc.h, M["cc_ag5"])
                        d_st.inc(s.dma_start(out=dbg["d_oe"], in_=oe_full), 16)
                    if DP >= 6:
                        s.wait_ge(n_os[0].h, M["sh_st"][0])
                        s.wait_ge(n_os[1].h, M["sh_st"][1])
                        if "shdump" not in KCUT:
                            d_st.inc(s.dma_start(out=dbg["d_shared"], in_=shared_my), 16)
                    s.wait_ge(d_st.h, d_st.n)
                if DP >= 7:
                    s.wait_ge(o_st[0].h, o_st[0].n)
                    s.wait_ge(o_st[1].h, o_st[1].n)

            @blk.gpsimd
            def _(g):
                g.wait_ge(g_gs.h, g_gs.n)

    return nc, dbg


def _tile_w(W):
    return np.ascontiguousarray(
        np.asarray(W).reshape(NDC, 128, NHC, 128).transpose(2, 0, 1, 3))


def _prep_inputs(inputs):
    f32 = np.float32
    x = np.ascontiguousarray(np.asarray(inputs["L0_attn_x"], f32).reshape(TTOK, D))
    WT = {w: np.ascontiguousarray(np.asarray(inputs[f"L0_attn_W{w}"], f32).T)
          for w in "qkvo"}
    A_T = {w: np.ascontiguousarray(np.asarray(inputs[f"L0_attn_A{w}"], f32).T)
           for w in "qkvo"}
    Bm = {w: np.ascontiguousarray(np.asarray(inputs[f"L0_attn_B{w}"], f32))
          for w in "qkvo"}
    bias = {w: np.asarray(inputs[f"L0_attn_b{w}"], f32) for w in "qkvo"}
    Wg = np.ascontiguousarray(np.asarray(inputs["L0_ffn_Wg"], f32))
    W1 = np.asarray(inputs["L0_ffn_W1"], f32)
    W3 = np.asarray(inputs["L0_ffn_W3"], f32)
    W2 = np.asarray(inputs["L0_ffn_W2"], f32)
    sw1t = _tile_w(inputs["L0_ffn_sw1"])
    sw3t = _tile_w(inputs["L0_ffn_sw3"])
    sw2 = np.ascontiguousarray(np.asarray(inputs["L0_ffn_sw2"], f32))

    bf = ml_dtypes.bfloat16
    consts = {
        "ident": np.eye(128, dtype=f32).astype(bf),
        "ident32": np.eye(128, dtype=f32),
        "TRI": np.triu(np.ones((128, 128), f32)).astype(bf),
        "iota8": np.tile(np.arange(8, dtype=f32), (128, 1)),
        "iota1k": np.tile(np.arange(8, dtype=f32) + 1000.0, (128, 1)),
        "onesrow": np.ones((1, 128), f32).astype(bf),
        "ones816": np.ones((8, 16), f32),
    }

    in_maps = []
    for c in range(NCORE):
        fsl = slice(256 * c, 256 * (c + 1))
        m = {
            "x_my": np.ascontiguousarray(x[c * TOWN:(c + 1) * TOWN]),
            "WqT": np.ascontiguousarray(WT["q"][:, fsl]),
            "WkT": np.ascontiguousarray(WT["k"][:, fsl]),
            "WvT": np.ascontiguousarray(WT["v"][:, fsl]),
            "WoT": WT["o"],
            "Bq": Bm["q"], "Bk": Bm["k"], "Bv": Bm["v"], "Bo": Bm["o"],
            "AqT": np.ascontiguousarray(A_T["q"][:, fsl]),
            "AkT": np.ascontiguousarray(A_T["k"][:, fsl]),
            "AvT": np.ascontiguousarray(A_T["v"][:, fsl]),
            "AoT": A_T["o"],
            "bq_c": np.ascontiguousarray(bias["q"][fsl].reshape(2, 128).T),
            "bk_c": np.ascontiguousarray(bias["k"][fsl].reshape(2, 128).T),
            "bv_r": np.ascontiguousarray(bias["v"][fsl].reshape(1, 256)),
            "bo_r": np.ascontiguousarray(bias["o"].reshape(1, D)),
            "Wg": Wg,
            "W1t": _tile_w(W1[c]),
            "W3t": _tile_w(W3[c]),
            "W2d": np.ascontiguousarray(W2[c]),
            "sw1t": sw1t, "sw3t": sw3t, "sw2d": sw2,
            "maskp": np.tile((np.arange(8, dtype=f32) < c).astype(f32), (8, 1)),
            "tok1": (c * TOWN + 1 + np.arange(128, dtype=f32)[:, None]
                     + 128.0 * np.arange(16, dtype=f32)[None, :]),
        }
        m.update(consts)
        in_maps.append(m)
    return in_maps


def kernel(**inputs):
    in_maps = _prep_inputs(inputs)
    nc, _ = build_nc(DP=99)
    res = run_bass_kernel_spmd(nc, in_maps, core_ids=list(range(NCORE)))
    out = np.concatenate([np.asarray(r["out_my"], np.float32) for r in res.results],
                         axis=0)
    return out.reshape(4, 4096, 2048)
